# version 03
# baseline (speedup 1.0000x reference)
import contextlib
import numpy as np
import concourse.bass as bass
import concourse.mybir as mybir
from concourse.bass_utils import run_bass_kernel_spmd

F32 = mybir.dt.float32
BF16 = mybir.dt.bfloat16
U32 = mybir.dt.uint32
AF = mybir.ActivationFunctionType
ALU = mybir.AluOpType

H = 1024
T = 2048
E = 8
HD = 64
NHC = H // 128

_CFG = {"n_seq": 2, "sub": ("f", "a", "f")}
_LAST = None


class Sem:
    def __init__(self, es, nc, name):
        self.h = es.enter_context(nc.semaphore(name))
        self.n = 0

    def inc(self, instr, k=1):
        instr.then_inc(self.h, k)
        self.n += k
        return self.n


def build_program(n_seq, sub):
    nc = bass.Bass("TRN2", target_bir_lowering=False)

    d_x = nc.dram_tensor("x_t", [n_seq, H, T], F32, kind="ExternalInput").ap()
    d_y = nc.dram_tensor("y_t", [n_seq, H, T], F32, kind="ExternalOutput").ap()
    d_w1 = nc.dram_tensor("W1", [E, H, H], F32, kind="ExternalInput").ap()
    d_w2 = nc.dram_tensor("W2", [E, H, H], F32, kind="ExternalInput").ap()
    d_b1c = nc.dram_tensor("b1c", [128, 64], F32, kind="ExternalInput").ap()
    d_b2 = nc.dram_tensor("b2", [E, H], F32, kind="ExternalInput").ap()
    d_wgc = nc.dram_tensor("wgc", [128, 64], F32, kind="ExternalInput").ap()
    d_bgc = nc.dram_tensor("bgc", [8, 1], F32, kind="ExternalInput").ap()
    d_wq = nc.dram_tensor("wqT", [H, H], F32, kind="ExternalInput").ap()
    d_wk = nc.dram_tensor("wkT", [H, H], F32, kind="ExternalInput").ap()
    d_wv = nc.dram_tensor("wvT", [H, H], F32, kind="ExternalInput").ap()
    d_wo = nc.dram_tensor("woT", [H, H], F32, kind="ExternalInput").ap()
    d_rf = nc.dram_tensor("rfdup", [128, HD], F32, kind="ExternalInput").ap()
    d_boc = nc.dram_tensor("boc", [128, 8], F32, kind="ExternalInput").ap()
    d_ident = nc.dram_tensor("ident", [128, 128], F32, kind="ExternalInput").ap()
    d_iota8 = nc.dram_tensor("iota8", [128, 8], F32, kind="ExternalInput").ap()
    d_negsel = nc.dram_tensor("negsel", [2, 128], F32, kind="ExternalInput").ap()
    d_hsel = nc.dram_tensor("hsel", [16, H], F32, kind="ExternalInput").ap()
    d_esel = nc.dram_tensor("esel", [8, H], F32, kind="ExternalInput").ap()

    dbg = _CFG.get("dbg", False)
    if dbg:
        d_dbg_xn = nc.dram_tensor("dbg_xn", [H, T], BF16, kind="ExternalOutput").ap()
        d_dbg_lg = nc.dram_tensor("dbg_lg", [8, T], F32, kind="ExternalOutput").ap()
        d_dbg_oh = nc.dram_tensor("dbg_oh", [8, T], BF16, kind="ExternalOutput").ap()
        d_dbg_h1 = nc.dram_tensor("dbg_h1", [128, 8192], BF16, kind="ExternalOutput").ap()
        d_dbg_mk = nc.dram_tensor("dbg_mk", [128, T], BF16, kind="ExternalOutput").ap()
        d_dbg_xe = nc.dram_tensor("dbg_xe", [E, H, T], F32, kind="ExternalOutput").ap()
        d_dbg_v = nc.dram_tensor("dbg_v", [128, 16384], BF16, kind="ExternalOutput").ap()
        d_dbg_kv = nc.dram_tensor("dbg_kv", [128, 520], BF16, kind="ExternalOutput").ap()
        d_dbg_ar = nc.dram_tensor("dbg_ar", [128, 16384], BF16, kind="ExternalOutput").ap()
        d_dbg_nr = nc.dram_tensor("dbg_nr", [16, T], BF16, kind="ExternalOutput").ap()
    es = contextlib.ExitStack()
    with es:
        sbt = lambda name, shape, dt: es.enter_context(nc.sbuf_tensor(name, shape, dt))
        pst = lambda name, shape: es.enter_context(nc.psum_tensor(name, shape, F32))

        x = [sbt(f"x{i}", [128, T], F32) for i in range(NHC)]
        xn = [sbt(f"xn{i}", [128, T], BF16) for i in range(NHC)]
        w1s = sbt("w1s", [128, 8192], BF16)
        w2s = sbt("w2s", [128, 8192], BF16)
        h1m = sbt("h1m", [128, 8192], BF16)
        vat = sbt("vat", [128, 16384], BF16)
        ar32 = sbt("ar32", [128, 3072], F32)
        ident = sbt("sb_ident", [128, 128], F32)
        iota8 = sbt("sb_iota8", [128, 8], F32)
        negsel = sbt("sb_negsel", [2, 128], BF16)
        hsel = sbt("sb_hsel", [16, H], BF16)
        esel = sbt("sb_esel", [8, H], BF16)
        rf_f = sbt("rf_f", [128, HD], F32)
        rf_s = sbt("rf_s", [128, HD], BF16)
        b1c = sbt("sb_b1c", [128, 64], F32)
        b2bf = sbt("b2bf", [8, H], BF16)
        wg = sbt("wg", [128, 64], BF16)
        bgc = sbt("sb_bgc", [8, 1], F32)
        boc = sbt("sb_boc", [128, 8], F32)
        ones_col = sbt("ones_col", [128, 1], BF16)
        ones_bc = sbt("ones_bc", [1, 128], BF16)
        bdiag = sbt("bdiag", [128, 2], BF16)
        eps_col = sbt("eps_col", [1, 1], F32)
        kvext = sbt("kvext", [128, 520], BF16)
        mx8 = sbt("mx8", [128, 8], F32)
        idxu = sbt("idxu", [128, 8], U32)
        idxf = sbt("idxf", [128, 1], F32)
        ohf = sbt("ohf", [128, 128], F32)
        ohstage = sbt("ohstage", [128, 8], F32)
        nrm16 = sbt("nrm16", [16, T], BF16)
        tmp_r = [sbt("tmp_r0", [1, 1024], BF16), sbt("tmp_r1", [1, 1024], BF16)]

        h1t = lambda ic: h1m[:, ic * 1024:(ic + 1) * 1024]
        xb_sl = [h1m[:, 0:2048], h1m[:, 2048:4096]]
        xsq_sl = [h1m[:, 4096:6144], h1m[:, 6144:8192]]
        kq_sb = h1m[:, 0:2048]
        sq_sb = h1m[:, 2048:4096]
        kp_sb = h1m[:, 4096:5120]
        nr_bf = h1m[0:2, 5120:7168]
        eq_sb = h1m[:, 2048:4096]
        mask_sl = [vat[:, 0:2048], vat[:, 2048:4096]]
        onehotT = vat[0:8, 4096:6144]
        m_bf = vat[0:1, 6144:7168]
        rstd_bf = vat[0:1, 7168:8192]
        vtile = lambda tt: vat[:, tt * 1024:(tt + 1) * 1024]
        araw = lambda oc: vat[:, oc * 2048:(oc + 1) * 2048]
        t1_sl = ar32[:, 0:1024]
        m_row = ar32[0:1, 1024:2048]
        pair_f = ar32[0:2, 1024:2048]
        msq_row = ar32[0:1, 2048:3072]
        var_row = ar32[0:1, 0:1024]
        rstd_f = ar32[0:1, 2048:3072]
        lg_sb = ar32[0:8, 0:2048]

        PG = [pst(f"PG{i}", [128, 1024]) for i in range(4)]

        dl = Sem(es, nc, "dl")
        w1m = Sem(es, nc, "w1m")
        w2m = Sem(es, nc, "w2m")
        im = Sem(es, nc, "im")
        xl = Sem(es, nc, "xl")
        pe = Sem(es, nc, "pe")
        act = Sem(es, nc, "act")
        dve = Sem(es, nc, "dve")
        sp = Sem(es, nc, "sp")
        st = Sem(es, nc, "st")

        block = es.enter_context(nc.Block())

        @block.gpsimd
        def _(g):
            for dst, src in [(ident[:], d_ident), (iota8[:], d_iota8),
                             (negsel[:], d_negsel), (hsel[:], d_hsel),
                             (esel[:], d_esel), (rf_f[:], d_rf),
                             (b1c[:], d_b1c), (b2bf[:], d_b2),
                             (wg[:], d_wgc), (bgc[:], d_bgc), (boc[:], d_boc)]:
                dl.inc(g.dma_start(dst, src), 16)
            im.inc(g.memset(ones_col[:], 1.0))
            im.inc(g.memset(ones_bc[:], 1.0))
            im.inc(g.memset(bdiag[:], 0.0))
            g.wait_ge(im.h, im.n)
            im.inc(g.memset(bdiag[0:64, 0:1], 1.0))
            g.wait_ge(im.h, im.n)
            im.inc(g.memset(bdiag[64:128, 1:2], 1.0))
            im.inc(g.memset(eps_col[:], 1e-5))
        init_dl = dl.n
        init_im = im.n

        @block.scalar
        def _(s):
            s.wait_ge(dl.h, init_dl)
            s.wait_ge(im.h, init_im)
            act.inc(s.activation(rf_s[:], rf_f[:], AF.Copy, scale=0.125))

        def vstep(v, instr):
            dve.inc(instr)
            v.wait_ge(dve.h, dve.n)

        def w_src(d_w):
            return d_w.rearrange("(c p) i -> p c i", p=128)

        def w_dst(slot):
            return slot[:].rearrange("p (c i) -> p c i", c=8)

        def emit_ln(x_ready):
            pe_base = pe.n
            pe_mark = lambda hc: pe_base + hc + 1
            act_base = act.n
            dve_base = dve.n

            @block.scalar
            def _(s):
                for hc in range(NHC):
                    sem, val = x_ready[hc]
                    s.wait_ge(sem, val)
                    if hc == 0:
                        s.wait_ge(x_ready[NHC - 1][0], x_ready[NHC - 1][1])
                    if hc >= 2:
                        s.wait_ge(pe.h, pe_mark(hc - 2))
                    act.inc(s.activation(xb_sl[hc % 2], x[hc][:], AF.Copy))
            assert act.n == act_base + NHC

            @block.vector
            def _(v):
                for hc in range(NHC):
                    sem, val = x_ready[hc]
                    v.wait_ge(sem, val)
                    if hc == 0:
                        v.wait_ge(x_ready[NHC - 1][0], x_ready[NHC - 1][1])
                    if hc >= 2:
                        v.wait_ge(pe.h, pe_mark(hc - 2))
                    dve.inc(v.tensor_mul(xsq_sl[hc % 2], x[hc][:], x[hc][:]))
            assert dve.n == dve_base + NHC

            @block.tensor
            def _(t):
                t.wait_ge(im.h, init_im)
                for hc in range(NHC):
                    t.wait_ge(act.h, act_base + hc + 1)
                    t.wait_ge(dve.h, dve_base + hc + 1)
                    first = hc == 0
                    last = hc == NHC - 1
                    sl = hc % 2
                    for half in range(2):
                        pgs, pgq = (PG[0], PG[1]) if half == 0 else (PG[2], PG[3])
                        for c in range(2):
                            co = half * 1024 + c * 512
                            t.matmul(pgs[0:1, c * 512:(c + 1) * 512],
                                     ones_col[:], xb_sl[sl][:, co:co + 512],
                                     start=first, stop=last)
                            i2 = t.matmul(pgq[0:1, c * 512:(c + 1) * 512],
                                          ones_col[:], xsq_sl[sl][:, co:co + 512],
                                          start=first, stop=last)
                    pe.inc(i2)
            sums_pe = pe.n
            assert sums_pe == pe_base + NHC

            xn_marks = [[0] * NHC, [0] * NHC]
            for half in range(2):
                pgs, pgq = (PG[0], PG[1]) if half == 0 else (PG[2], PG[3])

                @block.vector
                def _(v, pgs=pgs, pgq=pgq):
                    v.wait_ge(pe.h, sums_pe)
                    vstep(v, v.tensor_scalar_mul(m_row, pgs[0:1, :], 1.0 / H))
                    vstep(v, v.tensor_scalar_mul(msq_row, pgq[0:1, :], 1.0 / H))
                    vstep(v, v.tensor_mul(var_row, m_row, m_row))
                    dve.inc(v.tensor_sub(var_row, msq_row, var_row))
                v_rows = dve.n

                @block.scalar
                def _(s):
                    s.wait_ge(dve.h, v_rows)
                    act.inc(s.activation(rstd_f, var_row, AF.Ln,
                                         bias=eps_col[0:1, :]))
                    s.wait_ge(act.h, act.n)
                    act.inc(s.activation(rstd_f, rstd_f, AF.Exp, scale=-0.5))
                    s.wait_ge(act.h, act.n)
                    s.activation(m_bf[0:1, 0:1024], m_row, AF.Copy)
                    act.inc(s.activation(rstd_bf[0:1, 0:1024], rstd_f, AF.Copy))
                s_rows = act.n

                @block.tensor
                def _(t, pgs=pgs, pgq=pgq):
                    t.wait_ge(act.h, s_rows)
                    for c in range(2):
                        t.matmul(pgs[:, c * 512:(c + 1) * 512], ones_bc[:],
                                 m_bf[0:1, c * 512:(c + 1) * 512],
                                 start=True, stop=True)
                        i = t.matmul(pgq[:, c * 512:(c + 1) * 512], ones_bc[:],
                                     rstd_bf[0:1, c * 512:(c + 1) * 512],
                                     start=True, stop=True)
                    pe.inc(i)
                bc_pe = pe.n

                @block.vector
                def _(v, half=half, pgs=pgs, pgq=pgq):
                    v.wait_ge(pe.h, bc_pe)
                    co = half * 1024
                    for hc in range(NHC):
                        vstep(v, v.scalar_tensor_tensor(
                            t1_sl, x[hc][:, co:co + 1024], 1.0, pgs[:],
                            ALU.mult, ALU.subtract))
                        i = v.tensor_mul(xn[hc][:, co:co + 1024], t1_sl, pgq[:])
                        if hc < NHC - 1:
                            vstep(v, i)
                        else:
                            dve.inc(i)
                        xn_marks[half][hc] = dve.n
            return xn_marks

        def emit_gate(xn_marks):
            @block.tensor
            def _(t):
                t.wait_ge(dl.h, init_dl)
                t.wait_ge(im.h, init_im)
                for half in range(2):
                    pg = PG[half]
                    for hc in range(NHC):
                        t.wait_ge(dve.h, xn_marks[half][hc])
                        for c in range(2):
                            co = half * 1024 + c * 512
                            i = t.matmul(pg[0:8, c * 512:(c + 1) * 512],
                                         wg[:, hc * 8:hc * 8 + 8],
                                         xn[hc][:, co:co + 512],
                                         start=(hc == 0), stop=(hc == NHC - 1))
                pe.inc(i)
            g_pe = pe.n

            @block.scalar
            def _(s):
                s.wait_ge(pe.h, g_pe)
                for half in range(2):
                    i = s.activation(lg_sb[:, half * 1024:(half + 1) * 1024],
                                     PG[half][0:8, :], AF.Identity, bias=bgc[:])
                act.inc(i)
            lg_act = act.n

            @block.tensor
            def _(t):
                t.wait_ge(act.h, lg_act)
                for tt in range(16):
                    i = t.transpose(PG[2][:, tt * 8:tt * 8 + 8],
                                    lg_sb[:, tt * 128:(tt + 1) * 128],
                                    ident[0:8, 0:8])
                pe.inc(i)
            tr_pe = pe.n

            @block.vector
            def _(v):
                v.wait_ge(pe.h, tr_pe)
                for tt in range(16):
                    vstep(v, v.tensor_copy(ohstage[:], PG[2][:, tt * 8:tt * 8 + 8]))
                    vstep(v, v.max(mx8[:], ohstage[:]))
                    vstep(v, v.max_index(idxu[:], mx8[:], ohstage[:]))
                    vstep(v, v.tensor_copy(idxf[:], idxu[:, 0:1]))
                    i = v.tensor_scalar(ohf[:, tt * 8:tt * 8 + 8], iota8[:],
                                        idxf[:], None, ALU.is_equal)
                    if tt < 15:
                        vstep(v, i)
                dve.inc(i)
            oh_dve = dve.n

            pe_base = pe.n
            act_base = act.n

            @block.tensor
            def _(t):
                t.wait_ge(dve.h, oh_dve)
                for tt in range(16):
                    if tt >= 2:
                        t.wait_ge(act.h, act_base + tt - 1)
                    i = t.transpose(PG[3][0:8, (tt % 2) * 512:(tt % 2) * 512 + 128],
                                    ohf[:, tt * 8:tt * 8 + 8], ident[:])
                    pe.inc(i)

            @block.scalar
            def _(s):
                for tt in range(16):
                    s.wait_ge(pe.h, pe_base + tt + 1)
                    i = s.copy(onehotT[:, tt * 128:(tt + 1) * 128],
                               PG[3][0:8, (tt % 2) * 512:(tt % 2) * 512 + 128])
                    act.inc(i)
            return act.n

        def emit_mask(e_idx, slot, oh_act, war_act):
            @block.tensor
            def _(t):
                t.wait_ge(act.h, max(oh_act, war_act))
                for half in range(2):
                    for c in range(2):
                        co = half * 1024 + c * 512
                        i = t.matmul(PG[half][:, c * 512:(c + 1) * 512],
                                     esel[:, e_idx * 128:(e_idx + 1) * 128],
                                     onehotT[:, co:co + 512],
                                     start=True, stop=True)
                pe.inc(i)
            mpe = pe.n

            @block.scalar
            def _(s):
                s.wait_ge(pe.h, mpe)
                for half in range(2):
                    i = s.copy(mask_sl[slot][:, half * 1024:(half + 1) * 1024],
                               PG[half][:])
                act.inc(i)
            return act.n

        def emit_ffn(x_ready, w_marks, rearm):
            xn_marks = emit_ln(x_ready)
            oh_act = emit_gate(xn_marks)
            mask_acts = {0: emit_mask(0, 0, oh_act, war_act=oh_act)}
            last_ps1_act = mask_acts[0]
            prev_gelu = [0, 0]
            prev_acc = [0, 0]
            for e in range(E):
                for tb in range(2):
                    mm1_marks = []
                    act_base = act.n
                    gelu_mark = lambda ic: act_base + ic + 1
                    dve_base = dve.n
                    mask_mark = lambda ic: dve_base + ic + 1
                    for ic in range(8):
                        pg = PG[ic % 2]

                        @block.tensor
                        def _(t, e=e, tb=tb, ic=ic, pg=pg,
                              lpa=last_ps1_act, pga=prev_gelu[ic % 2]):
                            if tb == 0 and ic == 0:
                                t.wait_ge(*w_marks[("w1", e)])
                            if ic >= 2:
                                t.wait_ge(act.h, gelu_mark(ic - 2))
                            else:
                                t.wait_ge(act.h, max(lpa, pga))
                            for hc in range(NHC):
                                if e == 0:
                                    t.wait_ge(dve.h, xn_marks[tb][hc])
                                for c in range(2):
                                    i = t.matmul(
                                        pg[:, c * 512:(c + 1) * 512],
                                        w1s[:, hc * 1024 + ic * 128:
                                            hc * 1024 + (ic + 1) * 128],
                                        xn[hc][:, tb * 1024 + c * 512:
                                               tb * 1024 + (c + 1) * 512],
                                        start=(hc == 0), stop=(hc == NHC - 1))
                            pe.inc(i)
                        mm1_marks.append(pe.n)
                    if tb == 1:
                        w1_free_pe = pe.n

                    @block.scalar
                    def _(s, e=e, marks=tuple(mm1_marks)):
                        for ic in range(8):
                            s.wait_ge(pe.h, marks[ic])
                            act.inc(s.activation(
                                h1t(ic), PG[ic % 2][:], AF.Gelu,
                                bias=b1c[:, e * 8 + ic:e * 8 + ic + 1]))
                    assert act.n == act_base + 8
                    prev_gelu = [gelu_mark(6), gelu_mark(7)]

                    @block.vector
                    def _(v, e=e, tb=tb, ma=mask_acts[e]):
                        v.wait_ge(act.h, ma)
                        for ic in range(8):
                            v.wait_ge(act.h, gelu_mark(ic))
                            dve.inc(v.tensor_mul(
                                h1t(ic), h1t(ic),
                                mask_sl[e % 2][:, tb * 1024:(tb + 1) * 1024]))
                    assert dve.n == dve_base + 8

                    if tb == 1 and e + 1 < E:
                        mask_acts[e + 1] = emit_mask(
                            e + 1, (e + 1) % 2, oh_act, war_act=gelu_mark(7))
                        last_ps1_act = mask_acts[e + 1]

                    mm2_marks = []
                    acc_base = dve.n if (tb == 1 or e + 1 >= E) else dve.n
                    acc_base = dve.n
                    acc_mark = lambda hc: acc_base + hc + 1
                    for hc in range(NHC):
                        pg2 = PG[2 + hc % 2]

                        @block.tensor
                        def _(t, e=e, tb=tb, hc=hc, pg2=pg2,
                              md=mask_mark(7), pac=prev_acc[hc % 2]):
                            if tb == 0 and hc == 0:
                                t.wait_ge(*w_marks[("w2", e)])
                            t.wait_ge(dve.h, md)
                            if hc >= 2:
                                t.wait_ge(dve.h, acc_mark(hc - 2))
                            else:
                                t.wait_ge(dve.h, pac)
                            first = True
                            if e == 0:
                                for c in range(2):
                                    t.matmul(pg2[:, c * 512:(c + 1) * 512],
                                             b2bf[:, hc * 128:(hc + 1) * 128],
                                             onehotT[:, tb * 1024 + c * 512:
                                                     tb * 1024 + (c + 1) * 512],
                                             start=True, stop=False)
                                first = False
                            for ic in range(8):
                                for c in range(2):
                                    i = t.matmul(
                                        pg2[:, c * 512:(c + 1) * 512],
                                        w2s[:, ic * 1024 + hc * 128:
                                            ic * 1024 + (hc + 1) * 128],
                                        h1t(ic)[:, c * 512:(c + 1) * 512],
                                        start=(first and ic == 0),
                                        stop=(ic == 7))
                            pe.inc(i)
                        mm2_marks.append(pe.n)

                    @block.vector
                    def _(v, tb=tb, marks=tuple(mm2_marks)):
                        for hc in range(NHC):
                            v.wait_ge(pe.h, marks[hc])
                            dve.inc(v.scalar_tensor_tensor(
                                x[hc][:, tb * 1024:(tb + 1) * 1024],
                                PG[2 + hc % 2][:], 0.5,
                                x[hc][:, tb * 1024:(tb + 1) * 1024],
                                ALU.mult, ALU.add))
                    assert dve.n == acc_base + 8
                    prev_acc = [acc_mark(6), acc_mark(7)]
                    if e == E - 1 and tb == 1:
                        final_acc = [acc_mark(hc) for hc in range(NHC)]

                if e + 1 < E:
                    @block.gpsimd
                    def _(g, e=e, wf=w1_free_pe, wf2=pe.n):
                        g.wait_ge(pe.h, wf)
                        w_marks[("w1", e + 1)] = (w1m.h, w1m.inc(
                            g.dma_start(w_dst(w1s), w_src(d_w1[e + 1])), 16))
                        g.wait_ge(pe.h, wf2)
                        w_marks[("w2", e + 1)] = (w2m.h, w2m.inc(
                            g.dma_start(w_dst(w2s), w_src(d_w2[e + 1])), 16))
            if rearm is not None:
                d1, d2 = rearm

                @block.gpsimd
                def _(g, wf=pe.n):
                    g.wait_ge(pe.h, wf)
                    w_marks[("w1", 0)] = (w1m.h, w1m.inc(
                        g.dma_start(w_dst(w1s), w_src(d1)), 16))
                    w_marks[("w2", 0)] = (w2m.h, w2m.inc(
                        g.dma_start(w_dst(w2s), w_src(d2)), 16))
            return [(dve.h, m) for m in final_acc]

        def emit_attn(x_ready, w_marks):
            astage = _CFG.get("astage", 4)
            xn_marks = emit_ln(x_ready)
            xn_all = xn_marks[1][NHC - 1]

            v_acts = []
            for tt in range(16):
                pg = PG[tt % 2]

                @block.tensor
                def _(t, tt=tt, pg=pg):
                    if tt == 0:
                        t.wait_ge(*w_marks["wv"])
                    if tt >= 2:
                        t.wait_ge(act.h, v_acts[tt - 2])
                    half = tt // 8
                    for hc in range(NHC):
                        if tt % 8 == 0:
                            t.wait_ge(dve.h, xn_marks[half][hc])
                        for c in range(2):
                            i = t.matmul(
                                pg[:, c * 512:(c + 1) * 512],
                                xn[hc][:, tt * 128:(tt + 1) * 128],
                                w1s[:, hc * 1024 + c * 512:
                                    hc * 1024 + (c + 1) * 512],
                                start=(hc == 0), stop=(hc == NHC - 1))
                    pe.inc(i)
                v_pe = pe.n

                @block.scalar
                def _(s, tt=tt, pg=pg, vp=v_pe):
                    s.wait_ge(pe.h, vp)
                    act.inc(s.copy(vtile(tt), pg[:]))
                v_acts.append(act.n)
            v_done_pe = pe.n
            if dbg:
                @block.sync
                def _(sy, va=v_acts[-1]):
                    sy.wait_ge(act.h, va)
                    st.inc(sy.dma_start(d_dbg_v, vat[:]), 16)
            if astage <= 1:
                return [(dve.h, xn_all)] * NHC

            @block.gpsimd
            def _(g, wf=v_done_pe):
                g.wait_ge(pe.h, wf)
                w_marks["wq"] = (w1m.h, w1m.inc(
                    g.dma_start(w_dst(w1s), w_src(d_wq)), 16))

            qk_marks = {}
            evac_marks = {}
            kvext_marks = {}
            proj_marks = {-1: 0}

            def emit_qk(oc, wslot, wmark_key, pg01_free):
                @block.tensor
                def _(t, oc=oc, pg01_free=pg01_free):
                    if oc == 0:
                        t.wait_ge(*w_marks[wmark_key])
                    t.wait_ge(act.h, pg01_free)
                    for hc in range(NHC):
                        if oc == 0:
                            t.wait_ge(dve.h, xn_marks[1][hc])
                        for half in range(2):
                            for c in range(2):
                                i = t.matmul(
                                    PG[half][:, c * 512:(c + 1) * 512],
                                    wslot[:, hc * 1024 + oc * 128:
                                          hc * 1024 + (oc + 1) * 128],
                                    xn[hc][:, half * 1024 + c * 512:
                                           half * 1024 + (c + 1) * 512],
                                    start=(hc == 0), stop=(hc == NHC - 1))
                    pe.inc(i)
                qk_marks[oc] = pe.n

            emit_qk(0, w2s, "wk", v_acts[-1])
            for oc in range(8):
                @block.scalar
                def _(s, qp=qk_marks[oc], pp=proj_marks[oc - 1]):
                    s.wait_ge(pe.h, max(qp, pp))
                    s.copy(kq_sb[:, 0:1024], PG[0][:])
                    s.copy(kq_sb[:, 1024:2048], PG[1][:])
                    s.activation(sq_sb[:, 0:1024], PG[0][:], AF.Square)
                    i = s.activation(sq_sb[:, 1024:2048], PG[1][:], AF.Square)
                    act.inc(i)
                evac_marks[oc] = act.n

                @block.tensor
                def _(t, ea=evac_marks[oc], kv_free=kvext_marks.get(oc - 1, 0)):
                    t.wait_ge(act.h, ea)
                    t.wait_ge(act.h, kv_free)
                    for half in range(2):
                        pgn = PG[2] if half == 0 else PG[3]
                        for c in range(2):
                            i = t.matmul(pgn[0:2, c * 512:(c + 1) * 512],
                                         bdiag[:],
                                         sq_sb[:, half * 1024 + c * 512:
                                               half * 1024 + (c + 1) * 512],
                                         start=True, stop=True)
                    pe.inc(i)
                nrm_pe = pe.n

                if oc + 1 < 8:
                    emit_qk(oc + 1, w2s, "wk", evac_marks[oc])
                elif astage > 2:
                    @block.gpsimd
                    def _(g, wf=qk_marks[7]):
                        g.wait_ge(pe.h, wf)
                        w_marks["wo"] = (w2m.h, w2m.inc(
                            g.dma_start(w_dst(w2s), w_src(d_wo)), 16))

                @block.scalar
                def _(s, np_=nrm_pe):
                    s.wait_ge(pe.h, np_)
                    for half in range(2):
                        pgn = PG[2] if half == 0 else PG[3]
                        act.inc(s.activation(pair_f, pgn[0:2, :], AF.Ln))
                        s.wait_ge(act.h, act.n)
                        i = s.activation(
                            nr_bf[0:2, half * 1024:(half + 1) * 1024],
                            pair_f, AF.Exp, scale=0.5)
                        if half == 0:
                            act.inc(i)
                            s.wait_ge(act.h, act.n)
                    act.inc(i)
                nr_act = act.n

                for hh in range(2):
                    @block.tensor
                    def _(t, hh=hh, na=nr_act, pea=kvext_marks.get((oc, 0), 0)):
                        t.wait_ge(act.h, na)
                        t.wait_ge(act.h, pea)
                        for tt in range(16):
                            v_ = PG[2][:, tt * 64:(tt + 1) * 64]
                            t.matmul(v_,
                                     kq_sb[hh * 64:(hh + 1) * 64,
                                           tt * 128:(tt + 1) * 128],
                                     rf_s[hh * 64:(hh + 1) * 64, :],
                                     start=True, stop=False)
                            i = t.matmul(v_,
                                         nr_bf[0:2, tt * 128:(tt + 1) * 128],
                                         negsel[0:2, hh * 64:(hh + 1) * 64],
                                         start=False, stop=True)
                        pe.inc(i)
                    proj_pe = pe.n
                    if hh == 1:
                        proj_marks[oc] = proj_pe

                    @block.scalar
                    def _(s, pp=proj_pe):
                        s.wait_ge(pe.h, pp)
                        act.inc(s.activation(kp_sb, PG[2][:], AF.Exp))
                    kp_act = act.n

                    @block.tensor
                    def _(t, oc=oc, hh=hh, ka=kp_act):
                        t.wait_ge(act.h, ka)
                        h_ = 2 * oc + hh
                        base = hh * 512
                        for tt in range(16):
                            t.matmul(PG[3][0:64, base:base + 64],
                                     kp_sb[:, tt * 64:(tt + 1) * 64],
                                     vtile(tt)[:, h_ * 64:h_ * 64 + 64],
                                     start=(tt == 0), stop=(tt == 15))
                            i = t.matmul(PG[2][0:64, 960 + hh:960 + hh + 1],
                                         kp_sb[:, tt * 64:(tt + 1) * 64],
                                         ones_col[:],
                                         start=(tt == 0), stop=(tt == 15))
                        pe.inc(i)
                    kv_pe = pe.n

                    @block.scalar
                    def _(s, oc=oc, hh=hh, kp=kv_pe, base=hh * 512):
                        s.wait_ge(pe.h, kp)
                        s.copy(kvext[hh * 64:(hh + 1) * 64,
                                     oc * 65:oc * 65 + 64],
                               PG[3][0:64, base:base + 64])
                        i = s.copy(kvext[hh * 64:(hh + 1) * 64,
                                         oc * 65 + 64:oc * 65 + 65],
                                   PG[2][0:64, 960 + hh:960 + hh + 1])
                        act.inc(i)
                    kvext_marks[(oc, hh)] = act.n
                kvext_marks[oc] = act.n

            if dbg:
                @block.sync
                def _(sy, ka=kvext_marks[7]):
                    sy.wait_ge(act.h, ka)
                    st.inc(sy.dma_start(d_dbg_kv, kvext[:]), 16)
            if astage <= 2:
                return [(dve.h, xn_all)] * NHC

            q_evac = {}
            ar_evac = {-1: kvext_marks[7]}
            out_marks = {-1: proj_marks[7]}
            sp_marks = [0, 0]
            qk_marks = {}
            emit_qk(0, w1s, "wq", evac_marks[7])
            for oc in range(8):
                @block.scalar
                def _(s, qp=qk_marks[oc], pp=out_marks[oc - 1]):
                    s.wait_ge(pe.h, max(qp, pp))
                    s.copy(kq_sb[:, 0:1024], PG[0][:])
                    s.copy(kq_sb[:, 1024:2048], PG[1][:])
                    s.activation(sq_sb[:, 0:1024], PG[0][:], AF.Square)
                    i = s.activation(sq_sb[:, 1024:2048], PG[1][:], AF.Square)
                    act.inc(i)
                q_evac[oc] = act.n

                @block.tensor
                def _(t, ea=q_evac[oc], pg23_free=ar_evac[oc - 1]):
                    t.wait_ge(act.h, ea)
                    t.wait_ge(act.h, pg23_free)
                    for half in range(2):
                        pgn = PG[2] if half == 0 else PG[3]
                        for c in range(2):
                            i = t.matmul(pgn[0:2, c * 512:(c + 1) * 512],
                                         bdiag[:],
                                         sq_sb[:, half * 1024 + c * 512:
                                               half * 1024 + (c + 1) * 512],
                                         start=True, stop=True)
                    pe.inc(i)
                nrm_pe = pe.n

                if oc + 1 < 8:
                    emit_qk(oc + 1, w1s, "wq", q_evac[oc])

                @block.scalar
                def _(s, np_=nrm_pe):
                    s.wait_ge(pe.h, np_)
                    for half in range(2):
                        pgn = PG[2] if half == 0 else PG[3]
                        act.inc(s.activation(pair_f, pgn[0:2, :], AF.Ln))
                        s.wait_ge(act.h, act.n)
                        i = s.activation(
                            nr_bf[0:2, half * 1024:(half + 1) * 1024],
                            pair_f, AF.Exp, scale=0.5)
                        if half == 0:
                            act.inc(i)
                            s.wait_ge(act.h, act.n)
                    act.inc(i)
                nr_act = act.n

                @block.tensor
                def _(t, na=nr_act):
                    t.wait_ge(act.h, na)
                    for hh in range(2):
                        for half in range(2):
                            pgn = PG[2] if half == 0 else PG[3]
                            for c in range(2):
                                v_ = pgn[hh * 64:(hh + 1) * 64,
                                         c * 512:(c + 1) * 512]
                                t.matmul(v_, rf_s[hh * 64:(hh + 1) * 64, :],
                                         kq_sb[hh * 64:(hh + 1) * 64,
                                               half * 1024 + c * 512:
                                               half * 1024 + (c + 1) * 512],
                                         start=True, stop=False)
                                i = t.matmul(
                                    v_, negsel[0:2, hh * 64:(hh + 1) * 64],
                                    nr_bf[0:2, half * 1024 + c * 512:
                                          half * 1024 + (c + 1) * 512],
                                    start=False, stop=True)
                    pe.inc(i)
                proj_pe = pe.n
                proj_marks[oc] = proj_pe

                @block.scalar
                def _(s, pp=proj_pe):
                    s.wait_ge(pe.h, pp)
                    s.activation(eq_sb[:, 0:1024], PG[2][:], AF.Exp)
                    i = s.activation(eq_sb[:, 1024:2048], PG[3][:], AF.Exp)
                    act.inc(i)
                eq_act = act.n

                prev_ev = eq_act
                for hh in range(2):
                    @block.tensor
                    def _(t, oc=oc, hh=hh, ea=eq_act, pev=prev_ev):
                        t.wait_ge(act.h, max(ea, pev))
                        for half in range(2):
                            pgo = PG[2] if half == 0 else PG[3]
                            for c in range(2):
                                i = t.matmul(
                                    pgo[0:65, c * 512:(c + 1) * 512],
                                    kvext[hh * 64:(hh + 1) * 64,
                                          oc * 65:(oc + 1) * 65],
                                    eq_sb[hh * 64:(hh + 1) * 64,
                                          half * 1024 + c * 512:
                                          half * 1024 + (c + 1) * 512],
                                    start=True, stop=True)
                        pe.inc(i)
                    out_pe = pe.n

                    @block.scalar
                    def _(s, oc=oc, hh=hh, op=out_pe, spm=sp_marks[0]):
                        s.wait_ge(pe.h, op)
                        s.wait_ge(sp.h, spm)
                        for half in range(2):
                            pgo = PG[2] if half == 0 else PG[3]
                            s.copy(araw(oc)[hh * 64:(hh + 1) * 64,
                                            half * 1024:(half + 1) * 1024],
                                   pgo[0:64, :])
                            i = s.copy(tmp_r[half][0:1, :], pgo[64:65, :])
                        act.inc(i)
                    prev_ev = act.n

                    @block.sync
                    def _(sy, oc=oc, hh=hh, ea=prev_ev):
                        sy.wait_ge(act.h, ea)
                        row = 2 * oc + hh
                        sp.inc(sy.dma_start(
                            nrm16[row:row + 1, 0:1024], tmp_r[0][:]), 16)
                        sp.inc(sy.dma_start(
                            nrm16[row:row + 1, 1024:2048], tmp_r[1][:]), 16)
                    sp_marks[0] = sp.n
                    if hh == 1:
                        out_marks[oc] = out_pe
                ar_evac[oc] = prev_ev
            prev_q_evac = ar_evac[7]

            if dbg:
                @block.sync
                def _(sy, qa=prev_q_evac, spn=sp.n):
                    sy.wait_ge(act.h, qa)
                    sy.wait_ge(sp.h, spn)
                    st.inc(sy.dma_start(d_dbg_ar, vat[:]), 16)
                    st.inc(sy.dma_start(d_dbg_nr, nrm16[:]), 16)
            if astage <= 3:
                return [(dve.h, xn_all)] * NHC

            @block.vector
            def _(v, spn=sp.n):
                v.wait_ge(sp.h, spn)
                vstep(v, v.tensor_scalar_add(nrm16[:], nrm16[:], 1e-6))
                with nc.allow_low_precision(
                        reason="1/(nrm+1e-6): fp32 DVE internals, bf16 store "
                               "at matmul-operand precision"):
                    dve.inc(v.reciprocal(nrm16[:], nrm16[:]))
            al_dve = dve.n

            sc_dves = []
            for oc in range(8):
                pg_sc = (PG[2], PG[3]) if oc % 2 == 0 else (PG[0], PG[1])

                @block.tensor
                def _(t, oc=oc, ad=al_dve, pqe=prev_q_evac, pg_sc=pg_sc,
                      qe7=q_evac[7]):
                    if oc == 0:
                        t.wait_ge(dve.h, ad)
                        t.wait_ge(act.h, pqe)
                    if oc == 1:
                        t.wait_ge(act.h, qe7)
                    if oc >= 2:
                        t.wait_ge(dve.h, sc_dves[oc - 2])
                    for half in range(2):
                        pgn = pg_sc[half]
                        for c in range(2):
                            i = t.matmul(pgn[:, c * 512:(c + 1) * 512],
                                         hsel[:, oc * 128:(oc + 1) * 128],
                                         nrm16[:, half * 1024 + c * 512:
                                               half * 1024 + (c + 1) * 512],
                                         start=True, stop=True)
                    pe.inc(i)
                hs_pe = pe.n

                @block.vector
                def _(v, oc=oc, hp=hs_pe, pg_sc=pg_sc):
                    v.wait_ge(pe.h, hp)
                    for half in range(2):
                        pgn = pg_sc[half]
                        i = v.tensor_mul(
                            araw(oc)[:, half * 1024:(half + 1) * 1024],
                            araw(oc)[:, half * 1024:(half + 1) * 1024],
                            pgn[:])
                    dve.inc(i)
                sc_dves.append(dve.n)

            wo_marks = []
            for oco in range(8):
                pg_pair = (PG[0], PG[1]) if oco % 2 == 0 else (PG[2], PG[3])

                @block.tensor
                def _(t, oco=oco, pg_pair=pg_pair):
                    if oco == 0:
                        t.wait_ge(*w_marks["wo"])
                        t.wait_ge(dve.h, sc_dves[-1])
                    if oco >= 2:
                        t.wait_ge(dve.h, wo_marks[oco - 2])
                    for hc in range(NHC):
                        for half in range(2):
                            pgn = pg_pair[half]
                            for c in range(2):
                                i = t.matmul(
                                    pgn[:, c * 512:(c + 1) * 512],
                                    w2s[:, hc * 1024 + oco * 128:
                                        hc * 1024 + (oco + 1) * 128],
                                    araw(hc)[:, half * 1024 + c * 512:
                                             half * 1024 + (c + 1) * 512],
                                    start=(hc == 0), stop=(hc == NHC - 1))
                    pe.inc(i)
                wo_pe = pe.n

                @block.vector
                def _(v, oco=oco, wp=wo_pe, pg_pair=pg_pair):
                    v.wait_ge(pe.h, wp)
                    for half in range(2):
                        pgn = pg_pair[half]
                        i = v.scalar_tensor_tensor(
                            x[oco][:, half * 1024:(half + 1) * 1024],
                            pgn[:], boc[:, oco:oco + 1],
                            x[oco][:, half * 1024:(half + 1) * 1024],
                            ALU.add, ALU.add)
                    dve.inc(i)
                wo_marks.append(dve.n)
            return [(dve.h, wo_marks[hc]) for hc in range(NHC)]

        xl_base = xl.n

        @block.sync
        def _(sy):
            for hc in range(NHC):
                xl.inc(sy.dma_start(x[hc][:],
                                    d_x[0, hc * 128:(hc + 1) * 128, :]), 16)

        w_marks = {}
        first_kind = sub[0] if sub else "f"

        @block.gpsimd
        def _(g):
            if first_kind == "f":
                w_marks[("w1", 0)] = (w1m.h, w1m.inc(
                    g.dma_start(w_dst(w1s), w_src(d_w1[0])), 16))
                w_marks[("w2", 0)] = (w2m.h, w2m.inc(
                    g.dma_start(w_dst(w2s), w_src(d_w2[0])), 16))
            else:
                w_marks["wv"] = (w1m.h, w1m.inc(
                    g.dma_start(w_dst(w1s), w_src(d_wv)), 16))
                w_marks["wk"] = (w2m.h, w2m.inc(
                    g.dma_start(w_dst(w2s), w_src(d_wk)), 16))

        x_ready = [(xl.h, xl.n)] * NHC
        for s_i in range(n_seq):
            for sub_i, kind in enumerate(sub):
                last = (s_i, sub_i) == (n_seq - 1, len(sub) - 1)
                nxt_kind = None
                if not last:
                    nxt_kind = sub[sub_i + 1] if sub_i + 1 < len(sub) else sub[0]
                if kind == "f":
                    rearm = None
                    if nxt_kind == "a":
                        rearm = (d_wv, d_wk)
                    elif nxt_kind == "f":
                        rearm = (d_w1[0], d_w2[0])
                    done = emit_ffn(x_ready, w_marks, rearm=rearm)
                    if rearm is not None and nxt_kind == "a":
                        w_marks["wv"] = w_marks[("w1", 0)]
                        w_marks["wk"] = w_marks[("w2", 0)]
                    x_ready = done
                elif kind == "a":
                    done = emit_attn(x_ready, w_marks)
                    x_ready = done
                    if nxt_kind == "f":
                        @block.gpsimd
                        def _(g, wf=pe.n):
                            g.wait_ge(pe.h, wf)
                            w_marks[("w1", 0)] = (w1m.h, w1m.inc(
                                g.dma_start(w_dst(w1s), w_src(d_w1[0])), 16))
                            w_marks[("w2", 0)] = (w2m.h, w2m.inc(
                                g.dma_start(w_dst(w2s), w_src(d_w2[0])), 16))
                    elif nxt_kind == "a":
                        @block.gpsimd
                        def _(g, wf=pe.n):
                            g.wait_ge(pe.h, wf)
                            w_marks["wv"] = (w1m.h, w1m.inc(
                                g.dma_start(w_dst(w1s), w_src(d_wv)), 16))
                            w_marks["wk"] = (w2m.h, w2m.inc(
                                g.dma_start(w_dst(w2s), w_src(d_wk)), 16))

            xl_base2 = xl.n

            @block.sync
            def _(sy, s_i=s_i, xr=x_ready):
                for hc in range(NHC):
                    sem, val = xr[hc]
                    sy.wait_ge(sem, val)
                    st.inc(sy.dma_start(d_y[s_i, hc * 128:(hc + 1) * 128, :],
                                        x[hc][:]), 16)
                if s_i + 1 < n_seq:
                    sy.wait_ge(st.h, st.n)
                    for hc in range(NHC):
                        xl.inc(sy.dma_start(
                            x[hc][:],
                            d_x[s_i + 1, hc * 128:(hc + 1) * 128, :]), 16)
            if s_i + 1 < n_seq:
                x_ready = [(xl.h, xl.n)] * NHC

        @block.sync
        def _(sy):
            sy.wait_ge(st.h, st.n)

    return nc


def kernel(L0_attn_x, L0_attn_w_qkv, L0_attn_w_out, L0_attn_b_out,
           L0_attn_random_features, L0_ffn_Wg, L0_ffn_bg, L0_ffn_W1,
           L0_ffn_b1, L0_ffn_W2, L0_ffn_b2):
    n_seq = _CFG["n_seq"]
    B = L0_attn_x.shape[0]
    n_cores = B // n_seq
    xt = np.ascontiguousarray(np.transpose(np.asarray(L0_attn_x), (0, 2, 1)))

    wqkv = np.asarray(L0_attn_w_qkv)
    wqT = np.ascontiguousarray(wqkv[0:1024].T)
    wkT = np.ascontiguousarray(wqkv[1024:2048].T)
    wvT = np.ascontiguousarray(wqkv[2048:3072].T)
    woT = np.ascontiguousarray(np.asarray(L0_attn_w_out).T)
    rf = np.asarray(L0_attn_random_features)
    rfdup = np.ascontiguousarray(np.tile(rf.T, (2, 1)))
    b1 = np.asarray(L0_ffn_b1)
    b1c = np.ascontiguousarray(
        b1.reshape(E, 8, 128).transpose(2, 0, 1).reshape(128, 64))
    wgc = np.ascontiguousarray(
        np.asarray(L0_ffn_Wg).reshape(8, 128, 8).transpose(1, 0, 2).reshape(128, 64))
    bgc = np.ascontiguousarray(np.asarray(L0_ffn_bg).reshape(8, 1))
    boc = np.ascontiguousarray(np.asarray(L0_attn_b_out).reshape(8, 128).T)
    ident = np.eye(128, dtype=np.float32)
    io8 = np.tile(np.arange(8, dtype=np.float32), (128, 1))
    negsel = np.zeros((2, 128), np.float32)
    negsel[0, 0:64] = -1.0
    negsel[1, 64:128] = -1.0
    hselm = np.zeros((16, H), np.float32)
    for hh in range(16):
        hselm[hh, hh * 64:(hh + 1) * 64] = 1.0
    eselm = np.zeros((8, H), np.float32)
    for e in range(8):
        eselm[e, e * 128:(e + 1) * 128] = 1.0

    shared = {
        "W1": np.ascontiguousarray(np.asarray(L0_ffn_W1), dtype=np.float32),
        "W2": np.ascontiguousarray(np.asarray(L0_ffn_W2), dtype=np.float32),
        "b1c": b1c.astype(np.float32),
        "b2": np.ascontiguousarray(np.asarray(L0_ffn_b2), dtype=np.float32),
        "wgc": wgc.astype(np.float32), "bgc": bgc.astype(np.float32),
        "wqT": wqT.astype(np.float32), "wkT": wkT.astype(np.float32),
        "wvT": wvT.astype(np.float32), "woT": woT.astype(np.float32),
        "rfdup": rfdup.astype(np.float32), "boc": boc.astype(np.float32),
        "ident": ident, "iota8": io8, "negsel": negsel,
        "hsel": hselm, "esel": eselm,
    }
    nc = build_program(n_seq, _CFG["sub"])
    in_maps = []
    for c in range(n_cores):
        m = dict(shared)
        m["x_t"] = np.ascontiguousarray(xt[c * n_seq:(c + 1) * n_seq])
        in_maps.append(m)
    res = run_bass_kernel_spmd(nc, in_maps, core_ids=list(range(n_cores)))
    global _LAST
    _LAST = res.results
    outs = [r["y_t"] for r in res.results]
    y = np.concatenate(outs, axis=0)
    return np.ascontiguousarray(np.transpose(y, (0, 2, 1)))


# version 05
# speedup vs baseline: 1.5243x; 1.5243x over previous
import contextlib
import numpy as np
import concourse.bass as bass
import concourse.mybir as mybir
from concourse.bass_utils import run_bass_kernel_spmd
from concourse.bass_core_rust import IndirectOffsetOnAxis

F32 = mybir.dt.float32
BF16 = mybir.dt.bfloat16
U32 = mybir.dt.uint32
AF = mybir.ActivationFunctionType
ALU = mybir.AluOpType

H = 1024
T = 2048
E = 8
HD = 64
NHC = H // 128

_CFG = {"n_seq": 2, "sub": ("f", "a", "f")}
_LAST = None


class Sem:
    def __init__(self, es, nc, name):
        self.h = es.enter_context(nc.semaphore(name))
        self.n = 0

    def inc(self, instr, k=1):
        instr.then_inc(self.h, k)
        self.n += k
        return self.n


def build_program(n_seq, sub):
    nc = bass.Bass("TRN2", target_bir_lowering=False)

    d_x = nc.dram_tensor("x_t", [n_seq, H, T], F32, kind="ExternalInput").ap()
    d_y = nc.dram_tensor("y_t", [n_seq, H, T], F32, kind="ExternalOutput").ap()
    d_w1 = nc.dram_tensor("W1", [E, H, H], F32, kind="ExternalInput").ap()
    d_w2 = nc.dram_tensor("W2", [E, H, H], F32, kind="ExternalInput").ap()
    d_b1c = nc.dram_tensor("b1c", [128, 64], F32, kind="ExternalInput").ap()
    d_b2 = nc.dram_tensor("b2", [E, H], F32, kind="ExternalInput").ap()
    d_wgc = nc.dram_tensor("wgc", [128, 64], F32, kind="ExternalInput").ap()
    d_bgc = nc.dram_tensor("bgc", [8, 1], F32, kind="ExternalInput").ap()
    d_wq = nc.dram_tensor("wqT", [H, H], F32, kind="ExternalInput").ap()
    d_wk = nc.dram_tensor("wkT", [H, H], F32, kind="ExternalInput").ap()
    d_wv = nc.dram_tensor("wvT", [H, H], F32, kind="ExternalInput").ap()
    d_wo = nc.dram_tensor("woT", [H, H], F32, kind="ExternalInput").ap()
    d_rf = nc.dram_tensor("rfdup", [128, HD], F32, kind="ExternalInput").ap()
    d_boc = nc.dram_tensor("boc", [128, 8], F32, kind="ExternalInput").ap()
    d_ident = nc.dram_tensor("ident", [128, 128], F32, kind="ExternalInput").ap()
    d_iota8 = nc.dram_tensor("iota8", [128, 8], F32, kind="ExternalInput").ap()
    d_negsel = nc.dram_tensor("negsel", [2, 128], F32, kind="ExternalInput").ap()
    d_hsel = nc.dram_tensor("hsel", [16, H], F32, kind="ExternalInput").ap()
    d_esel = nc.dram_tensor("esel", [8, H], F32, kind="ExternalInput").ap()
    d_lt = nc.dram_tensor("ltstrict", [128, 128], F32, kind="ExternalInput").ap()
    d_on = nc.dram_tensor("ones2d", [128, 128], F32, kind="ExternalInput").ap()
    d_xs = nc.dram_tensor("xs_scr", [6144, H], BF16, kind="Internal").ap()
    d_ys = nc.dram_tensor("ys_scr", [6144, H], BF16, kind="Internal").ap()
    d_xnd = nc.dram_tensor("xnd_scr", [H, T], BF16, kind="Internal").ap()

    dbg = _CFG.get("dbg", False)
    if dbg:
        d_dbg_xn = nc.dram_tensor("dbg_xn", [H, T], BF16, kind="ExternalOutput").ap()
        d_dbg_lg = nc.dram_tensor("dbg_lg", [8, T], F32, kind="ExternalOutput").ap()
        d_dbg_oh = nc.dram_tensor("dbg_oh", [8, T], BF16, kind="ExternalOutput").ap()
        d_dbg_h1 = nc.dram_tensor("dbg_h1", [128, 8192], BF16, kind="ExternalOutput").ap()
        d_dbg_mk = nc.dram_tensor("dbg_mk", [128, T], BF16, kind="ExternalOutput").ap()
        d_dbg_xe = nc.dram_tensor("dbg_xe", [E, H, T], F32, kind="ExternalOutput").ap()
        d_dbg_v = nc.dram_tensor("dbg_v", [128, 16384], BF16, kind="ExternalOutput").ap()
        d_dbg_kv = nc.dram_tensor("dbg_kv", [128, 520], BF16, kind="ExternalOutput").ap()
        d_dbg_ar = nc.dram_tensor("dbg_ar", [128, 16384], BF16, kind="ExternalOutput").ap()
        d_dbg_nr = nc.dram_tensor("dbg_nr", [16, T], BF16, kind="ExternalOutput").ap()
    es = contextlib.ExitStack()
    with es:
        sbt = lambda name, shape, dt: es.enter_context(nc.sbuf_tensor(name, shape, dt))
        pst = lambda name, shape: es.enter_context(nc.psum_tensor(name, shape, F32))

        x = [sbt(f"x{i}", [128, T], F32) for i in range(NHC)]
        xn = [sbt(f"xn{i}", [128, T], BF16) for i in range(NHC)]
        w1s = sbt("w1s", [128, 8192], BF16)
        w2s = sbt("w2s", [128, 8192], BF16)
        h1m = sbt("h1m", [128, 8192], BF16)
        vat = sbt("vat", [128, 16384], BF16)
        ar32 = sbt("ar32", [128, 3072], F32)
        ident = sbt("sb_ident", [128, 128], F32)
        iota8 = sbt("sb_iota8", [128, 8], F32)
        negsel = sbt("sb_negsel", [2, 128], BF16)
        hsel = sbt("sb_hsel", [16, H], BF16)
        esel = sbt("sb_esel", [8, H], BF16)
        rf_f = sbt("rf_f", [128, HD], F32)
        rf_s = sbt("rf_s", [128, HD], BF16)
        b1c = sbt("sb_b1c", [128, 64], F32)
        b2bf = sbt("b2bf", [8, H], BF16)
        wg = sbt("wg", [128, 64], BF16)
        bgc = sbt("sb_bgc", [8, 1], F32)
        boc = sbt("sb_boc", [128, 8], F32)
        ones_col = sbt("ones_col", [128, 1], BF16)
        ones_bc = sbt("ones_bc", [1, 128], BF16)
        bdiag = sbt("bdiag", [128, 2], BF16)
        eps_col = sbt("eps_col", [1, 1], F32)
        kvext = sbt("kvext", [128, 520], BF16)
        mx8 = sbt("mx8", [128, 8], F32)
        idxu = sbt("idxu", [128, 8], U32)
        idxf = sbt("idxf", [128, 1], F32)
        ohf = sbt("ohf", [128, 128], F32)
        ohstage = sbt("ohstage", [128, 8], F32)
        ltst = sbt("sb_ltst", [128, 128], BF16)
        ones2d = sbt("sb_ones2d", [128, 128], BF16)
        idxs = sbt("idxs", [128, 16], F32)
        slotu = sbt("slotu", [128, 16], U32)
        ohb = sbt("ohb", [128, 128], BF16)
        rank_c = sbt("rank_c", [128, 1], F32)
        slot_f = sbt("slot_f", [128, 1], F32)
        ptmp = sbt("ptmp", [128, 8], F32)
        zt = sbt("zt", [128, 512], BF16)
        nrm16 = sbt("nrm16", [16, T], BF16)
        tmp_r = [sbt("tmp_r0", [1, 1024], BF16), sbt("tmp_r1", [1, 1024], BF16)]

        h1t = lambda ic: h1m[:, ic * 1024:(ic + 1) * 1024]
        xb_sl = [h1m[:, 0:2048], h1m[:, 2048:4096]]
        xsq_sl = [h1m[:, 4096:6144], h1m[:, 6144:8192]]
        kq_sb = h1m[:, 0:2048]
        sq_sb = h1m[:, 2048:4096]
        kp_sb = h1m[:, 4096:5120]
        nr_bf = h1m[0:2, 5120:7168]
        eq_sb = h1m[:, 2048:4096]
        mask_sl = [vat[:, 0:2048], vat[:, 2048:4096]]
        onehotT = vat[0:8, 4096:6144]
        m_bf = vat[0:1, 6144:7168]
        rstd_bf = vat[0:1, 7168:8192]
        vtile = lambda tt: vat[:, tt * 1024:(tt + 1) * 1024]
        araw = lambda oc: vat[:, oc * 2048:(oc + 1) * 2048]
        h1s = lambda ic: h1m[:, ic * 512:(ic + 1) * 512]
        xnt_b = [h1m[:, 4096 + 1024 * i:4096 + 1024 * (i + 1)] for i in range(2)]
        yt_b = [h1m[:, 6144 + 1024 * i:6144 + 1024 * (i + 1)] for i in range(2)]
        xgt = lambda sl, hc: vat[:, sl * 4096 + hc * 512:sl * 4096 + (hc + 1) * 512]
        p_sb = ar32[0:8, 0:2048]
        ytf_b = [ar32[:, 0:1024], ar32[:, 1024:2048]]
        t1_sl = ar32[:, 0:1024]
        m_row = ar32[0:1, 1024:2048]
        pair_f = ar32[0:2, 1024:2048]
        msq_row = ar32[0:1, 2048:3072]
        var_row = ar32[0:1, 0:1024]
        rstd_f = ar32[0:1, 2048:3072]
        lg_sb = ar32[0:8, 0:2048]

        PG = [pst(f"PG{i}", [128, 1024]) for i in range(4)]

        dl = Sem(es, nc, "dl")
        w1m = Sem(es, nc, "w1m")
        w2m = Sem(es, nc, "w2m")
        im = Sem(es, nc, "im")
        xl = Sem(es, nc, "xl")
        pe = Sem(es, nc, "pe")
        act = Sem(es, nc, "act")
        dve = Sem(es, nc, "dve")
        sp = Sem(es, nc, "sp")
        st = Sem(es, nc, "st")
        xd = Sem(es, nc, "xd")
        xtl = [Sem(es, nc, "xtl0"), Sem(es, nc, "xtl1")]
        sct = [Sem(es, nc, "sct0"), Sem(es, nc, "sct1")]
        xgl = [Sem(es, nc, "xgl0"), Sem(es, nc, "xgl1")]
        ysd = [Sem(es, nc, "ysd0"), Sem(es, nc, "ysd1")]
        ygt = [Sem(es, nc, "ygt0"), Sem(es, nc, "ygt1")]

        block = es.enter_context(nc.Block())

        @block.gpsimd
        def _(g):
            for dst, src in [(ident[:], d_ident), (iota8[:], d_iota8),
                             (negsel[:], d_negsel), (hsel[:], d_hsel),
                             (esel[:], d_esel), (rf_f[:], d_rf),
                             (ltst[:], d_lt), (ones2d[:], d_on),
                             (b1c[:], d_b1c), (b2bf[:], d_b2),
                             (wg[:], d_wgc), (bgc[:], d_bgc), (boc[:], d_boc)]:
                dl.inc(g.dma_start(dst, src), 16)
            im.inc(g.memset(ones_col[:], 1.0))
            im.inc(g.memset(ones_bc[:], 1.0))
            im.inc(g.memset(bdiag[:], 0.0))
            g.wait_ge(im.h, im.n)
            im.inc(g.memset(bdiag[0:64, 0:1], 1.0))
            g.wait_ge(im.h, im.n)
            im.inc(g.memset(bdiag[64:128, 1:2], 1.0))
            im.inc(g.memset(eps_col[:], 1e-5))
            im.inc(g.memset(zt[:], 0.0))
            g.wait_ge(im.h, im.n)
            for r in range(48):
                for c in range(2):
                    dl.inc(g.dma_start(
                        d_xs[r * 128:(r + 1) * 128, c * 512:(c + 1) * 512],
                        zt[:]), 16)
                    dl.inc(g.dma_start(
                        d_ys[r * 128:(r + 1) * 128, c * 512:(c + 1) * 512],
                        zt[:]), 16)
        init_dl = dl.n
        init_im = im.n

        @block.scalar
        def _(s):
            s.wait_ge(dl.h, init_dl)
            s.wait_ge(im.h, init_im)
            act.inc(s.activation(rf_s[:], rf_f[:], AF.Copy, scale=0.125))

        def vstep(v, instr):
            dve.inc(instr)
            v.wait_ge(dve.h, dve.n)

        def w_src(d_w):
            return d_w.rearrange("(c p) i -> p c i", p=128)

        def w_dst(slot):
            return slot[:].rearrange("p (c i) -> p c i", c=8)

        def emit_ln(x_ready):
            pe_base = pe.n
            pe_mark = lambda hc: pe_base + hc + 1
            act_base = act.n
            dve_base = dve.n

            @block.scalar
            def _(s):
                for hc in range(NHC):
                    sem, val = x_ready[hc]
                    s.wait_ge(sem, val)
                    if hc == 0:
                        s.wait_ge(x_ready[NHC - 1][0], x_ready[NHC - 1][1])
                    if hc >= 2:
                        s.wait_ge(pe.h, pe_mark(hc - 2))
                    act.inc(s.activation(xb_sl[hc % 2], x[hc][:], AF.Copy))
            assert act.n == act_base + NHC

            @block.vector
            def _(v):
                for hc in range(NHC):
                    sem, val = x_ready[hc]
                    v.wait_ge(sem, val)
                    if hc == 0:
                        v.wait_ge(x_ready[NHC - 1][0], x_ready[NHC - 1][1])
                    if hc >= 2:
                        v.wait_ge(pe.h, pe_mark(hc - 2))
                    dve.inc(v.tensor_mul(xsq_sl[hc % 2], x[hc][:], x[hc][:]))
            assert dve.n == dve_base + NHC

            @block.tensor
            def _(t):
                t.wait_ge(im.h, init_im)
                for hc in range(NHC):
                    t.wait_ge(act.h, act_base + hc + 1)
                    t.wait_ge(dve.h, dve_base + hc + 1)
                    first = hc == 0
                    last = hc == NHC - 1
                    sl = hc % 2
                    for half in range(2):
                        pgs, pgq = (PG[0], PG[1]) if half == 0 else (PG[2], PG[3])
                        for c in range(2):
                            co = half * 1024 + c * 512
                            t.matmul(pgs[0:1, c * 512:(c + 1) * 512],
                                     ones_col[:], xb_sl[sl][:, co:co + 512],
                                     start=first, stop=last)
                            i2 = t.matmul(pgq[0:1, c * 512:(c + 1) * 512],
                                          ones_col[:], xsq_sl[sl][:, co:co + 512],
                                          start=first, stop=last)
                    pe.inc(i2)
            sums_pe = pe.n
            assert sums_pe == pe_base + NHC

            xn_marks = [[0] * NHC, [0] * NHC]
            for half in range(2):
                pgs, pgq = (PG[0], PG[1]) if half == 0 else (PG[2], PG[3])

                @block.vector
                def _(v, pgs=pgs, pgq=pgq):
                    v.wait_ge(pe.h, sums_pe)
                    vstep(v, v.tensor_scalar_mul(m_row, pgs[0:1, :], 1.0 / H))
                    vstep(v, v.tensor_scalar_mul(msq_row, pgq[0:1, :], 1.0 / H))
                    vstep(v, v.tensor_mul(var_row, m_row, m_row))
                    dve.inc(v.tensor_sub(var_row, msq_row, var_row))
                v_rows = dve.n

                @block.scalar
                def _(s):
                    s.wait_ge(dve.h, v_rows)
                    act.inc(s.activation(rstd_f, var_row, AF.Ln,
                                         bias=eps_col[0:1, :]))
                    s.wait_ge(act.h, act.n)
                    act.inc(s.activation(rstd_f, rstd_f, AF.Exp, scale=-0.5))
                    s.wait_ge(act.h, act.n)
                    s.activation(m_bf[0:1, 0:1024], m_row, AF.Copy)
                    act.inc(s.activation(rstd_bf[0:1, 0:1024], rstd_f, AF.Copy))
                s_rows = act.n

                @block.tensor
                def _(t, pgs=pgs, pgq=pgq):
                    t.wait_ge(act.h, s_rows)
                    for c in range(2):
                        t.matmul(pgs[:, c * 512:(c + 1) * 512], ones_bc[:],
                                 m_bf[0:1, c * 512:(c + 1) * 512],
                                 start=True, stop=True)
                        i = t.matmul(pgq[:, c * 512:(c + 1) * 512], ones_bc[:],
                                     rstd_bf[0:1, c * 512:(c + 1) * 512],
                                     start=True, stop=True)
                    pe.inc(i)
                bc_pe = pe.n

                @block.vector
                def _(v, half=half, pgs=pgs, pgq=pgq):
                    v.wait_ge(pe.h, bc_pe)
                    co = half * 1024
                    for hc in range(NHC):
                        vstep(v, v.scalar_tensor_tensor(
                            t1_sl, x[hc][:, co:co + 1024], 1.0, pgs[:],
                            ALU.mult, ALU.subtract))
                        i = v.tensor_mul(xn[hc][:, co:co + 1024], t1_sl, pgq[:])
                        if hc < NHC - 1:
                            vstep(v, i)
                        else:
                            dve.inc(i)
                        xn_marks[half][hc] = dve.n
            return xn_marks

        def emit_gate(xn_marks):
            @block.tensor
            def _(t):
                t.wait_ge(dl.h, init_dl)
                t.wait_ge(im.h, init_im)
                for half in range(2):
                    pg = PG[half]
                    for hc in range(NHC):
                        t.wait_ge(dve.h, xn_marks[half][hc])
                        for c in range(2):
                            co = half * 1024 + c * 512
                            i = t.matmul(pg[0:8, c * 512:(c + 1) * 512],
                                         wg[:, hc * 8:hc * 8 + 8],
                                         xn[hc][:, co:co + 512],
                                         start=(hc == 0), stop=(hc == NHC - 1))
                pe.inc(i)
            g_pe = pe.n

            @block.scalar
            def _(s):
                s.wait_ge(pe.h, g_pe)
                for half in range(2):
                    i = s.activation(lg_sb[:, half * 1024:(half + 1) * 1024],
                                     PG[half][0:8, :], AF.Identity, bias=bgc[:])
                act.inc(i)
            lg_act = act.n

            @block.tensor
            def _(t):
                t.wait_ge(act.h, lg_act)
                for tt in range(16):
                    i = t.transpose(PG[2][:, tt * 8:tt * 8 + 8],
                                    lg_sb[:, tt * 128:(tt + 1) * 128],
                                    ident[0:8, 0:8])
                pe.inc(i)
            tr_pe = pe.n

            @block.vector
            def _(v):
                v.wait_ge(pe.h, tr_pe)
                for tt in range(16):
                    vstep(v, v.tensor_copy(ohstage[:], PG[2][:, tt * 8:tt * 8 + 8]))
                    vstep(v, v.max(mx8[:], ohstage[:]))
                    vstep(v, v.max_index(idxu[:], mx8[:], ohstage[:]))
                    vstep(v, v.tensor_copy(idxf[:], idxu[:, 0:1]))
                    vstep(v, v.tensor_copy(idxs[:, tt:tt + 1], idxf[:]))
                    vstep(v, v.tensor_scalar(ohf[:, tt * 8:tt * 8 + 8], iota8[:],
                                             idxf[:], None, ALU.is_equal))
                    i = v.tensor_copy(ohb[:, tt * 8:tt * 8 + 8],
                                      ohf[:, tt * 8:tt * 8 + 8])
                    if tt < 15:
                        vstep(v, i)
                dve.inc(i)
            oh_dve = dve.n

            pe_base = pe.n
            act_base = act.n

            @block.tensor
            def _(t):
                t.wait_ge(dve.h, oh_dve)
                for tt in range(16):
                    if tt >= 2:
                        t.wait_ge(act.h, act_base + tt - 1)
                    i = t.transpose(PG[3][0:8, (tt % 2) * 512:(tt % 2) * 512 + 128],
                                    ohf[:, tt * 8:tt * 8 + 8], ident[:])
                    pe.inc(i)

            @block.scalar
            def _(s):
                for tt in range(16):
                    s.wait_ge(pe.h, pe_base + tt + 1)
                    i = s.copy(onehotT[:, tt * 128:(tt + 1) * 128],
                               PG[3][0:8, (tt % 2) * 512:(tt % 2) * 512 + 128])
                    act.inc(i)
            return {"oh_act": act.n, "oh_dve": oh_dve, "lg_act": lg_act}

        def emit_mask(e_idx, slot, oh_act, war_act):
            @block.tensor
            def _(t):
                t.wait_ge(act.h, max(oh_act, war_act))
                for half in range(2):
                    for c in range(2):
                        co = half * 1024 + c * 512
                        i = t.matmul(PG[half][:, c * 512:(c + 1) * 512],
                                     esel[:, e_idx * 128:(e_idx + 1) * 128],
                                     onehotT[:, co:co + 512],
                                     start=True, stop=True)
                pe.inc(i)
            mpe = pe.n

            @block.scalar
            def _(s):
                s.wait_ge(pe.h, mpe)
                for half in range(2):
                    i = s.copy(mask_sl[slot][:, half * 1024:(half + 1) * 1024],
                               PG[half][:])
                act.inc(i)
            return act.n

        def emit_ffn(x_ready, w_marks, rearm):
            xn_marks = emit_ln(x_ready)
            oh_act = emit_gate(xn_marks)["oh_act"]
            mask_acts = {0: emit_mask(0, 0, oh_act, war_act=oh_act)}
            last_ps1_act = mask_acts[0]
            prev_gelu = [0, 0]
            prev_acc = [0, 0]
            for e in range(E):
                abase = act.n
                g_mark = lambda tb, ic: abase + tb * 8 + ic + 1
                has_mask = e + 1 < E
                dbase = dve.n
                mk_mark = lambda tb, ic: dbase + tb * 16 + ic + 1
                ac_mark = lambda tb, hc: dbase + tb * 16 + 8 + hc + 1

                mm1_marks = [[0] * 8, [0] * 8]
                mm2_marks = [[0] * 8, [0] * 8]
                mask_mpe = 0
                for tb in range(2):
                    for ic in range(8):
                        pg = PG[ic % 2]

                        @block.tensor
                        def _(t, e=e, tb=tb, ic=ic, pg=pg,
                              lpa=last_ps1_act, pga=prev_gelu[ic % 2]):
                            if tb == 0 and ic == 0:
                                t.wait_ge(*w_marks[("w1", e)])
                            if tb == 0 and ic < 2:
                                t.wait_ge(act.h, max(lpa, pga))
                            else:
                                t.wait_ge(act.h, g_mark(tb, ic - 2) if ic >= 2
                                          else g_mark(tb - 1, ic + 6))
                            for hc in range(NHC):
                                if e == 0:
                                    t.wait_ge(dve.h, xn_marks[tb][hc])
                                for c in range(2):
                                    i = t.matmul(
                                        pg[:, c * 512:(c + 1) * 512],
                                        w1s[:, hc * 1024 + ic * 128:
                                            hc * 1024 + (ic + 1) * 128],
                                        xn[hc][:, tb * 1024 + c * 512:
                                               tb * 1024 + (c + 1) * 512],
                                        start=(hc == 0), stop=(hc == NHC - 1))
                            pe.inc(i)
                        mm1_marks[tb][ic] = pe.n
                    if tb == 1:
                        w1_free_pe = pe.n

                    if tb == 1 and has_mask:
                        @block.tensor
                        def _(t, e=e, oa=oh_act):
                            t.wait_ge(act.h, max(oa, g_mark(1, 7)))
                            for half in range(2):
                                for c in range(2):
                                    co = half * 1024 + c * 512
                                    i = t.matmul(
                                        PG[half][:, c * 512:(c + 1) * 512],
                                        esel[:, (e + 1) * 128:(e + 2) * 128],
                                        onehotT[:, co:co + 512],
                                        start=True, stop=True)
                            pe.inc(i)
                        mask_mpe = pe.n

                    for hc in range(NHC):
                        pg2 = PG[2 + hc % 2]

                        @block.tensor
                        def _(t, e=e, tb=tb, hc=hc, pg2=pg2,
                              pac=prev_acc[hc % 2]):
                            if tb == 0 and hc == 0:
                                t.wait_ge(*w_marks[("w2", e)])
                            t.wait_ge(dve.h, mk_mark(tb, 7))
                            if hc >= 2:
                                t.wait_ge(dve.h, ac_mark(tb, hc - 2))
                            elif tb == 1:
                                t.wait_ge(dve.h, ac_mark(0, hc + 6))
                            else:
                                t.wait_ge(dve.h, pac)
                            first = True
                            if e == 0:
                                for c in range(2):
                                    t.matmul(pg2[:, c * 512:(c + 1) * 512],
                                             b2bf[:, hc * 128:(hc + 1) * 128],
                                             onehotT[:, tb * 1024 + c * 512:
                                                     tb * 1024 + (c + 1) * 512],
                                             start=True, stop=False)
                                first = False
                            for ic in range(8):
                                for c in range(2):
                                    i = t.matmul(
                                        pg2[:, c * 512:(c + 1) * 512],
                                        w2s[:, ic * 1024 + hc * 128:
                                            ic * 1024 + (hc + 1) * 128],
                                        h1t(ic)[:, c * 512:(c + 1) * 512],
                                        start=(first and ic == 0),
                                        stop=(ic == 7))
                            pe.inc(i)
                        mm2_marks[tb][hc] = pe.n

                @block.scalar
                def _(s, e=e, m1=mm1_marks, mpe=mask_mpe, hm=has_mask):
                    for tb in range(2):
                        for ic in range(8):
                            s.wait_ge(pe.h, m1[tb][ic])
                            act.inc(s.activation(
                                h1t(ic), PG[ic % 2][:], AF.Gelu,
                                bias=b1c[:, e * 8 + ic:e * 8 + ic + 1]))
                    if hm:
                        s.wait_ge(pe.h, mpe)
                        for half in range(2):
                            act.inc(s.copy(
                                mask_sl[(e + 1) % 2][:, half * 1024:
                                                     (half + 1) * 1024],
                                PG[half][:]))
                assert act.n == abase + 16 + (2 if has_mask else 0)
                if has_mask:
                    mask_acts[e + 1] = act.n
                    last_ps1_act = act.n
                prev_gelu = [g_mark(1, 6), g_mark(1, 7)]

                @block.vector
                def _(v, e=e, ma=mask_acts[e], m2=mm2_marks):
                    for tb in range(2):
                        if tb == 0:
                            v.wait_ge(act.h, ma)
                        for ic in range(8):
                            v.wait_ge(act.h, g_mark(tb, ic))
                            dve.inc(v.tensor_mul(
                                h1t(ic), h1t(ic),
                                mask_sl[e % 2][:, tb * 1024:(tb + 1) * 1024]))
                        for hc in range(NHC):
                            v.wait_ge(pe.h, m2[tb][hc])
                            dve.inc(v.scalar_tensor_tensor(
                                x[hc][:, tb * 1024:(tb + 1) * 1024],
                                PG[2 + hc % 2][:], 0.5,
                                x[hc][:, tb * 1024:(tb + 1) * 1024],
                                ALU.mult, ALU.add))
                assert dve.n == dbase + 32
                prev_acc = [ac_mark(1, 6), ac_mark(1, 7)]
                if e == E - 1:
                    final_acc = [ac_mark(1, hc) for hc in range(NHC)]

                if e + 1 < E:
                    @block.gpsimd
                    def _(g, e=e, wf=w1_free_pe, wf2=pe.n):
                        g.wait_ge(pe.h, wf)
                        w_marks[("w1", e + 1)] = (w1m.h, w1m.inc(
                            g.dma_start(w_dst(w1s), w_src(d_w1[e + 1])), 16))
                        g.wait_ge(pe.h, wf2)
                        w_marks[("w2", e + 1)] = (w2m.h, w2m.inc(
                            g.dma_start(w_dst(w2s), w_src(d_w2[e + 1])), 16))
            if rearm is not None:
                d1, d2 = rearm

                @block.gpsimd
                def _(g, wf=pe.n):
                    g.wait_ge(pe.h, wf)
                    w_marks[("w1", 0)] = (w1m.h, w1m.inc(
                        g.dma_start(w_dst(w1s), w_src(d1)), 16))
                    w_marks[("w2", 0)] = (w2m.h, w2m.inc(
                        g.dma_start(w_dst(w2s), w_src(d2)), 16))
            return [(dve.h, m) for m in final_acc]

        def emit_attn(x_ready, w_marks):
            astage = _CFG.get("astage", 4)
            xn_marks = emit_ln(x_ready)
            xn_all = xn_marks[1][NHC - 1]

            v_acts = []
            for tt in range(16):
                pg = PG[tt % 2]

                @block.tensor
                def _(t, tt=tt, pg=pg):
                    if tt == 0:
                        t.wait_ge(*w_marks["wv"])
                    if tt >= 2:
                        t.wait_ge(act.h, v_acts[tt - 2])
                    half = tt // 8
                    for hc in range(NHC):
                        if tt % 8 == 0:
                            t.wait_ge(dve.h, xn_marks[half][hc])
                        for c in range(2):
                            i = t.matmul(
                                pg[:, c * 512:(c + 1) * 512],
                                xn[hc][:, tt * 128:(tt + 1) * 128],
                                w1s[:, hc * 1024 + c * 512:
                                    hc * 1024 + (c + 1) * 512],
                                start=(hc == 0), stop=(hc == NHC - 1))
                    pe.inc(i)
                v_pe = pe.n

                @block.scalar
                def _(s, tt=tt, pg=pg, vp=v_pe):
                    s.wait_ge(pe.h, vp)
                    act.inc(s.copy(vtile(tt), pg[:]))
                v_acts.append(act.n)
            v_done_pe = pe.n
            if dbg:
                @block.sync
                def _(sy, va=v_acts[-1]):
                    sy.wait_ge(act.h, va)
                    st.inc(sy.dma_start(d_dbg_v, vat[:]), 16)
            if astage <= 1:
                return [(dve.h, xn_all)] * NHC

            @block.gpsimd
            def _(g, wf=v_done_pe):
                g.wait_ge(pe.h, wf)
                w_marks["wq"] = (w1m.h, w1m.inc(
                    g.dma_start(w_dst(w1s), w_src(d_wq)), 16))

            qk_marks = {}
            evac_marks = {}
            kvext_marks = {}
            proj_marks = {-1: 0}

            def emit_qk(oc, wslot, wmark_key, pg01_free):
                @block.tensor
                def _(t, oc=oc, pg01_free=pg01_free):
                    if oc == 0:
                        t.wait_ge(*w_marks[wmark_key])
                    t.wait_ge(act.h, pg01_free)
                    for hc in range(NHC):
                        if oc == 0:
                            t.wait_ge(dve.h, xn_marks[1][hc])
                        for half in range(2):
                            for c in range(2):
                                i = t.matmul(
                                    PG[half][:, c * 512:(c + 1) * 512],
                                    wslot[:, hc * 1024 + oc * 128:
                                          hc * 1024 + (oc + 1) * 128],
                                    xn[hc][:, half * 1024 + c * 512:
                                           half * 1024 + (c + 1) * 512],
                                    start=(hc == 0), stop=(hc == NHC - 1))
                    pe.inc(i)
                qk_marks[oc] = pe.n

            emit_qk(0, w2s, "wk", v_acts[-1])
            prev_tail = v_acts[-1]
            for oc in range(8):
                abase = act.n
                evac_mark = abase + 4
                rows_mark = abase + 8
                kp_mark = [abase + 9, abase + 12]
                kvx_mark = [abase + 11, abase + 14]

                @block.tensor
                def _(t, ea=evac_mark, kv_free=prev_tail):
                    t.wait_ge(act.h, ea)
                    if kv_free:
                        t.wait_ge(act.h, kv_free)
                    for half in range(2):
                        pgn = PG[2] if half == 0 else PG[3]
                        for c in range(2):
                            i = t.matmul(pgn[0:2, c * 512:(c + 1) * 512],
                                         bdiag[:],
                                         sq_sb[:, half * 1024 + c * 512:
                                               half * 1024 + (c + 1) * 512],
                                         start=True, stop=True)
                    pe.inc(i)
                nrm_pe = pe.n

                if oc + 1 < 8:
                    emit_qk(oc + 1, w2s, "wk", evac_mark)
                elif astage > 2:
                    @block.gpsimd
                    def _(g, wf=qk_marks[7]):
                        g.wait_ge(pe.h, wf)
                        w_marks["wo"] = (w2m.h, w2m.inc(
                            g.dma_start(w_dst(w2s), w_src(d_wo)), 16))

                proj_pe = [0, 0]
                kv_pe = [0, 0]
                for hh in range(2):
                    @block.tensor
                    def _(t, hh=hh, na=rows_mark,
                          pea=(kvx_mark[0] if hh == 1 else 0)):
                        t.wait_ge(act.h, na)
                        if pea:
                            t.wait_ge(act.h, pea)
                        for tt in range(16):
                            v_ = PG[2][:, tt * 64:(tt + 1) * 64]
                            t.matmul(v_,
                                     kq_sb[hh * 64:(hh + 1) * 64,
                                           tt * 128:(tt + 1) * 128],
                                     rf_s[hh * 64:(hh + 1) * 64, :],
                                     start=True, stop=False)
                            i = t.matmul(v_,
                                         nr_bf[0:2, tt * 128:(tt + 1) * 128],
                                         negsel[0:2, hh * 64:(hh + 1) * 64],
                                         start=False, stop=True)
                        pe.inc(i)
                    proj_pe[hh] = pe.n
                    if hh == 1:
                        proj_marks[oc] = pe.n

                    @block.tensor
                    def _(t, oc=oc, hh=hh, ka=kp_mark[hh]):
                        t.wait_ge(act.h, ka)
                        h_ = 2 * oc + hh
                        base = hh * 512
                        for tt in range(16):
                            t.matmul(PG[3][0:64, base:base + 64],
                                     kp_sb[:, tt * 64:(tt + 1) * 64],
                                     vtile(tt)[:, h_ * 64:h_ * 64 + 64],
                                     start=(tt == 0), stop=(tt == 15))
                            i = t.matmul(PG[2][0:64, 960 + hh:960 + hh + 1],
                                         kp_sb[:, tt * 64:(tt + 1) * 64],
                                         ones_col[:],
                                         start=(tt == 0), stop=(tt == 15))
                        pe.inc(i)
                    kv_pe[hh] = pe.n

                @block.scalar
                def _(s, oc=oc, qp=qk_marks[oc], pp=proj_marks[oc - 1],
                      nrm=nrm_pe, ppe=tuple(proj_pe), kpe=tuple(kv_pe)):
                    s.wait_ge(pe.h, max(qp, pp))
                    act.inc(s.copy(kq_sb[:, 0:1024], PG[0][:]))
                    act.inc(s.copy(kq_sb[:, 1024:2048], PG[1][:]))
                    act.inc(s.activation(sq_sb[:, 0:1024], PG[0][:], AF.Square))
                    act.inc(s.activation(sq_sb[:, 1024:2048], PG[1][:],
                                         AF.Square))
                    s.wait_ge(pe.h, nrm)
                    for half in range(2):
                        pgn = PG[2] if half == 0 else PG[3]
                        act.inc(s.activation(pair_f, pgn[0:2, :], AF.Ln))
                        s.wait_ge(act.h, act.n)
                        act.inc(s.activation(
                            nr_bf[0:2, half * 1024:(half + 1) * 1024],
                            pair_f, AF.Exp, scale=0.5))
                        if half == 0:
                            s.wait_ge(act.h, act.n)
                    for hh in range(2):
                        s.wait_ge(pe.h, ppe[hh])
                        act.inc(s.activation(kp_sb, PG[2][:], AF.Exp))
                        s.wait_ge(pe.h, kpe[hh])
                        base = hh * 512
                        act.inc(s.copy(kvext[hh * 64:(hh + 1) * 64,
                                             oc * 65:oc * 65 + 64],
                                       PG[3][0:64, base:base + 64]))
                        act.inc(s.copy(kvext[hh * 64:(hh + 1) * 64,
                                             oc * 65 + 64:oc * 65 + 65],
                                       PG[2][0:64, 960 + hh:960 + hh + 1]))
                assert act.n == abase + 14
                prev_tail = kvx_mark[1]
                kvext_marks[oc] = kvx_mark[1]
            kvext_marks[7] = prev_tail

            if dbg:
                @block.sync
                def _(sy, ka=kvext_marks[7]):
                    sy.wait_ge(act.h, ka)
                    st.inc(sy.dma_start(d_dbg_kv, kvext[:]), 16)
            if astage <= 2:
                return [(dve.h, xn_all)] * NHC

            out_free = proj_marks[7]
            ar_prev = kvext_marks[7]
            sp_prev = 0
            qk_marks = {}
            emit_qk(0, w1s, "wq", evac_mark)
            for oc in range(8):
                abase = act.n
                evac_mark = abase + 4
                rows_mark = abase + 8
                eq_mark = abase + 10
                ar_mark = [abase + 14, abase + 18]

                @block.tensor
                def _(t, ea=evac_mark, pg23_free=ar_prev):
                    t.wait_ge(act.h, ea)
                    t.wait_ge(act.h, pg23_free)
                    for half in range(2):
                        pgn = PG[2] if half == 0 else PG[3]
                        for c in range(2):
                            i = t.matmul(pgn[0:2, c * 512:(c + 1) * 512],
                                         bdiag[:],
                                         sq_sb[:, half * 1024 + c * 512:
                                               half * 1024 + (c + 1) * 512],
                                         start=True, stop=True)
                    pe.inc(i)
                nrm_pe = pe.n

                if oc + 1 < 8:
                    emit_qk(oc + 1, w1s, "wq", evac_mark)

                @block.tensor
                def _(t, na=rows_mark):
                    t.wait_ge(act.h, na)
                    for hh in range(2):
                        for half in range(2):
                            pgn = PG[2] if half == 0 else PG[3]
                            for c in range(2):
                                v_ = pgn[hh * 64:(hh + 1) * 64,
                                         c * 512:(c + 1) * 512]
                                t.matmul(v_, rf_s[hh * 64:(hh + 1) * 64, :],
                                         kq_sb[hh * 64:(hh + 1) * 64,
                                               half * 1024 + c * 512:
                                               half * 1024 + (c + 1) * 512],
                                         start=True, stop=False)
                                i = t.matmul(
                                    v_, negsel[0:2, hh * 64:(hh + 1) * 64],
                                    nr_bf[0:2, half * 1024 + c * 512:
                                          half * 1024 + (c + 1) * 512],
                                    start=False, stop=True)
                    pe.inc(i)
                proj_pe = pe.n

                out_pe = [0, 0]
                for hh in range(2):
                    @block.tensor
                    def _(t, oc=oc, hh=hh,
                          ea=(eq_mark if hh == 0 else ar_mark[0])):
                        t.wait_ge(act.h, ea)
                        for half in range(2):
                            pgo = PG[2] if half == 0 else PG[3]
                            for c in range(2):
                                i = t.matmul(
                                    pgo[0:65, c * 512:(c + 1) * 512],
                                    kvext[hh * 64:(hh + 1) * 64,
                                          oc * 65:(oc + 1) * 65],
                                    eq_sb[hh * 64:(hh + 1) * 64,
                                          half * 1024 + c * 512:
                                          half * 1024 + (c + 1) * 512],
                                    start=True, stop=True)
                        pe.inc(i)
                    out_pe[hh] = pe.n

                @block.scalar
                def _(s, oc=oc, qp=qk_marks[oc], pp=out_free, nrm=nrm_pe,
                      prj=proj_pe, ope=tuple(out_pe), spm=sp_prev, spb=sp.n):
                    s.wait_ge(pe.h, max(qp, pp))
                    act.inc(s.copy(kq_sb[:, 0:1024], PG[0][:]))
                    act.inc(s.copy(kq_sb[:, 1024:2048], PG[1][:]))
                    act.inc(s.activation(sq_sb[:, 0:1024], PG[0][:], AF.Square))
                    act.inc(s.activation(sq_sb[:, 1024:2048], PG[1][:],
                                         AF.Square))
                    s.wait_ge(pe.h, nrm)
                    for half in range(2):
                        pgn = PG[2] if half == 0 else PG[3]
                        act.inc(s.activation(pair_f, pgn[0:2, :], AF.Ln))
                        s.wait_ge(act.h, act.n)
                        act.inc(s.activation(
                            nr_bf[0:2, half * 1024:(half + 1) * 1024],
                            pair_f, AF.Exp, scale=0.5))
                        if half == 0:
                            s.wait_ge(act.h, act.n)
                    s.wait_ge(pe.h, prj)
                    act.inc(s.activation(eq_sb[:, 0:1024], PG[2][:], AF.Exp))
                    act.inc(s.activation(eq_sb[:, 1024:2048], PG[3][:],
                                         AF.Exp))
                    for hh in range(2):
                        s.wait_ge(pe.h, ope[hh])
                        s.wait_ge(sp.h, spm if hh == 0 else spb + 32)
                        for half in range(2):
                            pgo = PG[2] if half == 0 else PG[3]
                            act.inc(s.copy(
                                araw(oc)[hh * 64:(hh + 1) * 64,
                                         half * 1024:(half + 1) * 1024],
                                pgo[0:64, :]))
                            act.inc(s.copy(tmp_r[half][0:1, :],
                                           pgo[64:65, :]))
                assert act.n == abase + 18
                out_free = out_pe[1]
                ar_prev = ar_mark[1]

                for hh in range(2):
                    @block.sync
                    def _(sy, oc=oc, hh=hh, ea=ar_mark[hh]):
                        sy.wait_ge(act.h, ea)
                        row = 2 * oc + hh
                        sp.inc(sy.dma_start(
                            nrm16[row:row + 1, 0:1024], tmp_r[0][:]), 16)
                        sp.inc(sy.dma_start(
                            nrm16[row:row + 1, 1024:2048], tmp_r[1][:]), 16)
                sp_prev = sp.n
            prev_q_evac = ar_prev
            q_evac = {7: act.n}

            if dbg:
                @block.sync
                def _(sy, qa=prev_q_evac, spn=sp.n):
                    sy.wait_ge(act.h, qa)
                    sy.wait_ge(sp.h, spn)
                    st.inc(sy.dma_start(d_dbg_ar, vat[:]), 16)
                    st.inc(sy.dma_start(d_dbg_nr, nrm16[:]), 16)
            if astage <= 3:
                return [(dve.h, xn_all)] * NHC

            @block.vector
            def _(v, spn=sp.n):
                v.wait_ge(sp.h, spn)
                vstep(v, v.tensor_scalar_add(nrm16[:], nrm16[:], 1e-6))
                with nc.allow_low_precision(
                        reason="1/(nrm+1e-6): fp32 DVE internals, bf16 store "
                               "at matmul-operand precision"):
                    dve.inc(v.reciprocal(nrm16[:], nrm16[:]))
            al_dve = dve.n

            sc_dves = []
            for oc in range(8):
                pg_sc = (PG[2], PG[3]) if oc % 2 == 0 else (PG[0], PG[1])

                @block.tensor
                def _(t, oc=oc, ad=al_dve, pqe=prev_q_evac, pg_sc=pg_sc,
                      qe7=q_evac[7]):
                    if oc == 0:
                        t.wait_ge(dve.h, ad)
                        t.wait_ge(act.h, pqe)
                    if oc == 1:
                        t.wait_ge(act.h, qe7)
                    if oc >= 2:
                        t.wait_ge(dve.h, sc_dves[oc - 2])
                    for half in range(2):
                        pgn = pg_sc[half]
                        for c in range(2):
                            i = t.matmul(pgn[:, c * 512:(c + 1) * 512],
                                         hsel[:, oc * 128:(oc + 1) * 128],
                                         nrm16[:, half * 1024 + c * 512:
                                               half * 1024 + (c + 1) * 512],
                                         start=True, stop=True)
                    pe.inc(i)
                hs_pe = pe.n

                @block.vector
                def _(v, oc=oc, hp=hs_pe, pg_sc=pg_sc):
                    v.wait_ge(pe.h, hp)
                    for half in range(2):
                        pgn = pg_sc[half]
                        i = v.tensor_mul(
                            araw(oc)[:, half * 1024:(half + 1) * 1024],
                            araw(oc)[:, half * 1024:(half + 1) * 1024],
                            pgn[:])
                    dve.inc(i)
                sc_dves.append(dve.n)

            wo_marks = []
            for oco in range(8):
                pg_pair = (PG[0], PG[1]) if oco % 2 == 0 else (PG[2], PG[3])

                @block.tensor
                def _(t, oco=oco, pg_pair=pg_pair):
                    if oco == 0:
                        t.wait_ge(*w_marks["wo"])
                        t.wait_ge(dve.h, sc_dves[-1])
                    if oco >= 2:
                        t.wait_ge(dve.h, wo_marks[oco - 2])
                    for hc in range(NHC):
                        for half in range(2):
                            pgn = pg_pair[half]
                            for c in range(2):
                                i = t.matmul(
                                    pgn[:, c * 512:(c + 1) * 512],
                                    w2s[:, hc * 1024 + oco * 128:
                                        hc * 1024 + (oco + 1) * 128],
                                    araw(hc)[:, half * 1024 + c * 512:
                                             half * 1024 + (c + 1) * 512],
                                    start=(hc == 0), stop=(hc == NHC - 1))
                    pe.inc(i)
                wo_pe = pe.n

                @block.vector
                def _(v, oco=oco, wp=wo_pe, pg_pair=pg_pair):
                    v.wait_ge(pe.h, wp)
                    for half in range(2):
                        pgn = pg_pair[half]
                        i = v.scalar_tensor_tensor(
                            x[oco][:, half * 1024:(half + 1) * 1024],
                            pgn[:], boc[:, oco:oco + 1],
                            x[oco][:, half * 1024:(half + 1) * 1024],
                            ALU.add, ALU.add)
                    dve.inc(i)
                wo_marks.append(dve.n)
            return [(dve.h, wo_marks[hc]) for hc in range(NHC)]

        def emit_ffn_sparse(x_ready, w_marks, rearm):
            xn_marks = emit_ln(x_ready)
            g = emit_gate(xn_marks)
            oh_act, oh_dve = g["oh_act"], g["oh_dve"]

            @block.tensor
            def _(t, oa=oh_act, od=oh_dve):
                t.wait_ge(act.h, oa)
                t.wait_ge(dve.h, od)
                for tc in range(16):
                    reg = PG[tc // 8][0:8, (tc % 8) * 128:(tc % 8) * 128 + 128]
                    for kc in range(tc + 1):
                        i = t.matmul(reg, ohb[:, kc * 8:kc * 8 + 8],
                                     ltst[:] if kc == tc else ones2d[:],
                                     start=(kc == 0), stop=(kc == tc))
                    pe.inc(i)
            p_pe = pe.n

            @block.scalar
            def _(s, pp=p_pe):
                s.wait_ge(pe.h, pp)
                act.inc(s.copy(p_sb[:, 0:1024], PG[0][0:8, :]))
                act.inc(s.copy(p_sb[:, 1024:2048], PG[1][0:8, :]))
            p_act = act.n

            @block.tensor
            def _(t, pa=p_act):
                t.wait_ge(act.h, pa)
                for tt in range(16):
                    i = t.transpose(PG[3][:, 128 + tt * 8:128 + tt * 8 + 8],
                                    p_sb[:, tt * 128:(tt + 1) * 128],
                                    ident[0:8, 0:8])
                    pe.inc(i)
            ptr_pe = pe.n

            @block.vector
            def _(v, pp=ptr_pe):
                v.wait_ge(pe.h, pp)
                for tt in range(16):
                    vstep(v, v.tensor_mul(ptmp[:],
                                          PG[3][:, 128 + tt * 8:128 + tt * 8 + 8],
                                          ohf[:, tt * 8:tt * 8 + 8]))
                    vstep(v, v.reduce_sum(rank_c[:], ptmp[:],
                                          axis=mybir.AxisListType.X))
                    vstep(v, v.scalar_tensor_tensor(
                        slot_f[:], idxs[:, tt:tt + 1], 512.0, rank_c[:],
                        ALU.mult, ALU.add))
                    i = v.tensor_copy(slotu[:, tt:tt + 1], slot_f[:])
                    if tt < 15:
                        vstep(v, i)
                dve.inc(i)
            slot_dve = dve.n

            @block.sync
            def _(sy):
                for hc in range(NHC):
                    sy.wait_ge(dve.h, xn_marks[1][hc])
                    xd.inc(sy.dma_start(d_xnd[hc * 128:(hc + 1) * 128, :],
                                        xn[hc][:]), 16)
            xd_full = xd.n

            sct_done = []
            for tt in range(16):
                par = tt % 2

                @block.sync
                def _(sy, tt=tt, par=par, xdf=xd_full):
                    if tt == 0:
                        sy.wait_ge(xd.h, xdf)
                    if tt >= 2:
                        sy.wait_ge(sct[par].h, sct_done[tt - 2])
                    xtl[par].inc(sy.dma_start(
                        xnt_b[par][:],
                        d_xnd[:, tt * 128:(tt + 1) * 128], transpose=True), 16)

                @block.gpsimd
                def _(g_, tt=tt, par=par, sd=slot_dve, xt=xtl[par].n):
                    g_.wait_ge(dve.h, sd)
                    g_.wait_ge(xtl[par].h, xt)
                    sct[par].inc(g_.indirect_dma_start(
                        d_xs, IndirectOffsetOnAxis(slotu[:, tt:tt + 1], 0),
                        xnt_b[par][:], None), 16)
                sct_done.append(sct[par].n)
            sct_full = (sct[0].n, sct[1].n)

            gelu_prev = [0, 0]
            yd_prev = (0, 0)
            xg_marks = []
            for e in range(E):
                @block.sync
                def _(sy, e=e, sf=sct_full):
                    if e == 0:
                        sy.wait_ge(sct[0].h, sf[0])
                        sy.wait_ge(sct[1].h, sf[1])
                    if e >= 2:
                        sy.wait_ge(pe.h, xg_marks[e - 2][-1])
                    for hc in range(NHC):
                        xgl[e % 2].inc(sy.dma_start(
                            xgt(e % 2, hc),
                            d_xs[e * 512:(e + 1) * 512,
                                 hc * 128:(hc + 1) * 128], transpose=True), 16)
                xg_full = xgl[e % 2].n

                abase = act.n
                g_mark = lambda ic: abase + ic + 1
                mm1_marks = []
                for ic in range(8):
                    @block.tensor
                    def _(t, e=e, ic=ic, xf=xg_full,
                          pga=gelu_prev[ic % 2], pa=p_act):
                        if ic == 0:
                            t.wait_ge(*w_marks[("w1", e)])
                            t.wait_ge(xgl[e % 2].h, xf)
                        if ic >= 2:
                            t.wait_ge(act.h, g_mark(ic - 2))
                        else:
                            t.wait_ge(act.h, max(pga, pa))
                        for hc in range(NHC):
                            i = t.matmul(
                                PG[ic % 2][:, 0:512],
                                w1s[:, hc * 1024 + ic * 128:
                                    hc * 1024 + (ic + 1) * 128],
                                xgt(e % 2, hc),
                                start=(hc == 0), stop=(hc == NHC - 1))
                        pe.inc(i)
                    mm1_marks.append(pe.n)
                xg_marks.append(tuple(mm1_marks))

                mm2_marks = []
                for sc in range(4):
                    pg2 = PG[2 + sc % 2]

                    @block.tensor
                    def _(t, e=e, sc=sc, pg2=pg2, sd=slot_dve):
                        t.wait_ge(*w_marks[("w2", e)]) if sc == 0 else None
                        t.wait_ge(act.h, abase + 8)
                        if sc >= 2:
                            t.wait_ge(act.h, abase + 8 + (sc - 2) + 1)
                        elif e > 0:
                            t.wait_ge(act.h, abase - 12 + 8 + (sc + 2) + 1)
                        else:
                            t.wait_ge(dve.h, sd)
                        for c in range(2):
                            t.matmul(pg2[:, c * 512:(c + 1) * 512],
                                     esel[:, e * 128:(e + 1) * 128],
                                     b2bf[:, c * 512:(c + 1) * 512],
                                     start=True, stop=False)
                        for ic in range(8):
                            for c in range(2):
                                i = t.matmul(
                                    pg2[:, c * 512:(c + 1) * 512],
                                    h1s(ic)[:, sc * 128:(sc + 1) * 128],
                                    w2s[:, ic * 1024 + c * 512:
                                        ic * 1024 + (c + 1) * 512],
                                    start=False, stop=(ic == 7))
                            pass
                        pe.inc(i)
                    mm2_marks.append(pe.n)

                yd_base = (ysd[0].n, ysd[1].n)

                @block.scalar
                def _(s, e=e, m1=tuple(mm1_marks), m2=tuple(mm2_marks),
                      ydp=yd_prev, ydb=yd_base):
                    for ic in range(8):
                        s.wait_ge(pe.h, m1[ic])
                        act.inc(s.activation(
                            h1s(ic), PG[ic % 2][:, 0:512], AF.Gelu,
                            bias=b1c[:, e * 8 + ic:e * 8 + ic + 1]))
                    for sc in range(4):
                        s.wait_ge(pe.h, m2[sc])
                        if sc < 2:
                            s.wait_ge(ysd[sc % 2].h, ydp[sc % 2])
                        else:
                            s.wait_ge(ysd[sc % 2].h, ydb[sc % 2] + 16)
                        act.inc(s.copy(yt_b[sc % 2][:], PG[2 + sc % 2][:]))
                assert act.n == abase + 12
                gelu_prev = [abase + 7, abase + 8]

                @block.sync
                def _(sy, e=e, ab=abase):
                    for sc in range(4):
                        sy.wait_ge(act.h, ab + 8 + sc + 1)
                        ysd[sc % 2].inc(sy.dma_start(
                            d_ys[e * 512 + sc * 128:e * 512 + (sc + 1) * 128, :],
                            yt_b[sc % 2][:]), 16)
                yd_prev = (ysd[0].n, ysd[1].n)

                if e + 1 < E:
                    @block.gpsimd
                    def _(g_, e=e, wf=mm1_marks[-1], wf2=pe.n):
                        g_.wait_ge(pe.h, wf)
                        w_marks[("w1", e + 1)] = (w1m.h, w1m.inc(
                            g_.dma_start(w_dst(w1s), w_src(d_w1[e + 1])), 16))
                        g_.wait_ge(pe.h, wf2)
                        w_marks[("w2", e + 1)] = (w2m.h, w2m.inc(
                            g_.dma_start(w_dst(w2s), w_src(d_w2[e + 1])), 16))
            ys_full = (ysd[0].n, ysd[1].n)

            cast_acts = []
            tr_pes = []
            acc_dves = []
            for tt in range(16):
                par = tt % 2

                @block.gpsimd
                def _(g_, tt=tt, par=par, yf=ys_full):
                    if tt == 0:
                        g_.wait_ge(ysd[0].h, yf[0])
                        g_.wait_ge(ysd[1].h, yf[1])
                    if tt >= 2:
                        g_.wait_ge(act.h, cast_acts[tt - 2])
                    ygt[par].inc(g_.indirect_dma_start(
                        xnt_b[par][:], None,
                        d_ys, IndirectOffsetOnAxis(slotu[:, tt:tt + 1], 0)), 16)

                @block.scalar
                def _(s, tt=tt, par=par, yg=ygt[par].n, pp=(tr_pes[tt - 2] if tt >= 2 else 0)):
                    s.wait_ge(ygt[par].h, yg)
                    if pp:
                        s.wait_ge(pe.h, pp)
                    act.inc(s.copy(ytf_b[par][:], xnt_b[par][:]))
                cast_acts.append(act.n)

                @block.tensor
                def _(t, tt=tt, par=par, ca=cast_acts[-1],
                      pd=(acc_dves[tt - 2] if tt >= 2 else 0)):
                    t.wait_ge(act.h, ca)
                    if pd:
                        t.wait_ge(dve.h, pd)
                    for hc in range(NHC):
                        i = t.transpose(PG[par][:, hc * 128:(hc + 1) * 128],
                                        ytf_b[par][:, hc * 128:(hc + 1) * 128],
                                        ident[:])
                        pe.inc(i)
                tr_pes.append(pe.n)

                @block.vector
                def _(v, tt=tt, par=par, tp=tr_pes[-1]):
                    v.wait_ge(pe.h, tp)
                    for hc in range(NHC):
                        dve.inc(v.scalar_tensor_tensor(
                            x[hc][:, tt * 128:(tt + 1) * 128],
                            PG[par][:, hc * 128:(hc + 1) * 128], 0.5,
                            x[hc][:, tt * 128:(tt + 1) * 128],
                            ALU.mult, ALU.add))
                acc_dves.append(dve.n)

            if rearm is not None:
                d1, d2 = rearm

                @block.gpsimd
                def _(g_, wf=pe.n):
                    g_.wait_ge(pe.h, wf)
                    w_marks[("w1", 0)] = (w1m.h, w1m.inc(
                        g_.dma_start(w_dst(w1s), w_src(d1)), 16))
                    w_marks[("w2", 0)] = (w2m.h, w2m.inc(
                        g_.dma_start(w_dst(w2s), w_src(d2)), 16))
            return [(dve.h, acc_dves[-1])] * NHC

        xl_base = xl.n

        @block.sync
        def _(sy):
            for hc in range(NHC):
                xl.inc(sy.dma_start(x[hc][:],
                                    d_x[0, hc * 128:(hc + 1) * 128, :]), 16)

        w_marks = {}
        first_kind = sub[0] if sub else "f"

        @block.gpsimd
        def _(g):
            if first_kind == "f":
                w_marks[("w1", 0)] = (w1m.h, w1m.inc(
                    g.dma_start(w_dst(w1s), w_src(d_w1[0])), 16))
                w_marks[("w2", 0)] = (w2m.h, w2m.inc(
                    g.dma_start(w_dst(w2s), w_src(d_w2[0])), 16))
            else:
                w_marks["wv"] = (w1m.h, w1m.inc(
                    g.dma_start(w_dst(w1s), w_src(d_wv)), 16))
                w_marks["wk"] = (w2m.h, w2m.inc(
                    g.dma_start(w_dst(w2s), w_src(d_wk)), 16))

        x_ready = [(xl.h, xl.n)] * NHC
        for s_i in range(n_seq):
            for sub_i, kind in enumerate(sub):
                last = (s_i, sub_i) == (n_seq - 1, len(sub) - 1)
                nxt_kind = None
                if not last:
                    nxt_kind = sub[sub_i + 1] if sub_i + 1 < len(sub) else sub[0]
                if kind == "f":
                    rearm = None
                    if nxt_kind == "a":
                        rearm = (d_wv, d_wk)
                    elif nxt_kind == "f":
                        rearm = (d_w1[0], d_w2[0])
                    if _CFG.get("sparse", True):
                        done = emit_ffn_sparse(x_ready, w_marks, rearm=rearm)
                    else:
                        done = emit_ffn(x_ready, w_marks, rearm=rearm)
                    if rearm is not None and nxt_kind == "a":
                        w_marks["wv"] = w_marks[("w1", 0)]
                        w_marks["wk"] = w_marks[("w2", 0)]
                    x_ready = done
                elif kind == "a":
                    done = emit_attn(x_ready, w_marks)
                    x_ready = done
                    if nxt_kind == "f":
                        @block.gpsimd
                        def _(g, wf=pe.n):
                            g.wait_ge(pe.h, wf)
                            w_marks[("w1", 0)] = (w1m.h, w1m.inc(
                                g.dma_start(w_dst(w1s), w_src(d_w1[0])), 16))
                            w_marks[("w2", 0)] = (w2m.h, w2m.inc(
                                g.dma_start(w_dst(w2s), w_src(d_w2[0])), 16))
                    elif nxt_kind == "a":
                        @block.gpsimd
                        def _(g, wf=pe.n):
                            g.wait_ge(pe.h, wf)
                            w_marks["wv"] = (w1m.h, w1m.inc(
                                g.dma_start(w_dst(w1s), w_src(d_wv)), 16))
                            w_marks["wk"] = (w2m.h, w2m.inc(
                                g.dma_start(w_dst(w2s), w_src(d_wk)), 16))

            xl_base2 = xl.n

            @block.sync
            def _(sy, s_i=s_i, xr=x_ready):
                for hc in range(NHC):
                    sem, val = xr[hc]
                    sy.wait_ge(sem, val)
                    st.inc(sy.dma_start(d_y[s_i, hc * 128:(hc + 1) * 128, :],
                                        x[hc][:]), 16)
                if s_i + 1 < n_seq:
                    sy.wait_ge(st.h, st.n)
                    for hc in range(NHC):
                        xl.inc(sy.dma_start(
                            x[hc][:],
                            d_x[s_i + 1, hc * 128:(hc + 1) * 128, :]), 16)
            if s_i + 1 < n_seq:
                x_ready = [(xl.h, xl.n)] * NHC

        @block.sync
        def _(sy):
            sy.wait_ge(st.h, st.n)

    return nc


def kernel(L0_attn_x, L0_attn_w_qkv, L0_attn_w_out, L0_attn_b_out,
           L0_attn_random_features, L0_ffn_Wg, L0_ffn_bg, L0_ffn_W1,
           L0_ffn_b1, L0_ffn_W2, L0_ffn_b2):
    n_seq = _CFG["n_seq"]
    B = L0_attn_x.shape[0]
    n_cores = B // n_seq
    xt = np.ascontiguousarray(np.transpose(np.asarray(L0_attn_x), (0, 2, 1)))

    wqkv = np.asarray(L0_attn_w_qkv)
    wqT = np.ascontiguousarray(wqkv[0:1024].T)
    wkT = np.ascontiguousarray(wqkv[1024:2048].T)
    wvT = np.ascontiguousarray(wqkv[2048:3072].T)
    woT = np.ascontiguousarray(np.asarray(L0_attn_w_out).T)
    rf = np.asarray(L0_attn_random_features)
    rfdup = np.ascontiguousarray(np.tile(rf.T, (2, 1)))
    b1 = np.asarray(L0_ffn_b1)
    b1c = np.ascontiguousarray(
        b1.reshape(E, 8, 128).transpose(2, 0, 1).reshape(128, 64))
    wgc = np.ascontiguousarray(
        np.asarray(L0_ffn_Wg).reshape(8, 128, 8).transpose(1, 0, 2).reshape(128, 64))
    bgc = np.ascontiguousarray(np.asarray(L0_ffn_bg).reshape(8, 1))
    boc = np.ascontiguousarray(np.asarray(L0_attn_b_out).reshape(8, 128).T)
    ident = np.eye(128, dtype=np.float32)
    io8 = np.tile(np.arange(8, dtype=np.float32), (128, 1))
    negsel = np.zeros((2, 128), np.float32)
    negsel[0, 0:64] = -1.0
    negsel[1, 64:128] = -1.0
    hselm = np.zeros((16, H), np.float32)
    for hh in range(16):
        hselm[hh, hh * 64:(hh + 1) * 64] = 1.0
    eselm = np.zeros((8, H), np.float32)
    for e in range(8):
        eselm[e, e * 128:(e + 1) * 128] = 1.0

    shared = {
        "W1": np.ascontiguousarray(np.asarray(L0_ffn_W1), dtype=np.float32),
        "W2": np.ascontiguousarray(np.asarray(L0_ffn_W2), dtype=np.float32),
        "b1c": b1c.astype(np.float32),
        "b2": np.ascontiguousarray(np.asarray(L0_ffn_b2), dtype=np.float32),
        "wgc": wgc.astype(np.float32), "bgc": bgc.astype(np.float32),
        "wqT": wqT.astype(np.float32), "wkT": wkT.astype(np.float32),
        "wvT": wvT.astype(np.float32), "woT": woT.astype(np.float32),
        "rfdup": rfdup.astype(np.float32), "boc": boc.astype(np.float32),
        "ident": ident, "iota8": io8, "negsel": negsel,
        "hsel": hselm, "esel": eselm,
        "ltstrict": np.triu(np.ones((128, 128), np.float32), k=1),
        "ones2d": np.ones((128, 128), np.float32),
    }
    nc = build_program(n_seq, _CFG["sub"])
    in_maps = []
    for c in range(n_cores):
        m = dict(shared)
        m["x_t"] = np.ascontiguousarray(xt[c * n_seq:(c + 1) * n_seq])
        in_maps.append(m)
    res = run_bass_kernel_spmd(nc, in_maps, core_ids=list(range(n_cores)))
    global _LAST
    _LAST = res.results
    outs = [r["y_t"] for r in res.results]
    y = np.concatenate(outs, axis=0)
    return np.ascontiguousarray(np.transpose(y, (0, 2, 1)))


# version 06
# speedup vs baseline: 1.5289x; 1.0030x over previous
import contextlib
import numpy as np
import concourse.bass as bass
import concourse.mybir as mybir
from concourse.bass_utils import run_bass_kernel_spmd
from concourse.bass_core_rust import IndirectOffsetOnAxis

F32 = mybir.dt.float32
BF16 = mybir.dt.bfloat16
U32 = mybir.dt.uint32
AF = mybir.ActivationFunctionType
ALU = mybir.AluOpType

H = 1024
T = 2048
E = 8
HD = 64
NHC = H // 128

_CFG = {"n_seq": 2, "sub": ("f", "a", "f")}
_LAST = None


class Sem:
    def __init__(self, es, nc, name):
        self.h = es.enter_context(nc.semaphore(name))
        self.n = 0

    def inc(self, instr, k=1):
        instr.then_inc(self.h, k)
        self.n += k
        return self.n


def build_program(n_seq, sub):
    nc = bass.Bass("TRN2", target_bir_lowering=False)

    d_x = nc.dram_tensor("x_t", [n_seq, H, T], F32, kind="ExternalInput").ap()
    d_y = nc.dram_tensor("y_t", [n_seq, H, T], F32, kind="ExternalOutput").ap()
    d_w1 = nc.dram_tensor("W1", [E, H, H], F32, kind="ExternalInput").ap()
    d_w2 = nc.dram_tensor("W2", [E, H, H], F32, kind="ExternalInput").ap()
    d_b1c = nc.dram_tensor("b1c", [128, 64], F32, kind="ExternalInput").ap()
    d_b2 = nc.dram_tensor("b2", [E, H], F32, kind="ExternalInput").ap()
    d_wgc = nc.dram_tensor("wgc", [128, 64], F32, kind="ExternalInput").ap()
    d_bgc = nc.dram_tensor("bgc", [8, 1], F32, kind="ExternalInput").ap()
    d_wq = nc.dram_tensor("wqT", [H, H], F32, kind="ExternalInput").ap()
    d_wk = nc.dram_tensor("wkT", [H, H], F32, kind="ExternalInput").ap()
    d_wv = nc.dram_tensor("wvT", [H, H], F32, kind="ExternalInput").ap()
    d_wo = nc.dram_tensor("woT", [H, H], F32, kind="ExternalInput").ap()
    d_rf = nc.dram_tensor("rfdup", [128, HD], F32, kind="ExternalInput").ap()
    d_boc = nc.dram_tensor("boc", [128, 8], F32, kind="ExternalInput").ap()
    d_ident = nc.dram_tensor("ident", [128, 128], F32, kind="ExternalInput").ap()
    d_iota8 = nc.dram_tensor("iota8", [128, 8], F32, kind="ExternalInput").ap()
    d_negsel = nc.dram_tensor("negsel", [2, 128], F32, kind="ExternalInput").ap()
    d_hsel = nc.dram_tensor("hsel", [16, H], F32, kind="ExternalInput").ap()
    d_esel = nc.dram_tensor("esel", [8, H], F32, kind="ExternalInput").ap()
    d_lt = nc.dram_tensor("ltstrict", [128, 128], F32, kind="ExternalInput").ap()
    d_on = nc.dram_tensor("ones2d", [128, 128], F32, kind="ExternalInput").ap()
    d_xs = nc.dram_tensor("xs_scr", [6144, H], BF16, kind="Internal").ap()
    d_ys = nc.dram_tensor("ys_scr", [6144, H], BF16, kind="Internal").ap()
    d_xnd = nc.dram_tensor("xnd_scr", [H, T], BF16, kind="Internal").ap()

    dbg = _CFG.get("dbg", False)
    if dbg:
        d_dbg_xn = nc.dram_tensor("dbg_xn", [H, T], BF16, kind="ExternalOutput").ap()
        d_dbg_lg = nc.dram_tensor("dbg_lg", [8, T], F32, kind="ExternalOutput").ap()
        d_dbg_oh = nc.dram_tensor("dbg_oh", [8, T], BF16, kind="ExternalOutput").ap()
        d_dbg_h1 = nc.dram_tensor("dbg_h1", [128, 8192], BF16, kind="ExternalOutput").ap()
        d_dbg_mk = nc.dram_tensor("dbg_mk", [128, T], BF16, kind="ExternalOutput").ap()
        d_dbg_xe = nc.dram_tensor("dbg_xe", [E, H, T], F32, kind="ExternalOutput").ap()
        d_dbg_v = nc.dram_tensor("dbg_v", [128, 16384], BF16, kind="ExternalOutput").ap()
        d_dbg_kv = nc.dram_tensor("dbg_kv", [128, 520], BF16, kind="ExternalOutput").ap()
        d_dbg_ar = nc.dram_tensor("dbg_ar", [128, 16384], BF16, kind="ExternalOutput").ap()
        d_dbg_nr = nc.dram_tensor("dbg_nr", [16, T], BF16, kind="ExternalOutput").ap()
    es = contextlib.ExitStack()
    with es:
        sbt = lambda name, shape, dt: es.enter_context(nc.sbuf_tensor(name, shape, dt))
        pst = lambda name, shape: es.enter_context(nc.psum_tensor(name, shape, F32))

        x = [sbt(f"x{i}", [128, T], F32) for i in range(NHC)]
        xn = [sbt(f"xn{i}", [128, T], BF16) for i in range(NHC)]
        w1s = sbt("w1s", [128, 8192], BF16)
        w2s = sbt("w2s", [128, 8192], BF16)
        h1m = sbt("h1m", [128, 8192], BF16)
        vat = sbt("vat", [128, 16384], BF16)
        ar32 = sbt("ar32", [128, 3072], F32)
        ident = sbt("sb_ident", [128, 128], F32)
        iota8 = sbt("sb_iota8", [128, 8], F32)
        negsel = sbt("sb_negsel", [2, 128], BF16)
        hsel = sbt("sb_hsel", [16, H], BF16)
        esel = sbt("sb_esel", [8, H], BF16)
        rf_f = sbt("rf_f", [128, HD], F32)
        rf_s = sbt("rf_s", [128, HD], BF16)
        b1c = sbt("sb_b1c", [128, 64], F32)
        b2bf = sbt("b2bf", [8, H], BF16)
        wg = sbt("wg", [128, 64], BF16)
        bgc = sbt("sb_bgc", [8, 1], F32)
        boc = sbt("sb_boc", [128, 8], F32)
        ones_col = sbt("ones_col", [128, 1], BF16)
        ones_bc = sbt("ones_bc", [1, 128], BF16)
        bdiag = sbt("bdiag", [128, 2], BF16)
        eps_col = sbt("eps_col", [1, 1], F32)
        kvext = sbt("kvext", [128, 520], BF16)
        mx8 = sbt("mx8", [128, 8], F32)
        idxu = sbt("idxu", [128, 8], U32)
        idxf = sbt("idxf", [128, 1], F32)
        ohf = sbt("ohf", [128, 128], F32)
        ohstage = sbt("ohstage", [128, 8], F32)
        ltst = sbt("sb_ltst", [128, 128], BF16)
        ones2d = sbt("sb_ones2d", [128, 128], BF16)
        idxs = sbt("idxs", [128, 16], F32)
        slotu = sbt("slotu", [128, 16], U32)
        ohb = sbt("ohb", [128, 128], BF16)
        rank_c = sbt("rank_c", [128, 1], F32)
        slot_f = sbt("slot_f", [128, 1], F32)
        ptmp = sbt("ptmp", [128, 8], F32)
        zt = sbt("zt", [128, 512], BF16)
        nrm16 = sbt("nrm16", [16, T], BF16)
        tmp_r = [sbt("tmp_r0", [1, 1024], BF16), sbt("tmp_r1", [1, 1024], BF16)]

        h1t = lambda ic: h1m[:, ic * 1024:(ic + 1) * 1024]
        xb_sl = [h1m[:, 0:2048], h1m[:, 2048:4096]]
        xsq_sl = [h1m[:, 4096:6144], h1m[:, 6144:8192]]
        kq_sb = h1m[:, 0:2048]
        sq_sb = h1m[:, 2048:4096]
        kp_sb = h1m[:, 4096:5120]
        nr_bf = h1m[0:2, 5120:7168]
        eq_sb = h1m[:, 2048:4096]
        mask_sl = [vat[:, 0:2048], vat[:, 2048:4096]]
        onehotT = vat[0:8, 4096:6144]
        m_bf = vat[0:1, 6144:7168]
        rstd_bf = vat[0:1, 7168:8192]
        vtile = lambda tt: vat[:, tt * 1024:(tt + 1) * 1024]
        araw = lambda oc: vat[:, oc * 2048:(oc + 1) * 2048]
        h1s = lambda ic: h1m[:, ic * 512:(ic + 1) * 512]
        xnt_b = [h1m[:, 4096 + 1024 * i:4096 + 1024 * (i + 1)] for i in range(2)]
        yt_b = [h1m[:, 6144 + 1024 * i:6144 + 1024 * (i + 1)] for i in range(2)]
        xgt = lambda sl, hc: vat[:, sl * 4096 + hc * 512:sl * 4096 + (hc + 1) * 512]
        p_sb = ar32[0:8, 0:2048]
        ytf_b = [ar32[:, 0:1024], ar32[:, 1024:2048]]
        t1_sl = ar32[:, 0:1024]
        m_row = ar32[0:1, 1024:2048]
        pair_f = ar32[0:2, 1024:2048]
        msq_row = ar32[0:1, 2048:3072]
        var_row = ar32[0:1, 0:1024]
        rstd_f = ar32[0:1, 2048:3072]
        lg_sb = ar32[0:8, 0:2048]

        PG = [pst(f"PG{i}", [128, 1024]) for i in range(4)]

        dl = Sem(es, nc, "dl")
        w1m = Sem(es, nc, "w1m")
        w2m = Sem(es, nc, "w2m")
        im = Sem(es, nc, "im")
        xl = Sem(es, nc, "xl")
        pe = Sem(es, nc, "pe")
        act = Sem(es, nc, "act")
        dve = Sem(es, nc, "dve")
        sp = Sem(es, nc, "sp")
        st = Sem(es, nc, "st")
        xd = Sem(es, nc, "xd")
        xtl = [Sem(es, nc, "xtl0"), Sem(es, nc, "xtl1")]
        sct = [Sem(es, nc, "sct0"), Sem(es, nc, "sct1")]
        xgl = [Sem(es, nc, "xgl0"), Sem(es, nc, "xgl1")]
        ysd = [Sem(es, nc, "ysd0"), Sem(es, nc, "ysd1")]
        ygt = [Sem(es, nc, "ygt0"), Sem(es, nc, "ygt1")]
        zs = Sem(es, nc, "zs")

        block = es.enter_context(nc.Block())

        @block.gpsimd
        def _(g):
            for dst, src in [(ident[:], d_ident), (iota8[:], d_iota8),
                             (negsel[:], d_negsel), (hsel[:], d_hsel),
                             (esel[:], d_esel), (rf_f[:], d_rf),
                             (ltst[:], d_lt), (ones2d[:], d_on),
                             (b1c[:], d_b1c), (b2bf[:], d_b2),
                             (wg[:], d_wgc), (bgc[:], d_bgc), (boc[:], d_boc)]:
                dl.inc(g.dma_start(dst, src), 16)
            im.inc(g.memset(ones_col[:], 1.0))
            im.inc(g.memset(ones_bc[:], 1.0))
            im.inc(g.memset(bdiag[:], 0.0))
            g.wait_ge(im.h, im.n)
            im.inc(g.memset(bdiag[0:64, 0:1], 1.0))
            g.wait_ge(im.h, im.n)
            im.inc(g.memset(bdiag[64:128, 1:2], 1.0))
            im.inc(g.memset(eps_col[:], 1e-5))
            im.inc(g.memset(zt[:], 0.0))
        init_dl = dl.n
        init_im = im.n

        @block.scalar
        def _(s):
            s.wait_ge(dl.h, init_dl)
            s.wait_ge(im.h, init_im)
            act.inc(s.activation(rf_s[:], rf_f[:], AF.Copy, scale=0.125))

        def vstep(v, instr):
            dve.inc(instr)
            v.wait_ge(dve.h, dve.n)

        def w_src(d_w):
            return d_w.rearrange("(c p) i -> p c i", p=128)

        def w_dst(slot):
            return slot[:].rearrange("p (c i) -> p c i", c=8)

        def emit_ln(x_ready):
            pe_base = pe.n
            pe_mark = lambda hc: pe_base + hc + 1
            act_base = act.n
            dve_base = dve.n

            @block.scalar
            def _(s):
                for hc in range(NHC):
                    sem, val = x_ready[hc]
                    s.wait_ge(sem, val)
                    if hc == 0:
                        s.wait_ge(x_ready[NHC - 1][0], x_ready[NHC - 1][1])
                    if hc >= 2:
                        s.wait_ge(pe.h, pe_mark(hc - 2))
                    act.inc(s.activation(xb_sl[hc % 2], x[hc][:], AF.Copy))
            assert act.n == act_base + NHC

            @block.vector
            def _(v):
                for hc in range(NHC):
                    sem, val = x_ready[hc]
                    v.wait_ge(sem, val)
                    if hc == 0:
                        v.wait_ge(x_ready[NHC - 1][0], x_ready[NHC - 1][1])
                    if hc >= 2:
                        v.wait_ge(pe.h, pe_mark(hc - 2))
                    dve.inc(v.tensor_mul(xsq_sl[hc % 2], x[hc][:], x[hc][:]))
            assert dve.n == dve_base + NHC

            @block.tensor
            def _(t):
                t.wait_ge(im.h, init_im)
                for hc in range(NHC):
                    t.wait_ge(act.h, act_base + hc + 1)
                    t.wait_ge(dve.h, dve_base + hc + 1)
                    first = hc == 0
                    last = hc == NHC - 1
                    sl = hc % 2
                    for half in range(2):
                        pgs, pgq = (PG[0], PG[1]) if half == 0 else (PG[2], PG[3])
                        for c in range(2):
                            co = half * 1024 + c * 512
                            t.matmul(pgs[0:1, c * 512:(c + 1) * 512],
                                     ones_col[:], xb_sl[sl][:, co:co + 512],
                                     start=first, stop=last)
                            i2 = t.matmul(pgq[0:1, c * 512:(c + 1) * 512],
                                          ones_col[:], xsq_sl[sl][:, co:co + 512],
                                          start=first, stop=last)
                    pe.inc(i2)
            sums_pe = pe.n
            assert sums_pe == pe_base + NHC

            xn_marks = [[0] * NHC, [0] * NHC]
            for half in range(2):
                pgs, pgq = (PG[0], PG[1]) if half == 0 else (PG[2], PG[3])

                @block.vector
                def _(v, pgs=pgs, pgq=pgq):
                    v.wait_ge(pe.h, sums_pe)
                    vstep(v, v.tensor_scalar_mul(m_row, pgs[0:1, :], 1.0 / H))
                    vstep(v, v.tensor_scalar_mul(msq_row, pgq[0:1, :], 1.0 / H))
                    vstep(v, v.tensor_mul(var_row, m_row, m_row))
                    dve.inc(v.tensor_sub(var_row, msq_row, var_row))
                v_rows = dve.n

                @block.scalar
                def _(s):
                    s.wait_ge(dve.h, v_rows)
                    act.inc(s.activation(rstd_f, var_row, AF.Ln,
                                         bias=eps_col[0:1, :]))
                    s.wait_ge(act.h, act.n)
                    act.inc(s.activation(rstd_f, rstd_f, AF.Exp, scale=-0.5))
                    s.wait_ge(act.h, act.n)
                    s.activation(m_bf[0:1, 0:1024], m_row, AF.Copy)
                    act.inc(s.activation(rstd_bf[0:1, 0:1024], rstd_f, AF.Copy))
                s_rows = act.n

                @block.tensor
                def _(t, pgs=pgs, pgq=pgq):
                    t.wait_ge(act.h, s_rows)
                    for c in range(2):
                        t.matmul(pgs[:, c * 512:(c + 1) * 512], ones_bc[:],
                                 m_bf[0:1, c * 512:(c + 1) * 512],
                                 start=True, stop=True)
                        i = t.matmul(pgq[:, c * 512:(c + 1) * 512], ones_bc[:],
                                     rstd_bf[0:1, c * 512:(c + 1) * 512],
                                     start=True, stop=True)
                    pe.inc(i)
                bc_pe = pe.n

                @block.vector
                def _(v, half=half, pgs=pgs, pgq=pgq):
                    v.wait_ge(pe.h, bc_pe)
                    co = half * 1024
                    for hc in range(NHC):
                        vstep(v, v.scalar_tensor_tensor(
                            t1_sl, x[hc][:, co:co + 1024], 1.0, pgs[:],
                            ALU.mult, ALU.subtract))
                        i = v.tensor_mul(xn[hc][:, co:co + 1024], t1_sl, pgq[:])
                        if hc < NHC - 1:
                            vstep(v, i)
                        else:
                            dve.inc(i)
                        xn_marks[half][hc] = dve.n
            return xn_marks

        def emit_gate(xn_marks):
            @block.tensor
            def _(t):
                t.wait_ge(dl.h, init_dl)
                t.wait_ge(im.h, init_im)
                for half in range(2):
                    pg = PG[half]
                    for hc in range(NHC):
                        t.wait_ge(dve.h, xn_marks[half][hc])
                        for c in range(2):
                            co = half * 1024 + c * 512
                            i = t.matmul(pg[0:8, c * 512:(c + 1) * 512],
                                         wg[:, hc * 8:hc * 8 + 8],
                                         xn[hc][:, co:co + 512],
                                         start=(hc == 0), stop=(hc == NHC - 1))
                pe.inc(i)
            g_pe = pe.n

            @block.scalar
            def _(s):
                s.wait_ge(pe.h, g_pe)
                for half in range(2):
                    i = s.activation(lg_sb[:, half * 1024:(half + 1) * 1024],
                                     PG[half][0:8, :], AF.Identity, bias=bgc[:])
                act.inc(i)
            lg_act = act.n

            @block.tensor
            def _(t):
                t.wait_ge(act.h, lg_act)
                for tt in range(16):
                    i = t.transpose(PG[2][:, tt * 8:tt * 8 + 8],
                                    lg_sb[:, tt * 128:(tt + 1) * 128],
                                    ident[0:8, 0:8])
                pe.inc(i)
            tr_pe = pe.n

            @block.vector
            def _(v):
                v.wait_ge(pe.h, tr_pe)
                for tt in range(16):
                    vstep(v, v.tensor_copy(ohstage[:], PG[2][:, tt * 8:tt * 8 + 8]))
                    vstep(v, v.max(mx8[:], ohstage[:]))
                    vstep(v, v.max_index(idxu[:], mx8[:], ohstage[:]))
                    vstep(v, v.tensor_copy(idxf[:], idxu[:, 0:1]))
                    vstep(v, v.tensor_copy(idxs[:, tt:tt + 1], idxf[:]))
                    vstep(v, v.tensor_scalar(ohf[:, tt * 8:tt * 8 + 8], iota8[:],
                                             idxf[:], None, ALU.is_equal))
                    i = v.tensor_copy(ohb[:, tt * 8:tt * 8 + 8],
                                      ohf[:, tt * 8:tt * 8 + 8])
                    if tt < 15:
                        vstep(v, i)
                dve.inc(i)
            oh_dve = dve.n

            pe_base = pe.n
            act_base = act.n

            @block.tensor
            def _(t):
                t.wait_ge(dve.h, oh_dve)
                for tt in range(16):
                    if tt >= 2:
                        t.wait_ge(act.h, act_base + tt - 1)
                    i = t.transpose(PG[3][0:8, (tt % 2) * 512:(tt % 2) * 512 + 128],
                                    ohf[:, tt * 8:tt * 8 + 8], ident[:])
                    pe.inc(i)

            @block.scalar
            def _(s):
                for tt in range(16):
                    s.wait_ge(pe.h, pe_base + tt + 1)
                    i = s.copy(onehotT[:, tt * 128:(tt + 1) * 128],
                               PG[3][0:8, (tt % 2) * 512:(tt % 2) * 512 + 128])
                    act.inc(i)
            return {"oh_act": act.n, "oh_dve": oh_dve, "lg_act": lg_act}

        def emit_mask(e_idx, slot, oh_act, war_act):
            @block.tensor
            def _(t):
                t.wait_ge(act.h, max(oh_act, war_act))
                for half in range(2):
                    for c in range(2):
                        co = half * 1024 + c * 512
                        i = t.matmul(PG[half][:, c * 512:(c + 1) * 512],
                                     esel[:, e_idx * 128:(e_idx + 1) * 128],
                                     onehotT[:, co:co + 512],
                                     start=True, stop=True)
                pe.inc(i)
            mpe = pe.n

            @block.scalar
            def _(s):
                s.wait_ge(pe.h, mpe)
                for half in range(2):
                    i = s.copy(mask_sl[slot][:, half * 1024:(half + 1) * 1024],
                               PG[half][:])
                act.inc(i)
            return act.n

        def emit_ffn(x_ready, w_marks, rearm):
            xn_marks = emit_ln(x_ready)
            oh_act = emit_gate(xn_marks)["oh_act"]
            mask_acts = {0: emit_mask(0, 0, oh_act, war_act=oh_act)}
            last_ps1_act = mask_acts[0]
            prev_gelu = [0, 0]
            prev_acc = [0, 0]
            for e in range(E):
                abase = act.n
                g_mark = lambda tb, ic: abase + tb * 8 + ic + 1
                has_mask = e + 1 < E
                dbase = dve.n
                mk_mark = lambda tb, ic: dbase + tb * 16 + ic + 1
                ac_mark = lambda tb, hc: dbase + tb * 16 + 8 + hc + 1

                mm1_marks = [[0] * 8, [0] * 8]
                mm2_marks = [[0] * 8, [0] * 8]
                mask_mpe = 0
                for tb in range(2):
                    for ic in range(8):
                        pg = PG[ic % 2]

                        @block.tensor
                        def _(t, e=e, tb=tb, ic=ic, pg=pg,
                              lpa=last_ps1_act, pga=prev_gelu[ic % 2]):
                            if tb == 0 and ic == 0:
                                t.wait_ge(*w_marks[("w1", e)])
                            if tb == 0 and ic < 2:
                                t.wait_ge(act.h, max(lpa, pga))
                            else:
                                t.wait_ge(act.h, g_mark(tb, ic - 2) if ic >= 2
                                          else g_mark(tb - 1, ic + 6))
                            for hc in range(NHC):
                                if e == 0:
                                    t.wait_ge(dve.h, xn_marks[tb][hc])
                                for c in range(2):
                                    i = t.matmul(
                                        pg[:, c * 512:(c + 1) * 512],
                                        w1s[:, hc * 1024 + ic * 128:
                                            hc * 1024 + (ic + 1) * 128],
                                        xn[hc][:, tb * 1024 + c * 512:
                                               tb * 1024 + (c + 1) * 512],
                                        start=(hc == 0), stop=(hc == NHC - 1))
                            pe.inc(i)
                        mm1_marks[tb][ic] = pe.n
                    if tb == 1:
                        w1_free_pe = pe.n

                    if tb == 1 and has_mask:
                        @block.tensor
                        def _(t, e=e, oa=oh_act):
                            t.wait_ge(act.h, max(oa, g_mark(1, 7)))
                            for half in range(2):
                                for c in range(2):
                                    co = half * 1024 + c * 512
                                    i = t.matmul(
                                        PG[half][:, c * 512:(c + 1) * 512],
                                        esel[:, (e + 1) * 128:(e + 2) * 128],
                                        onehotT[:, co:co + 512],
                                        start=True, stop=True)
                            pe.inc(i)
                        mask_mpe = pe.n

                    for hc in range(NHC):
                        pg2 = PG[2 + hc % 2]

                        @block.tensor
                        def _(t, e=e, tb=tb, hc=hc, pg2=pg2,
                              pac=prev_acc[hc % 2]):
                            if tb == 0 and hc == 0:
                                t.wait_ge(*w_marks[("w2", e)])
                            t.wait_ge(dve.h, mk_mark(tb, 7))
                            if hc >= 2:
                                t.wait_ge(dve.h, ac_mark(tb, hc - 2))
                            elif tb == 1:
                                t.wait_ge(dve.h, ac_mark(0, hc + 6))
                            else:
                                t.wait_ge(dve.h, pac)
                            first = True
                            if e == 0:
                                for c in range(2):
                                    t.matmul(pg2[:, c * 512:(c + 1) * 512],
                                             b2bf[:, hc * 128:(hc + 1) * 128],
                                             onehotT[:, tb * 1024 + c * 512:
                                                     tb * 1024 + (c + 1) * 512],
                                             start=True, stop=False)
                                first = False
                            for ic in range(8):
                                for c in range(2):
                                    i = t.matmul(
                                        pg2[:, c * 512:(c + 1) * 512],
                                        w2s[:, ic * 1024 + hc * 128:
                                            ic * 1024 + (hc + 1) * 128],
                                        h1t(ic)[:, c * 512:(c + 1) * 512],
                                        start=(first and ic == 0),
                                        stop=(ic == 7))
                            pe.inc(i)
                        mm2_marks[tb][hc] = pe.n

                @block.scalar
                def _(s, e=e, m1=mm1_marks, mpe=mask_mpe, hm=has_mask):
                    for tb in range(2):
                        for ic in range(8):
                            s.wait_ge(pe.h, m1[tb][ic])
                            act.inc(s.activation(
                                h1t(ic), PG[ic % 2][:], AF.Gelu,
                                bias=b1c[:, e * 8 + ic:e * 8 + ic + 1]))
                    if hm:
                        s.wait_ge(pe.h, mpe)
                        for half in range(2):
                            act.inc(s.copy(
                                mask_sl[(e + 1) % 2][:, half * 1024:
                                                     (half + 1) * 1024],
                                PG[half][:]))
                assert act.n == abase + 16 + (2 if has_mask else 0)
                if has_mask:
                    mask_acts[e + 1] = act.n
                    last_ps1_act = act.n
                prev_gelu = [g_mark(1, 6), g_mark(1, 7)]

                @block.vector
                def _(v, e=e, ma=mask_acts[e], m2=mm2_marks):
                    for tb in range(2):
                        if tb == 0:
                            v.wait_ge(act.h, ma)
                        for ic in range(8):
                            v.wait_ge(act.h, g_mark(tb, ic))
                            dve.inc(v.tensor_mul(
                                h1t(ic), h1t(ic),
                                mask_sl[e % 2][:, tb * 1024:(tb + 1) * 1024]))
                        for hc in range(NHC):
                            v.wait_ge(pe.h, m2[tb][hc])
                            dve.inc(v.scalar_tensor_tensor(
                                x[hc][:, tb * 1024:(tb + 1) * 1024],
                                PG[2 + hc % 2][:], 0.5,
                                x[hc][:, tb * 1024:(tb + 1) * 1024],
                                ALU.mult, ALU.add))
                assert dve.n == dbase + 32
                prev_acc = [ac_mark(1, 6), ac_mark(1, 7)]
                if e == E - 1:
                    final_acc = [ac_mark(1, hc) for hc in range(NHC)]

                if e + 1 < E:
                    @block.gpsimd
                    def _(g, e=e, wf=w1_free_pe, wf2=pe.n):
                        g.wait_ge(pe.h, wf)
                        w_marks[("w1", e + 1)] = (w1m.h, w1m.inc(
                            g.dma_start(w_dst(w1s), w_src(d_w1[e + 1])), 16))
                        g.wait_ge(pe.h, wf2)
                        w_marks[("w2", e + 1)] = (w2m.h, w2m.inc(
                            g.dma_start(w_dst(w2s), w_src(d_w2[e + 1])), 16))
            if rearm is not None:
                d1, d2 = rearm

                @block.gpsimd
                def _(g, wf=pe.n):
                    g.wait_ge(pe.h, wf)
                    w_marks[("w1", 0)] = (w1m.h, w1m.inc(
                        g.dma_start(w_dst(w1s), w_src(d1)), 16))
                    w_marks[("w2", 0)] = (w2m.h, w2m.inc(
                        g.dma_start(w_dst(w2s), w_src(d2)), 16))
            return [(dve.h, m) for m in final_acc]

        def emit_attn(x_ready, w_marks):
            astage = _CFG.get("astage", 4)
            xn_marks = emit_ln(x_ready)
            xn_all = xn_marks[1][NHC - 1]

            v_acts = []
            for tt in range(16):
                pg = PG[tt % 2]

                @block.tensor
                def _(t, tt=tt, pg=pg):
                    if tt == 0:
                        t.wait_ge(*w_marks["wv"])
                    if tt >= 2:
                        t.wait_ge(act.h, v_acts[tt - 2])
                    half = tt // 8
                    for hc in range(NHC):
                        if tt % 8 == 0:
                            t.wait_ge(dve.h, xn_marks[half][hc])
                        for c in range(2):
                            i = t.matmul(
                                pg[:, c * 512:(c + 1) * 512],
                                xn[hc][:, tt * 128:(tt + 1) * 128],
                                w1s[:, hc * 1024 + c * 512:
                                    hc * 1024 + (c + 1) * 512],
                                start=(hc == 0), stop=(hc == NHC - 1))
                    pe.inc(i)
                v_pe = pe.n

                @block.scalar
                def _(s, tt=tt, pg=pg, vp=v_pe):
                    s.wait_ge(pe.h, vp)
                    act.inc(s.copy(vtile(tt), pg[:]))
                v_acts.append(act.n)
            v_done_pe = pe.n
            if dbg:
                @block.sync
                def _(sy, va=v_acts[-1]):
                    sy.wait_ge(act.h, va)
                    st.inc(sy.dma_start(d_dbg_v, vat[:]), 16)
            if astage <= 1:
                return [(dve.h, xn_all)] * NHC

            @block.gpsimd
            def _(g, wf=v_done_pe):
                g.wait_ge(pe.h, wf)
                w_marks["wq"] = (w1m.h, w1m.inc(
                    g.dma_start(w_dst(w1s), w_src(d_wq)), 16))

            qk_marks = {}
            evac_marks = {}
            kvext_marks = {}
            proj_marks = {-1: 0}

            def emit_qk(oc, wslot, wmark_key, pg01_free):
                @block.tensor
                def _(t, oc=oc, pg01_free=pg01_free):
                    if oc == 0:
                        t.wait_ge(*w_marks[wmark_key])
                    t.wait_ge(act.h, pg01_free)
                    for hc in range(NHC):
                        if oc == 0:
                            t.wait_ge(dve.h, xn_marks[1][hc])
                        for half in range(2):
                            for c in range(2):
                                i = t.matmul(
                                    PG[half][:, c * 512:(c + 1) * 512],
                                    wslot[:, hc * 1024 + oc * 128:
                                          hc * 1024 + (oc + 1) * 128],
                                    xn[hc][:, half * 1024 + c * 512:
                                           half * 1024 + (c + 1) * 512],
                                    start=(hc == 0), stop=(hc == NHC - 1))
                    pe.inc(i)
                qk_marks[oc] = pe.n

            emit_qk(0, w2s, "wk", v_acts[-1])
            prev_tail = v_acts[-1]
            for oc in range(8):
                abase = act.n
                evac_mark = abase + 4
                rows_mark = abase + 8
                kp_mark = [abase + 9, abase + 12]
                kvx_mark = [abase + 11, abase + 14]

                @block.tensor
                def _(t, ea=evac_mark, kv_free=prev_tail):
                    t.wait_ge(act.h, ea)
                    if kv_free:
                        t.wait_ge(act.h, kv_free)
                    for half in range(2):
                        pgn = PG[2] if half == 0 else PG[3]
                        for c in range(2):
                            i = t.matmul(pgn[0:2, c * 512:(c + 1) * 512],
                                         bdiag[:],
                                         sq_sb[:, half * 1024 + c * 512:
                                               half * 1024 + (c + 1) * 512],
                                         start=True, stop=True)
                    pe.inc(i)
                nrm_pe = pe.n

                if oc + 1 < 8:
                    emit_qk(oc + 1, w2s, "wk", evac_mark)
                elif astage > 2:
                    @block.gpsimd
                    def _(g, wf=qk_marks[7]):
                        g.wait_ge(pe.h, wf)
                        w_marks["wo"] = (w2m.h, w2m.inc(
                            g.dma_start(w_dst(w2s), w_src(d_wo)), 16))

                proj_pe = [0, 0]
                kv_pe = [0, 0]
                for hh in range(2):
                    @block.tensor
                    def _(t, hh=hh, na=rows_mark,
                          pea=(kvx_mark[0] if hh == 1 else 0)):
                        t.wait_ge(act.h, na)
                        if pea:
                            t.wait_ge(act.h, pea)
                        for tt in range(16):
                            v_ = PG[2][:, tt * 64:(tt + 1) * 64]
                            t.matmul(v_,
                                     kq_sb[hh * 64:(hh + 1) * 64,
                                           tt * 128:(tt + 1) * 128],
                                     rf_s[hh * 64:(hh + 1) * 64, :],
                                     start=True, stop=False)
                            i = t.matmul(v_,
                                         nr_bf[0:2, tt * 128:(tt + 1) * 128],
                                         negsel[0:2, hh * 64:(hh + 1) * 64],
                                         start=False, stop=True)
                        pe.inc(i)
                    proj_pe[hh] = pe.n
                    if hh == 1:
                        proj_marks[oc] = pe.n

                    @block.tensor
                    def _(t, oc=oc, hh=hh, ka=kp_mark[hh]):
                        t.wait_ge(act.h, ka)
                        h_ = 2 * oc + hh
                        base = hh * 512
                        for tt in range(16):
                            t.matmul(PG[3][0:64, base:base + 64],
                                     kp_sb[:, tt * 64:(tt + 1) * 64],
                                     vtile(tt)[:, h_ * 64:h_ * 64 + 64],
                                     start=(tt == 0), stop=(tt == 15))
                            i = t.matmul(PG[2][0:64, 960 + hh:960 + hh + 1],
                                         kp_sb[:, tt * 64:(tt + 1) * 64],
                                         ones_col[:],
                                         start=(tt == 0), stop=(tt == 15))
                        pe.inc(i)
                    kv_pe[hh] = pe.n

                @block.scalar
                def _(s, oc=oc, qp=qk_marks[oc], pp=proj_marks[oc - 1],
                      nrm=nrm_pe, ppe=tuple(proj_pe), kpe=tuple(kv_pe)):
                    s.wait_ge(pe.h, max(qp, pp))
                    act.inc(s.copy(kq_sb[:, 0:1024], PG[0][:]))
                    act.inc(s.copy(kq_sb[:, 1024:2048], PG[1][:]))
                    act.inc(s.activation(sq_sb[:, 0:1024], PG[0][:], AF.Square))
                    act.inc(s.activation(sq_sb[:, 1024:2048], PG[1][:],
                                         AF.Square))
                    s.wait_ge(pe.h, nrm)
                    for half in range(2):
                        pgn = PG[2] if half == 0 else PG[3]
                        act.inc(s.activation(pair_f, pgn[0:2, :], AF.Ln))
                        s.wait_ge(act.h, act.n)
                        act.inc(s.activation(
                            nr_bf[0:2, half * 1024:(half + 1) * 1024],
                            pair_f, AF.Exp, scale=0.5))
                        if half == 0:
                            s.wait_ge(act.h, act.n)
                    for hh in range(2):
                        s.wait_ge(pe.h, ppe[hh])
                        act.inc(s.activation(kp_sb, PG[2][:], AF.Exp))
                        s.wait_ge(pe.h, kpe[hh])
                        base = hh * 512
                        act.inc(s.copy(kvext[hh * 64:(hh + 1) * 64,
                                             oc * 65:oc * 65 + 64],
                                       PG[3][0:64, base:base + 64]))
                        act.inc(s.copy(kvext[hh * 64:(hh + 1) * 64,
                                             oc * 65 + 64:oc * 65 + 65],
                                       PG[2][0:64, 960 + hh:960 + hh + 1]))
                assert act.n == abase + 14
                prev_tail = kvx_mark[1]
                kvext_marks[oc] = kvx_mark[1]
            kvext_marks[7] = prev_tail

            if dbg:
                @block.sync
                def _(sy, ka=kvext_marks[7]):
                    sy.wait_ge(act.h, ka)
                    st.inc(sy.dma_start(d_dbg_kv, kvext[:]), 16)
            if astage <= 2:
                return [(dve.h, xn_all)] * NHC

            out_free = proj_marks[7]
            ar_prev = kvext_marks[7]
            sp_prev = 0
            qk_marks = {}
            emit_qk(0, w1s, "wq", evac_mark)
            for oc in range(8):
                abase = act.n
                evac_mark = abase + 4
                rows_mark = abase + 8
                eq_mark = abase + 10
                ar_mark = [abase + 14, abase + 18]

                @block.tensor
                def _(t, ea=evac_mark, pg23_free=ar_prev):
                    t.wait_ge(act.h, ea)
                    t.wait_ge(act.h, pg23_free)
                    for half in range(2):
                        pgn = PG[2] if half == 0 else PG[3]
                        for c in range(2):
                            i = t.matmul(pgn[0:2, c * 512:(c + 1) * 512],
                                         bdiag[:],
                                         sq_sb[:, half * 1024 + c * 512:
                                               half * 1024 + (c + 1) * 512],
                                         start=True, stop=True)
                    pe.inc(i)
                nrm_pe = pe.n

                if oc + 1 < 8:
                    emit_qk(oc + 1, w1s, "wq", evac_mark)

                @block.tensor
                def _(t, na=rows_mark):
                    t.wait_ge(act.h, na)
                    for hh in range(2):
                        for half in range(2):
                            pgn = PG[2] if half == 0 else PG[3]
                            for c in range(2):
                                v_ = pgn[hh * 64:(hh + 1) * 64,
                                         c * 512:(c + 1) * 512]
                                t.matmul(v_, rf_s[hh * 64:(hh + 1) * 64, :],
                                         kq_sb[hh * 64:(hh + 1) * 64,
                                               half * 1024 + c * 512:
                                               half * 1024 + (c + 1) * 512],
                                         start=True, stop=False)
                                i = t.matmul(
                                    v_, negsel[0:2, hh * 64:(hh + 1) * 64],
                                    nr_bf[0:2, half * 1024 + c * 512:
                                          half * 1024 + (c + 1) * 512],
                                    start=False, stop=True)
                    pe.inc(i)
                proj_pe = pe.n

                out_pe = [0, 0]
                for hh in range(2):
                    @block.tensor
                    def _(t, oc=oc, hh=hh,
                          ea=(eq_mark if hh == 0 else ar_mark[0])):
                        t.wait_ge(act.h, ea)
                        for half in range(2):
                            pgo = PG[2] if half == 0 else PG[3]
                            for c in range(2):
                                i = t.matmul(
                                    pgo[0:65, c * 512:(c + 1) * 512],
                                    kvext[hh * 64:(hh + 1) * 64,
                                          oc * 65:(oc + 1) * 65],
                                    eq_sb[hh * 64:(hh + 1) * 64,
                                          half * 1024 + c * 512:
                                          half * 1024 + (c + 1) * 512],
                                    start=True, stop=True)
                        pe.inc(i)
                    out_pe[hh] = pe.n

                @block.scalar
                def _(s, oc=oc, qp=qk_marks[oc], pp=out_free, nrm=nrm_pe,
                      prj=proj_pe, ope=tuple(out_pe), spm=sp_prev, spb=sp.n):
                    s.wait_ge(pe.h, max(qp, pp))
                    act.inc(s.copy(kq_sb[:, 0:1024], PG[0][:]))
                    act.inc(s.copy(kq_sb[:, 1024:2048], PG[1][:]))
                    act.inc(s.activation(sq_sb[:, 0:1024], PG[0][:], AF.Square))
                    act.inc(s.activation(sq_sb[:, 1024:2048], PG[1][:],
                                         AF.Square))
                    s.wait_ge(pe.h, nrm)
                    for half in range(2):
                        pgn = PG[2] if half == 0 else PG[3]
                        act.inc(s.activation(pair_f, pgn[0:2, :], AF.Ln))
                        s.wait_ge(act.h, act.n)
                        act.inc(s.activation(
                            nr_bf[0:2, half * 1024:(half + 1) * 1024],
                            pair_f, AF.Exp, scale=0.5))
                        if half == 0:
                            s.wait_ge(act.h, act.n)
                    s.wait_ge(pe.h, prj)
                    act.inc(s.activation(eq_sb[:, 0:1024], PG[2][:], AF.Exp))
                    act.inc(s.activation(eq_sb[:, 1024:2048], PG[3][:],
                                         AF.Exp))
                    for hh in range(2):
                        s.wait_ge(pe.h, ope[hh])
                        s.wait_ge(sp.h, spm if hh == 0 else spb + 32)
                        for half in range(2):
                            pgo = PG[2] if half == 0 else PG[3]
                            act.inc(s.copy(
                                araw(oc)[hh * 64:(hh + 1) * 64,
                                         half * 1024:(half + 1) * 1024],
                                pgo[0:64, :]))
                            act.inc(s.copy(tmp_r[half][0:1, :],
                                           pgo[64:65, :]))
                assert act.n == abase + 18
                out_free = out_pe[1]
                ar_prev = ar_mark[1]

                for hh in range(2):
                    @block.sync
                    def _(sy, oc=oc, hh=hh, ea=ar_mark[hh]):
                        sy.wait_ge(act.h, ea)
                        row = 2 * oc + hh
                        sp.inc(sy.dma_start(
                            nrm16[row:row + 1, 0:1024], tmp_r[0][:]), 16)
                        sp.inc(sy.dma_start(
                            nrm16[row:row + 1, 1024:2048], tmp_r[1][:]), 16)
                sp_prev = sp.n
            prev_q_evac = ar_prev
            q_evac = {7: act.n}

            if dbg:
                @block.sync
                def _(sy, qa=prev_q_evac, spn=sp.n):
                    sy.wait_ge(act.h, qa)
                    sy.wait_ge(sp.h, spn)
                    st.inc(sy.dma_start(d_dbg_ar, vat[:]), 16)
                    st.inc(sy.dma_start(d_dbg_nr, nrm16[:]), 16)
            if astage <= 3:
                return [(dve.h, xn_all)] * NHC

            @block.vector
            def _(v, spn=sp.n):
                v.wait_ge(sp.h, spn)
                vstep(v, v.tensor_scalar_add(nrm16[:], nrm16[:], 1e-6))
                with nc.allow_low_precision(
                        reason="1/(nrm+1e-6): fp32 DVE internals, bf16 store "
                               "at matmul-operand precision"):
                    dve.inc(v.reciprocal(nrm16[:], nrm16[:]))
            al_dve = dve.n

            sc_dves = []
            for oc in range(8):
                pg_sc = (PG[2], PG[3]) if oc % 2 == 0 else (PG[0], PG[1])

                @block.tensor
                def _(t, oc=oc, ad=al_dve, pqe=prev_q_evac, pg_sc=pg_sc,
                      qe7=q_evac[7]):
                    if oc == 0:
                        t.wait_ge(dve.h, ad)
                        t.wait_ge(act.h, pqe)
                    if oc == 1:
                        t.wait_ge(act.h, qe7)
                    if oc >= 2:
                        t.wait_ge(dve.h, sc_dves[oc - 2])
                    for half in range(2):
                        pgn = pg_sc[half]
                        for c in range(2):
                            i = t.matmul(pgn[:, c * 512:(c + 1) * 512],
                                         hsel[:, oc * 128:(oc + 1) * 128],
                                         nrm16[:, half * 1024 + c * 512:
                                               half * 1024 + (c + 1) * 512],
                                         start=True, stop=True)
                    pe.inc(i)
                hs_pe = pe.n

                @block.vector
                def _(v, oc=oc, hp=hs_pe, pg_sc=pg_sc):
                    v.wait_ge(pe.h, hp)
                    for half in range(2):
                        pgn = pg_sc[half]
                        i = v.tensor_mul(
                            araw(oc)[:, half * 1024:(half + 1) * 1024],
                            araw(oc)[:, half * 1024:(half + 1) * 1024],
                            pgn[:])
                    dve.inc(i)
                sc_dves.append(dve.n)

            wo_marks = []
            for oco in range(8):
                pg_pair = (PG[0], PG[1]) if oco % 2 == 0 else (PG[2], PG[3])

                @block.tensor
                def _(t, oco=oco, pg_pair=pg_pair):
                    if oco == 0:
                        t.wait_ge(*w_marks["wo"])
                        t.wait_ge(dve.h, sc_dves[-1])
                    if oco >= 2:
                        t.wait_ge(dve.h, wo_marks[oco - 2])
                    for hc in range(NHC):
                        for half in range(2):
                            pgn = pg_pair[half]
                            for c in range(2):
                                i = t.matmul(
                                    pgn[:, c * 512:(c + 1) * 512],
                                    w2s[:, hc * 1024 + oco * 128:
                                        hc * 1024 + (oco + 1) * 128],
                                    araw(hc)[:, half * 1024 + c * 512:
                                             half * 1024 + (c + 1) * 512],
                                    start=(hc == 0), stop=(hc == NHC - 1))
                    pe.inc(i)
                wo_pe = pe.n

                @block.vector
                def _(v, oco=oco, wp=wo_pe, pg_pair=pg_pair):
                    v.wait_ge(pe.h, wp)
                    for half in range(2):
                        pgn = pg_pair[half]
                        i = v.scalar_tensor_tensor(
                            x[oco][:, half * 1024:(half + 1) * 1024],
                            pgn[:], boc[:, oco:oco + 1],
                            x[oco][:, half * 1024:(half + 1) * 1024],
                            ALU.add, ALU.add)
                    dve.inc(i)
                wo_marks.append(dve.n)
            return [(dve.h, wo_marks[hc]) for hc in range(NHC)]

        def emit_ffn_sparse(x_ready, w_marks, rearm):
            xn_marks = emit_ln(x_ready)
            g = emit_gate(xn_marks)
            oh_act, oh_dve = g["oh_act"], g["oh_dve"]

            @block.tensor
            def _(t, oa=oh_act, od=oh_dve):
                t.wait_ge(act.h, oa)
                t.wait_ge(dve.h, od)
                for tc in range(16):
                    reg = PG[tc // 8][0:8, (tc % 8) * 128:(tc % 8) * 128 + 128]
                    for kc in range(tc + 1):
                        i = t.matmul(reg, ohb[:, kc * 8:kc * 8 + 8],
                                     ltst[:] if kc == tc else ones2d[:],
                                     start=(kc == 0), stop=(kc == tc))
                    pe.inc(i)
            p_pe = pe.n

            @block.scalar
            def _(s, pp=p_pe):
                s.wait_ge(pe.h, pp)
                act.inc(s.copy(p_sb[:, 0:1024], PG[0][0:8, :]))
                act.inc(s.copy(p_sb[:, 1024:2048], PG[1][0:8, :]))
            p_act = act.n

            @block.tensor
            def _(t, pa=p_act):
                t.wait_ge(act.h, pa)
                for tt in range(16):
                    i = t.transpose(PG[3][:, 128 + tt * 8:128 + tt * 8 + 8],
                                    p_sb[:, tt * 128:(tt + 1) * 128],
                                    ident[0:8, 0:8])
                    pe.inc(i)
            ptr_pe = pe.n

            slot_marks = []

            @block.vector
            def _(v, pp=ptr_pe):
                v.wait_ge(pe.h, pp)
                for tt in range(16):
                    vstep(v, v.tensor_mul(ptmp[:],
                                          PG[3][:, 128 + tt * 8:128 + tt * 8 + 8],
                                          ohf[:, tt * 8:tt * 8 + 8]))
                    vstep(v, v.reduce_sum(rank_c[:], ptmp[:],
                                          axis=mybir.AxisListType.X))
                    vstep(v, v.scalar_tensor_tensor(
                        slot_f[:], idxs[:, tt:tt + 1], 512.0, rank_c[:],
                        ALU.mult, ALU.add))
                    i = v.tensor_copy(slotu[:, tt:tt + 1], slot_f[:])
                    if tt < 15:
                        vstep(v, i)
                    slot_marks.append(dve.n + (1 if tt == 15 else 0))
                dve.inc(i)
            slot_dve = dve.n

            @block.sync
            def _(sy):
                for hc in range(NHC):
                    sy.wait_ge(dve.h, xn_marks[1][hc])
                    xd.inc(sy.dma_start(d_xnd[hc * 128:(hc + 1) * 128, :],
                                        xn[hc][:]), 16)
            xd_full = xd.n

            sct_done = []
            for tt in range(16):
                par = tt % 2

                @block.sync
                def _(sy, tt=tt, par=par, xdf=xd_full):
                    if tt == 0:
                        sy.wait_ge(xd.h, xdf)
                    if tt >= 2:
                        sy.wait_ge(sct[par].h, sct_done[tt - 2])
                    xtl[par].inc(sy.dma_start(
                        xnt_b[par][:],
                        d_xnd[:, tt * 128:(tt + 1) * 128], transpose=True), 16)

                @block.gpsimd
                def _(g_, tt=tt, par=par, sd=slot_marks[tt], xt=xtl[par].n):
                    if tt == 0:
                        g_.wait_ge(zs.h, zs_full)
                    g_.wait_ge(dve.h, sd)
                    g_.wait_ge(xtl[par].h, xt)
                    sct[par].inc(g_.indirect_dma_start(
                        d_xs, IndirectOffsetOnAxis(slotu[:, tt:tt + 1], 0),
                        xnt_b[par][:], None), 16)
                sct_done.append(sct[par].n)
            sct_full = (sct[0].n, sct[1].n)

            gelu_prev = [0, 0]
            yd_prev = (0, 0)
            xg_marks = []
            for e in range(E):
                @block.sync
                def _(sy, e=e, sf=sct_full):
                    if e == 0:
                        sy.wait_ge(sct[0].h, sf[0])
                        sy.wait_ge(sct[1].h, sf[1])
                    if e >= 2:
                        sy.wait_ge(pe.h, xg_marks[e - 2][-1])
                    for hc in range(NHC):
                        xgl[e % 2].inc(sy.dma_start(
                            xgt(e % 2, hc),
                            d_xs[e * 512:(e + 1) * 512,
                                 hc * 128:(hc + 1) * 128], transpose=True), 16)
                xg_full = xgl[e % 2].n

                abase = act.n
                g_mark = lambda ic: abase + ic + 1
                mm1_marks = []
                for ic in range(8):
                    @block.tensor
                    def _(t, e=e, ic=ic, xf=xg_full,
                          pga=gelu_prev[ic % 2], pa=p_act):
                        if ic == 0:
                            t.wait_ge(*w_marks[("w1", e)])
                            t.wait_ge(xgl[e % 2].h, xf)
                        if ic >= 2:
                            t.wait_ge(act.h, g_mark(ic - 2))
                        else:
                            t.wait_ge(act.h, max(pga, pa))
                        for hc in range(NHC):
                            i = t.matmul(
                                PG[ic % 2][:, 0:512],
                                w1s[:, hc * 1024 + ic * 128:
                                    hc * 1024 + (ic + 1) * 128],
                                xgt(e % 2, hc),
                                start=(hc == 0), stop=(hc == NHC - 1))
                        pe.inc(i)
                    mm1_marks.append(pe.n)
                xg_marks.append(tuple(mm1_marks))

                mm2_marks = []
                for sc in range(4):
                    pg2 = PG[2 + sc % 2]

                    @block.tensor
                    def _(t, e=e, sc=sc, pg2=pg2, sd=slot_dve):
                        t.wait_ge(*w_marks[("w2", e)]) if sc == 0 else None
                        t.wait_ge(act.h, abase + 8)
                        if sc >= 2:
                            t.wait_ge(act.h, abase + 8 + (sc - 2) + 1)
                        elif e > 0:
                            t.wait_ge(act.h, abase - 12 + 8 + (sc + 2) + 1)
                        else:
                            t.wait_ge(dve.h, sd)
                        for c in range(2):
                            t.matmul(pg2[:, c * 512:(c + 1) * 512],
                                     esel[:, e * 128:(e + 1) * 128],
                                     b2bf[:, c * 512:(c + 1) * 512],
                                     start=True, stop=False)
                        for ic in range(8):
                            for c in range(2):
                                i = t.matmul(
                                    pg2[:, c * 512:(c + 1) * 512],
                                    h1s(ic)[:, sc * 128:(sc + 1) * 128],
                                    w2s[:, ic * 1024 + c * 512:
                                        ic * 1024 + (c + 1) * 512],
                                    start=False, stop=(ic == 7))
                            pass
                        pe.inc(i)
                    mm2_marks.append(pe.n)

                yd_base = (ysd[0].n, ysd[1].n)

                @block.scalar
                def _(s, e=e, m1=tuple(mm1_marks), m2=tuple(mm2_marks),
                      ydp=yd_prev, ydb=yd_base):
                    for ic in range(8):
                        s.wait_ge(pe.h, m1[ic])
                        act.inc(s.activation(
                            h1s(ic), PG[ic % 2][:, 0:512], AF.Gelu,
                            bias=b1c[:, e * 8 + ic:e * 8 + ic + 1]))
                    for sc in range(4):
                        s.wait_ge(pe.h, m2[sc])
                        if sc < 2:
                            s.wait_ge(ysd[sc % 2].h, ydp[sc % 2])
                        else:
                            s.wait_ge(ysd[sc % 2].h, ydb[sc % 2] + 16)
                        act.inc(s.copy(yt_b[sc % 2][:], PG[2 + sc % 2][:]))
                assert act.n == abase + 12
                gelu_prev = [abase + 7, abase + 8]

                @block.sync
                def _(sy, e=e, ab=abase):
                    for sc in range(4):
                        sy.wait_ge(act.h, ab + 8 + sc + 1)
                        ysd[sc % 2].inc(sy.dma_start(
                            d_ys[e * 512 + sc * 128:e * 512 + (sc + 1) * 128, :],
                            yt_b[sc % 2][:]), 16)
                yd_prev = (ysd[0].n, ysd[1].n)

                if e + 1 < E:
                    @block.gpsimd
                    def _(g_, e=e, wf=mm1_marks[-1], wf2=pe.n):
                        g_.wait_ge(pe.h, wf)
                        w_marks[("w1", e + 1)] = (w1m.h, w1m.inc(
                            g_.dma_start(w_dst(w1s), w_src(d_w1[e + 1])), 16))
                        g_.wait_ge(pe.h, wf2)
                        w_marks[("w2", e + 1)] = (w2m.h, w2m.inc(
                            g_.dma_start(w_dst(w2s), w_src(d_w2[e + 1])), 16))
            ys_full = (ysd[0].n, ysd[1].n)

            cast_acts = []
            tr_pes = []
            acc_dves = []
            for tt in range(16):
                par = tt % 2

                @block.gpsimd
                def _(g_, tt=tt, par=par, yf=ys_full):
                    if tt == 0:
                        g_.wait_ge(ysd[0].h, yf[0])
                        g_.wait_ge(ysd[1].h, yf[1])
                    if tt >= 2:
                        g_.wait_ge(act.h, cast_acts[tt - 2])
                    ygt[par].inc(g_.indirect_dma_start(
                        xnt_b[par][:], None,
                        d_ys, IndirectOffsetOnAxis(slotu[:, tt:tt + 1], 0)), 16)

                @block.scalar
                def _(s, tt=tt, par=par, yg=ygt[par].n, pp=(tr_pes[tt - 2] if tt >= 2 else 0)):
                    s.wait_ge(ygt[par].h, yg)
                    if pp:
                        s.wait_ge(pe.h, pp)
                    act.inc(s.copy(ytf_b[par][:], xnt_b[par][:]))
                cast_acts.append(act.n)

                @block.tensor
                def _(t, tt=tt, par=par, ca=cast_acts[-1],
                      pd=(acc_dves[tt - 2] if tt >= 2 else 0)):
                    t.wait_ge(act.h, ca)
                    if pd:
                        t.wait_ge(dve.h, pd)
                    for hc in range(NHC):
                        i = t.transpose(PG[par][:, hc * 128:(hc + 1) * 128],
                                        ytf_b[par][:, hc * 128:(hc + 1) * 128],
                                        ident[:])
                        pe.inc(i)
                tr_pes.append(pe.n)

                @block.vector
                def _(v, tt=tt, par=par, tp=tr_pes[-1]):
                    v.wait_ge(pe.h, tp)
                    for hc in range(NHC):
                        dve.inc(v.scalar_tensor_tensor(
                            x[hc][:, tt * 128:(tt + 1) * 128],
                            PG[par][:, hc * 128:(hc + 1) * 128], 0.5,
                            x[hc][:, tt * 128:(tt + 1) * 128],
                            ALU.mult, ALU.add))
                acc_dves.append(dve.n)

            if rearm is not None:
                d1, d2 = rearm

                @block.gpsimd
                def _(g_, wf=pe.n):
                    g_.wait_ge(pe.h, wf)
                    w_marks[("w1", 0)] = (w1m.h, w1m.inc(
                        g_.dma_start(w_dst(w1s), w_src(d1)), 16))
                    w_marks[("w2", 0)] = (w2m.h, w2m.inc(
                        g_.dma_start(w_dst(w2s), w_src(d2)), 16))
            return [(dve.h, acc_dves[-1])] * NHC

        xl_base = xl.n

        @block.sync
        def _(sy):
            for hc in range(NHC):
                xl.inc(sy.dma_start(x[hc][:],
                                    d_x[0, hc * 128:(hc + 1) * 128, :]), 16)
            sy.wait_ge(im.h, init_im)
            for r in range(48):
                for c in range(2):
                    zs.inc(sy.dma_start(
                        d_xs[r * 128:(r + 1) * 128, c * 512:(c + 1) * 512],
                        zt[:]), 16)

        @block.scalar
        def _(s):
            s.wait_ge(im.h, init_im)
            for r in range(48):
                for c in range(2):
                    zs.inc(s.dma_start(
                        d_ys[r * 128:(r + 1) * 128, c * 512:(c + 1) * 512],
                        zt[:]), 16)
        zs_full = zs.n

        w_marks = {}
        first_kind = sub[0] if sub else "f"

        @block.gpsimd
        def _(g):
            if first_kind == "f":
                w_marks[("w1", 0)] = (w1m.h, w1m.inc(
                    g.dma_start(w_dst(w1s), w_src(d_w1[0])), 16))
                w_marks[("w2", 0)] = (w2m.h, w2m.inc(
                    g.dma_start(w_dst(w2s), w_src(d_w2[0])), 16))
            else:
                w_marks["wv"] = (w1m.h, w1m.inc(
                    g.dma_start(w_dst(w1s), w_src(d_wv)), 16))
                w_marks["wk"] = (w2m.h, w2m.inc(
                    g.dma_start(w_dst(w2s), w_src(d_wk)), 16))

        x_ready = [(xl.h, xl.n)] * NHC
        for s_i in range(n_seq):
            for sub_i, kind in enumerate(sub):
                last = (s_i, sub_i) == (n_seq - 1, len(sub) - 1)
                nxt_kind = None
                if not last:
                    nxt_kind = sub[sub_i + 1] if sub_i + 1 < len(sub) else sub[0]
                if kind == "f":
                    rearm = None
                    if nxt_kind == "a":
                        rearm = (d_wv, d_wk)
                    elif nxt_kind == "f":
                        rearm = (d_w1[0], d_w2[0])
                    if _CFG.get("sparse", True):
                        done = emit_ffn_sparse(x_ready, w_marks, rearm=rearm)
                    else:
                        done = emit_ffn(x_ready, w_marks, rearm=rearm)
                    if rearm is not None and nxt_kind == "a":
                        w_marks["wv"] = w_marks[("w1", 0)]
                        w_marks["wk"] = w_marks[("w2", 0)]
                    x_ready = done
                elif kind == "a":
                    done = emit_attn(x_ready, w_marks)
                    x_ready = done
                    if nxt_kind == "f":
                        @block.gpsimd
                        def _(g, wf=pe.n):
                            g.wait_ge(pe.h, wf)
                            w_marks[("w1", 0)] = (w1m.h, w1m.inc(
                                g.dma_start(w_dst(w1s), w_src(d_w1[0])), 16))
                            w_marks[("w2", 0)] = (w2m.h, w2m.inc(
                                g.dma_start(w_dst(w2s), w_src(d_w2[0])), 16))
                    elif nxt_kind == "a":
                        @block.gpsimd
                        def _(g, wf=pe.n):
                            g.wait_ge(pe.h, wf)
                            w_marks["wv"] = (w1m.h, w1m.inc(
                                g.dma_start(w_dst(w1s), w_src(d_wv)), 16))
                            w_marks["wk"] = (w2m.h, w2m.inc(
                                g.dma_start(w_dst(w2s), w_src(d_wk)), 16))

            xl_base2 = xl.n

            @block.sync
            def _(sy, s_i=s_i, xr=x_ready):
                for hc in range(NHC):
                    sem, val = xr[hc]
                    sy.wait_ge(sem, val)
                    st.inc(sy.dma_start(d_y[s_i, hc * 128:(hc + 1) * 128, :],
                                        x[hc][:]), 16)
                if s_i + 1 < n_seq:
                    sy.wait_ge(st.h, st.n)
                    for hc in range(NHC):
                        xl.inc(sy.dma_start(
                            x[hc][:],
                            d_x[s_i + 1, hc * 128:(hc + 1) * 128, :]), 16)
            if s_i + 1 < n_seq:
                x_ready = [(xl.h, xl.n)] * NHC

        @block.sync
        def _(sy):
            sy.wait_ge(st.h, st.n)

    return nc


def kernel(L0_attn_x, L0_attn_w_qkv, L0_attn_w_out, L0_attn_b_out,
           L0_attn_random_features, L0_ffn_Wg, L0_ffn_bg, L0_ffn_W1,
           L0_ffn_b1, L0_ffn_W2, L0_ffn_b2):
    n_seq = _CFG["n_seq"]
    B = L0_attn_x.shape[0]
    n_cores = B // n_seq
    xt = np.ascontiguousarray(np.transpose(np.asarray(L0_attn_x), (0, 2, 1)))

    wqkv = np.asarray(L0_attn_w_qkv)
    wqT = np.ascontiguousarray(wqkv[0:1024].T)
    wkT = np.ascontiguousarray(wqkv[1024:2048].T)
    wvT = np.ascontiguousarray(wqkv[2048:3072].T)
    woT = np.ascontiguousarray(np.asarray(L0_attn_w_out).T)
    rf = np.asarray(L0_attn_random_features)
    rfdup = np.ascontiguousarray(np.tile(rf.T, (2, 1)))
    b1 = np.asarray(L0_ffn_b1)
    b1c = np.ascontiguousarray(
        b1.reshape(E, 8, 128).transpose(2, 0, 1).reshape(128, 64))
    wgc = np.ascontiguousarray(
        np.asarray(L0_ffn_Wg).reshape(8, 128, 8).transpose(1, 0, 2).reshape(128, 64))
    bgc = np.ascontiguousarray(np.asarray(L0_ffn_bg).reshape(8, 1))
    boc = np.ascontiguousarray(np.asarray(L0_attn_b_out).reshape(8, 128).T)
    ident = np.eye(128, dtype=np.float32)
    io8 = np.tile(np.arange(8, dtype=np.float32), (128, 1))
    negsel = np.zeros((2, 128), np.float32)
    negsel[0, 0:64] = -1.0
    negsel[1, 64:128] = -1.0
    hselm = np.zeros((16, H), np.float32)
    for hh in range(16):
        hselm[hh, hh * 64:(hh + 1) * 64] = 1.0
    eselm = np.zeros((8, H), np.float32)
    for e in range(8):
        eselm[e, e * 128:(e + 1) * 128] = 1.0

    shared = {
        "W1": np.ascontiguousarray(np.asarray(L0_ffn_W1), dtype=np.float32),
        "W2": np.ascontiguousarray(np.asarray(L0_ffn_W2), dtype=np.float32),
        "b1c": b1c.astype(np.float32),
        "b2": np.ascontiguousarray(np.asarray(L0_ffn_b2), dtype=np.float32),
        "wgc": wgc.astype(np.float32), "bgc": bgc.astype(np.float32),
        "wqT": wqT.astype(np.float32), "wkT": wkT.astype(np.float32),
        "wvT": wvT.astype(np.float32), "woT": woT.astype(np.float32),
        "rfdup": rfdup.astype(np.float32), "boc": boc.astype(np.float32),
        "ident": ident, "iota8": io8, "negsel": negsel,
        "hsel": hselm, "esel": eselm,
        "ltstrict": np.triu(np.ones((128, 128), np.float32), k=1),
        "ones2d": np.ones((128, 128), np.float32),
    }
    nc = build_program(n_seq, _CFG["sub"])
    in_maps = []
    for c in range(n_cores):
        m = dict(shared)
        m["x_t"] = np.ascontiguousarray(xt[c * n_seq:(c + 1) * n_seq])
        in_maps.append(m)
    res = run_bass_kernel_spmd(nc, in_maps, core_ids=list(range(n_cores)))
    global _LAST
    _LAST = res.results
    outs = [r["y_t"] for r in res.results]
    y = np.concatenate(outs, axis=0)
    return np.ascontiguousarray(np.transpose(y, (0, 2, 1)))


# version 07
# speedup vs baseline: 1.5661x; 1.0243x over previous
import contextlib
import numpy as np
import concourse.bass as bass
import concourse.mybir as mybir
from concourse.bass_utils import run_bass_kernel_spmd
from concourse.bass_core_rust import IndirectOffsetOnAxis

F32 = mybir.dt.float32
BF16 = mybir.dt.bfloat16
U32 = mybir.dt.uint32
AF = mybir.ActivationFunctionType
ALU = mybir.AluOpType

H = 1024
T = 2048
E = 8
HD = 64
NHC = H // 128

_CFG = {"n_seq": 2, "sub": ("f", "a", "f")}
_LAST = None


class Sem:
    def __init__(self, es, nc, name):
        self.h = es.enter_context(nc.semaphore(name))
        self.n = 0

    def inc(self, instr, k=1):
        instr.then_inc(self.h, k)
        self.n += k
        return self.n


def build_program(n_seq, sub):
    nc = bass.Bass("TRN2", target_bir_lowering=False)

    d_x = nc.dram_tensor("x_t", [n_seq, H, T], F32, kind="ExternalInput").ap()
    d_y = nc.dram_tensor("y_t", [n_seq, H, T], F32, kind="ExternalOutput").ap()
    d_w1 = nc.dram_tensor("W1", [E, H, H], F32, kind="ExternalInput").ap()
    d_w2 = nc.dram_tensor("W2", [E, H, H], F32, kind="ExternalInput").ap()
    d_b1c = nc.dram_tensor("b1c", [128, 64], F32, kind="ExternalInput").ap()
    d_b2 = nc.dram_tensor("b2", [E, H], F32, kind="ExternalInput").ap()
    d_wgc = nc.dram_tensor("wgc", [128, 64], F32, kind="ExternalInput").ap()
    d_bgc = nc.dram_tensor("bgc", [8, 1], F32, kind="ExternalInput").ap()
    d_wq = nc.dram_tensor("wqT", [H, H], F32, kind="ExternalInput").ap()
    d_wk = nc.dram_tensor("wkT", [H, H], F32, kind="ExternalInput").ap()
    d_wv = nc.dram_tensor("wvT", [H, H], F32, kind="ExternalInput").ap()
    d_wo = nc.dram_tensor("woT", [H, H], F32, kind="ExternalInput").ap()
    d_rf = nc.dram_tensor("rfdup", [128, HD], F32, kind="ExternalInput").ap()
    d_boc = nc.dram_tensor("boc", [128, 8], F32, kind="ExternalInput").ap()
    d_ident = nc.dram_tensor("ident", [128, 128], F32, kind="ExternalInput").ap()
    d_iota8 = nc.dram_tensor("iota8", [128, 8], F32, kind="ExternalInput").ap()
    d_negsel = nc.dram_tensor("negsel", [2, 128], F32, kind="ExternalInput").ap()
    d_hsel = nc.dram_tensor("hsel", [16, H], F32, kind="ExternalInput").ap()
    d_esel = nc.dram_tensor("esel", [8, H], F32, kind="ExternalInput").ap()
    d_lt = nc.dram_tensor("ltstrict", [128, 128], F32, kind="ExternalInput").ap()
    d_on = nc.dram_tensor("ones2d", [128, 128], F32, kind="ExternalInput").ap()
    d_xs = nc.dram_tensor("xs_scr", [6144, H], BF16, kind="Internal").ap()
    d_ys = nc.dram_tensor("ys_scr", [6144, H], BF16, kind="Internal").ap()
    d_xnd = nc.dram_tensor("xnd_scr", [H, T], BF16, kind="Internal").ap()

    dbg = _CFG.get("dbg", False)
    if dbg:
        d_dbg_xn = nc.dram_tensor("dbg_xn", [H, T], BF16, kind="ExternalOutput").ap()
        d_dbg_lg = nc.dram_tensor("dbg_lg", [8, T], F32, kind="ExternalOutput").ap()
        d_dbg_oh = nc.dram_tensor("dbg_oh", [8, T], BF16, kind="ExternalOutput").ap()
        d_dbg_h1 = nc.dram_tensor("dbg_h1", [128, 8192], BF16, kind="ExternalOutput").ap()
        d_dbg_mk = nc.dram_tensor("dbg_mk", [128, T], BF16, kind="ExternalOutput").ap()
        d_dbg_xe = nc.dram_tensor("dbg_xe", [E, H, T], F32, kind="ExternalOutput").ap()
        d_dbg_v = nc.dram_tensor("dbg_v", [128, 16384], BF16, kind="ExternalOutput").ap()
        d_dbg_kv = nc.dram_tensor("dbg_kv", [128, 520], BF16, kind="ExternalOutput").ap()
        d_dbg_ar = nc.dram_tensor("dbg_ar", [128, 16384], BF16, kind="ExternalOutput").ap()
        d_dbg_nr = nc.dram_tensor("dbg_nr", [16, T], BF16, kind="ExternalOutput").ap()
    es = contextlib.ExitStack()
    with es:
        sbt = lambda name, shape, dt: es.enter_context(nc.sbuf_tensor(name, shape, dt))
        pst = lambda name, shape: es.enter_context(nc.psum_tensor(name, shape, F32))

        x = [sbt(f"x{i}", [128, T], F32) for i in range(NHC)]
        xn = [sbt(f"xn{i}", [128, T], BF16) for i in range(NHC)]
        w1s = sbt("w1s", [128, 8192], BF16)
        w2s = sbt("w2s", [128, 8192], BF16)
        h1m = sbt("h1m", [128, 8192], BF16)
        vat = sbt("vat", [128, 16384], BF16)
        ar32 = sbt("ar32", [128, 3072], F32)
        ident = sbt("sb_ident", [128, 128], F32)
        iota8 = sbt("sb_iota8", [128, 8], F32)
        negsel = sbt("sb_negsel", [2, 128], BF16)
        hsel = sbt("sb_hsel", [16, H], BF16)
        esel = sbt("sb_esel", [8, H], BF16)
        rf_f = sbt("rf_f", [128, HD], F32)
        rf_s = sbt("rf_s", [128, HD], BF16)
        b1c = sbt("sb_b1c", [128, 64], F32)
        b2bf = sbt("b2bf", [8, H], BF16)
        wg = sbt("wg", [128, 64], BF16)
        bgc = sbt("sb_bgc", [8, 1], F32)
        boc = sbt("sb_boc", [128, 8], F32)
        ones_col = sbt("ones_col", [128, 1], BF16)
        ones_bc = sbt("ones_bc", [1, 128], BF16)
        bdiag = sbt("bdiag", [128, 2], BF16)
        eps_col = sbt("eps_col", [1, 1], F32)
        kvext = sbt("kvext", [128, 520], BF16)
        mx8 = sbt("mx8", [128, 8], F32)
        idxu = sbt("idxu", [128, 8], U32)
        idxf = sbt("idxf", [128, 1], F32)
        ohf = sbt("ohf", [128, 128], F32)
        ohstage = sbt("ohstage", [128, 8], F32)
        ltst = sbt("sb_ltst", [128, 128], BF16)
        ones2d = sbt("sb_ones2d", [128, 128], BF16)
        idxs = sbt("idxs", [128, 16], F32)
        slotu = sbt("slotu", [128, 16], U32)
        ohb = sbt("ohb", [128, 128], BF16)
        rank_c = sbt("rank_c", [128, 1], F32)
        slot_f = sbt("slot_f", [128, 1], F32)
        ptmp = sbt("ptmp", [128, 8], F32)
        zt = sbt("zt", [128, 512], BF16)
        nrm16 = sbt("nrm16", [16, T], BF16)
        tmp_r = [sbt("tmp_r0", [1, 1024], BF16), sbt("tmp_r1", [1, 1024], BF16)]

        h1t = lambda ic: h1m[:, ic * 1024:(ic + 1) * 1024]
        xb_sl = [h1m[:, 0:2048], h1m[:, 2048:4096]]
        xsq_sl = [h1m[:, 4096:6144], h1m[:, 6144:8192]]
        kq_sb = h1m[:, 0:2048]
        sq_sb = h1m[:, 2048:4096]
        kp_sb = h1m[:, 4096:5120]
        nr_bf = h1m[0:2, 5120:7168]
        eq_sb = h1m[:, 2048:4096]
        mask_sl = [vat[:, 0:2048], vat[:, 2048:4096]]
        onehotT = vat[0:8, 4096:6144]
        m_bf = vat[0:1, 6144:7168]
        rstd_bf = vat[0:1, 7168:8192]
        vtile = lambda tt: vat[:, tt * 1024:(tt + 1) * 1024]
        araw = lambda oc: vat[:, oc * 2048:(oc + 1) * 2048]
        h1s = lambda ic: h1m[:, ic * 512:(ic + 1) * 512]
        xnt_b = [h1m[:, 4096 + 1024 * i:4096 + 1024 * (i + 1)] for i in range(2)]
        yt_b = [h1m[:, 6144 + 1024 * i:6144 + 1024 * (i + 1)] for i in range(2)]
        xgt = lambda sl, hc: vat[:, sl * 4096 + hc * 512:sl * 4096 + (hc + 1) * 512]
        p_sb = ar32[0:8, 0:2048]
        ytf_b = [ar32[:, 0:1024], ar32[:, 1024:2048]]
        t1_sl = ar32[:, 0:1024]
        m_row = ar32[0:1, 1024:2048]
        pair_f = ar32[0:2, 1024:2048]
        msq_row = ar32[0:1, 2048:3072]
        var_row = ar32[0:1, 0:1024]
        rstd_f = ar32[0:1, 2048:3072]
        lg_sb = ar32[0:8, 0:2048]

        PG = [pst(f"PG{i}", [128, 1024]) for i in range(4)]

        dl = Sem(es, nc, "dl")
        w1m = Sem(es, nc, "w1m")
        w2m = Sem(es, nc, "w2m")
        im = Sem(es, nc, "im")
        xl = Sem(es, nc, "xl")
        pe = Sem(es, nc, "pe")
        act = Sem(es, nc, "act")
        dve = Sem(es, nc, "dve")
        sp = Sem(es, nc, "sp")
        st = Sem(es, nc, "st")
        xd = Sem(es, nc, "xd")
        xtl = [Sem(es, nc, "xtl0"), Sem(es, nc, "xtl1")]
        sct = [Sem(es, nc, "sct0"), Sem(es, nc, "sct1")]
        xgl = [Sem(es, nc, "xgl0"), Sem(es, nc, "xgl1")]
        ysd = [Sem(es, nc, "ysd0"), Sem(es, nc, "ysd1")]
        ygt = [Sem(es, nc, "ygt0"), Sem(es, nc, "ygt1")]
        zs = Sem(es, nc, "zs")

        block = es.enter_context(nc.Block())

        @block.gpsimd
        def _(g):
            for dst, src in [(ident[:], d_ident), (iota8[:], d_iota8),
                             (negsel[:], d_negsel), (hsel[:], d_hsel),
                             (esel[:], d_esel), (rf_f[:], d_rf),
                             (ltst[:], d_lt), (ones2d[:], d_on),
                             (b1c[:], d_b1c), (b2bf[:], d_b2),
                             (wg[:], d_wgc), (bgc[:], d_bgc), (boc[:], d_boc)]:
                dl.inc(g.dma_start(dst, src), 16)
            im.inc(g.memset(ones_col[:], 1.0))
            im.inc(g.memset(ones_bc[:], 1.0))
            im.inc(g.memset(bdiag[:], 0.0))
            g.wait_ge(im.h, im.n)
            im.inc(g.memset(bdiag[0:64, 0:1], 1.0))
            g.wait_ge(im.h, im.n)
            im.inc(g.memset(bdiag[64:128, 1:2], 1.0))
            im.inc(g.memset(eps_col[:], 1e-5))
            im.inc(g.memset(zt[:], 0.0))
        init_dl = dl.n
        init_im = im.n

        @block.scalar
        def _(s):
            s.wait_ge(dl.h, init_dl)
            s.wait_ge(im.h, init_im)
            act.inc(s.activation(rf_s[:], rf_f[:], AF.Copy, scale=0.125))

        def vstep(v, instr):
            dve.inc(instr)
            v.wait_ge(dve.h, dve.n)

        def w_src(d_w):
            return d_w.rearrange("(c p) i -> p c i", p=128)

        def w_dst(slot):
            return slot[:].rearrange("p (c i) -> p c i", c=8)

        def emit_ln(x_ready):
            pe_base = pe.n
            pe_mark = lambda hc: pe_base + hc + 1
            act_base = act.n
            dve_base = dve.n

            @block.scalar
            def _(s):
                for hc in range(NHC):
                    sem, val = x_ready[hc]
                    s.wait_ge(sem, val)
                    if hc == 0:
                        s.wait_ge(x_ready[NHC - 1][0], x_ready[NHC - 1][1])
                    if hc >= 2:
                        s.wait_ge(pe.h, pe_mark(hc - 2))
                    act.inc(s.activation(xb_sl[hc % 2], x[hc][:], AF.Copy))
            assert act.n == act_base + NHC

            @block.vector
            def _(v):
                for hc in range(NHC):
                    sem, val = x_ready[hc]
                    v.wait_ge(sem, val)
                    if hc == 0:
                        v.wait_ge(x_ready[NHC - 1][0], x_ready[NHC - 1][1])
                    if hc >= 2:
                        v.wait_ge(pe.h, pe_mark(hc - 2))
                    dve.inc(v.tensor_mul(xsq_sl[hc % 2], x[hc][:], x[hc][:]))
            assert dve.n == dve_base + NHC

            @block.tensor
            def _(t):
                t.wait_ge(im.h, init_im)
                for hc in range(NHC):
                    t.wait_ge(act.h, act_base + hc + 1)
                    t.wait_ge(dve.h, dve_base + hc + 1)
                    first = hc == 0
                    last = hc == NHC - 1
                    sl = hc % 2
                    for half in range(2):
                        pgs, pgq = (PG[0], PG[1]) if half == 0 else (PG[2], PG[3])
                        for c in range(2):
                            co = half * 1024 + c * 512
                            t.matmul(pgs[0:1, c * 512:(c + 1) * 512],
                                     ones_col[:], xb_sl[sl][:, co:co + 512],
                                     start=first, stop=last)
                            i2 = t.matmul(pgq[0:1, c * 512:(c + 1) * 512],
                                          ones_col[:], xsq_sl[sl][:, co:co + 512],
                                          start=first, stop=last)
                    pe.inc(i2)
            sums_pe = pe.n
            assert sums_pe == pe_base + NHC

            xn_marks = [[0] * NHC, [0] * NHC]
            for half in range(2):
                pgs, pgq = (PG[0], PG[1]) if half == 0 else (PG[2], PG[3])

                @block.vector
                def _(v, pgs=pgs, pgq=pgq):
                    v.wait_ge(pe.h, sums_pe)
                    vstep(v, v.tensor_scalar_mul(m_row, pgs[0:1, :], 1.0 / H))
                    vstep(v, v.tensor_scalar_mul(msq_row, pgq[0:1, :], 1.0 / H))
                    vstep(v, v.tensor_mul(var_row, m_row, m_row))
                    dve.inc(v.tensor_sub(var_row, msq_row, var_row))
                v_rows = dve.n

                @block.scalar
                def _(s):
                    s.wait_ge(dve.h, v_rows)
                    act.inc(s.activation(rstd_f, var_row, AF.Ln,
                                         bias=eps_col[0:1, :]))
                    s.wait_ge(act.h, act.n)
                    act.inc(s.activation(rstd_f, rstd_f, AF.Exp, scale=-0.5))
                    s.wait_ge(act.h, act.n)
                    s.activation(m_bf[0:1, 0:1024], m_row, AF.Copy)
                    act.inc(s.activation(rstd_bf[0:1, 0:1024], rstd_f, AF.Copy))
                s_rows = act.n

                @block.tensor
                def _(t, pgs=pgs, pgq=pgq):
                    t.wait_ge(act.h, s_rows)
                    for c in range(2):
                        t.matmul(pgs[:, c * 512:(c + 1) * 512], ones_bc[:],
                                 m_bf[0:1, c * 512:(c + 1) * 512],
                                 start=True, stop=True)
                        i = t.matmul(pgq[:, c * 512:(c + 1) * 512], ones_bc[:],
                                     rstd_bf[0:1, c * 512:(c + 1) * 512],
                                     start=True, stop=True)
                    pe.inc(i)
                bc_pe = pe.n

                @block.vector
                def _(v, half=half, pgs=pgs, pgq=pgq):
                    v.wait_ge(pe.h, bc_pe)
                    co = half * 1024
                    for hc in range(NHC):
                        vstep(v, v.scalar_tensor_tensor(
                            t1_sl, x[hc][:, co:co + 1024], 1.0, pgs[:],
                            ALU.mult, ALU.subtract))
                        i = v.tensor_mul(xn[hc][:, co:co + 1024], t1_sl, pgq[:])
                        if hc < NHC - 1:
                            vstep(v, i)
                        else:
                            dve.inc(i)
                        xn_marks[half][hc] = dve.n
            return xn_marks

        def emit_gate(xn_marks):
            @block.tensor
            def _(t):
                t.wait_ge(dl.h, init_dl)
                t.wait_ge(im.h, init_im)
                for half in range(2):
                    pg = PG[half]
                    for hc in range(NHC):
                        t.wait_ge(dve.h, xn_marks[half][hc])
                        for c in range(2):
                            co = half * 1024 + c * 512
                            i = t.matmul(pg[0:8, c * 512:(c + 1) * 512],
                                         wg[:, hc * 8:hc * 8 + 8],
                                         xn[hc][:, co:co + 512],
                                         start=(hc == 0), stop=(hc == NHC - 1))
                pe.inc(i)
            g_pe = pe.n

            @block.scalar
            def _(s):
                s.wait_ge(pe.h, g_pe)
                for half in range(2):
                    i = s.activation(lg_sb[:, half * 1024:(half + 1) * 1024],
                                     PG[half][0:8, :], AF.Identity, bias=bgc[:])
                act.inc(i)
            lg_act = act.n

            @block.tensor
            def _(t):
                t.wait_ge(act.h, lg_act)
                for tt in range(16):
                    i = t.transpose(PG[2][:, tt * 8:tt * 8 + 8],
                                    lg_sb[:, tt * 128:(tt + 1) * 128],
                                    ident[0:8, 0:8])
                pe.inc(i)
            tr_pe = pe.n

            @block.vector
            def _(v):
                v.wait_ge(pe.h, tr_pe)
                for tt in range(16):
                    vstep(v, v.tensor_copy(ohstage[:], PG[2][:, tt * 8:tt * 8 + 8]))
                    vstep(v, v.max(mx8[:], ohstage[:]))
                    vstep(v, v.max_index(idxu[:], mx8[:], ohstage[:]))
                    vstep(v, v.tensor_copy(idxf[:], idxu[:, 0:1]))
                    vstep(v, v.tensor_copy(idxs[:, tt:tt + 1], idxf[:]))
                    vstep(v, v.tensor_scalar(ohf[:, tt * 8:tt * 8 + 8], iota8[:],
                                             idxf[:], None, ALU.is_equal))
                    i = v.tensor_copy(ohb[:, tt * 8:tt * 8 + 8],
                                      ohf[:, tt * 8:tt * 8 + 8])
                    if tt < 15:
                        vstep(v, i)
                dve.inc(i)
            oh_dve = dve.n

            pe_base = pe.n
            act_base = act.n

            @block.tensor
            def _(t):
                t.wait_ge(dve.h, oh_dve)
                for tt in range(16):
                    if tt >= 2:
                        t.wait_ge(act.h, act_base + tt - 1)
                    i = t.transpose(PG[3][0:8, (tt % 2) * 512:(tt % 2) * 512 + 128],
                                    ohf[:, tt * 8:tt * 8 + 8], ident[:])
                    pe.inc(i)

            @block.scalar
            def _(s):
                for tt in range(16):
                    s.wait_ge(pe.h, pe_base + tt + 1)
                    i = s.copy(onehotT[:, tt * 128:(tt + 1) * 128],
                               PG[3][0:8, (tt % 2) * 512:(tt % 2) * 512 + 128])
                    act.inc(i)
            return {"oh_act": act.n, "oh_dve": oh_dve, "lg_act": lg_act}

        def emit_mask(e_idx, slot, oh_act, war_act):
            @block.tensor
            def _(t):
                t.wait_ge(act.h, max(oh_act, war_act))
                for half in range(2):
                    for c in range(2):
                        co = half * 1024 + c * 512
                        i = t.matmul(PG[half][:, c * 512:(c + 1) * 512],
                                     esel[:, e_idx * 128:(e_idx + 1) * 128],
                                     onehotT[:, co:co + 512],
                                     start=True, stop=True)
                pe.inc(i)
            mpe = pe.n

            @block.scalar
            def _(s):
                s.wait_ge(pe.h, mpe)
                for half in range(2):
                    i = s.copy(mask_sl[slot][:, half * 1024:(half + 1) * 1024],
                               PG[half][:])
                act.inc(i)
            return act.n

        def emit_ffn(x_ready, w_marks, rearm):
            xn_marks = emit_ln(x_ready)
            oh_act = emit_gate(xn_marks)["oh_act"]
            mask_acts = {0: emit_mask(0, 0, oh_act, war_act=oh_act)}
            last_ps1_act = mask_acts[0]
            prev_gelu = [0, 0]
            prev_acc = [0, 0]
            for e in range(E):
                abase = act.n
                g_mark = lambda tb, ic: abase + tb * 8 + ic + 1
                has_mask = e + 1 < E
                dbase = dve.n
                mk_mark = lambda tb, ic: dbase + tb * 16 + ic + 1
                ac_mark = lambda tb, hc: dbase + tb * 16 + 8 + hc + 1

                mm1_marks = [[0] * 8, [0] * 8]
                mm2_marks = [[0] * 8, [0] * 8]
                mask_mpe = 0
                for tb in range(2):
                    for ic in range(8):
                        pg = PG[ic % 2]

                        @block.tensor
                        def _(t, e=e, tb=tb, ic=ic, pg=pg,
                              lpa=last_ps1_act, pga=prev_gelu[ic % 2]):
                            if tb == 0 and ic == 0:
                                t.wait_ge(*w_marks[("w1", e)])
                            if tb == 0 and ic < 2:
                                t.wait_ge(act.h, max(lpa, pga))
                            else:
                                t.wait_ge(act.h, g_mark(tb, ic - 2) if ic >= 2
                                          else g_mark(tb - 1, ic + 6))
                            for hc in range(NHC):
                                if e == 0:
                                    t.wait_ge(dve.h, xn_marks[tb][hc])
                                for c in range(2):
                                    i = t.matmul(
                                        pg[:, c * 512:(c + 1) * 512],
                                        w1s[:, hc * 1024 + ic * 128:
                                            hc * 1024 + (ic + 1) * 128],
                                        xn[hc][:, tb * 1024 + c * 512:
                                               tb * 1024 + (c + 1) * 512],
                                        start=(hc == 0), stop=(hc == NHC - 1))
                            pe.inc(i)
                        mm1_marks[tb][ic] = pe.n
                    if tb == 1:
                        w1_free_pe = pe.n

                    if tb == 1 and has_mask:
                        @block.tensor
                        def _(t, e=e, oa=oh_act):
                            t.wait_ge(act.h, max(oa, g_mark(1, 7)))
                            for half in range(2):
                                for c in range(2):
                                    co = half * 1024 + c * 512
                                    i = t.matmul(
                                        PG[half][:, c * 512:(c + 1) * 512],
                                        esel[:, (e + 1) * 128:(e + 2) * 128],
                                        onehotT[:, co:co + 512],
                                        start=True, stop=True)
                            pe.inc(i)
                        mask_mpe = pe.n

                    for hc in range(NHC):
                        pg2 = PG[2 + hc % 2]

                        @block.tensor
                        def _(t, e=e, tb=tb, hc=hc, pg2=pg2,
                              pac=prev_acc[hc % 2]):
                            if tb == 0 and hc == 0:
                                t.wait_ge(*w_marks[("w2", e)])
                            t.wait_ge(dve.h, mk_mark(tb, 7))
                            if hc >= 2:
                                t.wait_ge(dve.h, ac_mark(tb, hc - 2))
                            elif tb == 1:
                                t.wait_ge(dve.h, ac_mark(0, hc + 6))
                            else:
                                t.wait_ge(dve.h, pac)
                            first = True
                            if e == 0:
                                for c in range(2):
                                    t.matmul(pg2[:, c * 512:(c + 1) * 512],
                                             b2bf[:, hc * 128:(hc + 1) * 128],
                                             onehotT[:, tb * 1024 + c * 512:
                                                     tb * 1024 + (c + 1) * 512],
                                             start=True, stop=False)
                                first = False
                            for ic in range(8):
                                for c in range(2):
                                    i = t.matmul(
                                        pg2[:, c * 512:(c + 1) * 512],
                                        w2s[:, ic * 1024 + hc * 128:
                                            ic * 1024 + (hc + 1) * 128],
                                        h1t(ic)[:, c * 512:(c + 1) * 512],
                                        start=(first and ic == 0),
                                        stop=(ic == 7))
                            pe.inc(i)
                        mm2_marks[tb][hc] = pe.n

                @block.scalar
                def _(s, e=e, m1=mm1_marks, mpe=mask_mpe, hm=has_mask):
                    for tb in range(2):
                        for ic in range(8):
                            s.wait_ge(pe.h, m1[tb][ic])
                            act.inc(s.activation(
                                h1t(ic), PG[ic % 2][:], AF.Gelu,
                                bias=b1c[:, e * 8 + ic:e * 8 + ic + 1]))
                    if hm:
                        s.wait_ge(pe.h, mpe)
                        for half in range(2):
                            act.inc(s.copy(
                                mask_sl[(e + 1) % 2][:, half * 1024:
                                                     (half + 1) * 1024],
                                PG[half][:]))
                assert act.n == abase + 16 + (2 if has_mask else 0)
                if has_mask:
                    mask_acts[e + 1] = act.n
                    last_ps1_act = act.n
                prev_gelu = [g_mark(1, 6), g_mark(1, 7)]

                @block.vector
                def _(v, e=e, ma=mask_acts[e], m2=mm2_marks):
                    for tb in range(2):
                        if tb == 0:
                            v.wait_ge(act.h, ma)
                        for ic in range(8):
                            v.wait_ge(act.h, g_mark(tb, ic))
                            dve.inc(v.tensor_mul(
                                h1t(ic), h1t(ic),
                                mask_sl[e % 2][:, tb * 1024:(tb + 1) * 1024]))
                        for hc in range(NHC):
                            v.wait_ge(pe.h, m2[tb][hc])
                            dve.inc(v.scalar_tensor_tensor(
                                x[hc][:, tb * 1024:(tb + 1) * 1024],
                                PG[2 + hc % 2][:], 0.5,
                                x[hc][:, tb * 1024:(tb + 1) * 1024],
                                ALU.mult, ALU.add))
                assert dve.n == dbase + 32
                prev_acc = [ac_mark(1, 6), ac_mark(1, 7)]
                if e == E - 1:
                    final_acc = [ac_mark(1, hc) for hc in range(NHC)]

                if e + 1 < E:
                    @block.gpsimd
                    def _(g, e=e, wf=w1_free_pe, wf2=pe.n):
                        g.wait_ge(pe.h, wf)
                        w_marks[("w1", e + 1)] = (w1m.h, w1m.inc(
                            g.dma_start(w_dst(w1s), w_src(d_w1[e + 1])), 16))
                        g.wait_ge(pe.h, wf2)
                        w_marks[("w2", e + 1)] = (w2m.h, w2m.inc(
                            g.dma_start(w_dst(w2s), w_src(d_w2[e + 1])), 16))
            if rearm is not None:
                d1, d2 = rearm

                @block.gpsimd
                def _(g, wf=pe.n):
                    g.wait_ge(pe.h, wf)
                    w_marks[("w1", 0)] = (w1m.h, w1m.inc(
                        g.dma_start(w_dst(w1s), w_src(d1)), 16))
                    w_marks[("w2", 0)] = (w2m.h, w2m.inc(
                        g.dma_start(w_dst(w2s), w_src(d2)), 16))
            return [(dve.h, m) for m in final_acc]

        def emit_attn(x_ready, w_marks):
            astage = _CFG.get("astage", 4)
            xn_marks = emit_ln(x_ready)
            xn_all = xn_marks[1][NHC - 1]

            v_acts = []
            for tt in range(16):
                pg = PG[tt % 2]

                @block.tensor
                def _(t, tt=tt, pg=pg):
                    if tt == 0:
                        t.wait_ge(*w_marks["wv"])
                    if tt >= 2:
                        t.wait_ge(act.h, v_acts[tt - 2])
                    half = tt // 8
                    for hc in range(NHC):
                        if tt % 8 == 0:
                            t.wait_ge(dve.h, xn_marks[half][hc])
                        for c in range(2):
                            i = t.matmul(
                                pg[:, c * 512:(c + 1) * 512],
                                xn[hc][:, tt * 128:(tt + 1) * 128],
                                w1s[:, hc * 1024 + c * 512:
                                    hc * 1024 + (c + 1) * 512],
                                start=(hc == 0), stop=(hc == NHC - 1))
                    pe.inc(i)
                v_pe = pe.n

                @block.scalar
                def _(s, tt=tt, pg=pg, vp=v_pe):
                    s.wait_ge(pe.h, vp)
                    act.inc(s.copy(vtile(tt), pg[:]))
                v_acts.append(act.n)
            v_done_pe = pe.n
            if dbg:
                @block.sync
                def _(sy, va=v_acts[-1]):
                    sy.wait_ge(act.h, va)
                    st.inc(sy.dma_start(d_dbg_v, vat[:]), 16)
            if astage <= 1:
                return [(dve.h, xn_all)] * NHC

            @block.gpsimd
            def _(g, wf=v_done_pe):
                g.wait_ge(pe.h, wf)
                w_marks["wq"] = (w1m.h, w1m.inc(
                    g.dma_start(w_dst(w1s), w_src(d_wq)), 16))

            qk_marks = {}
            evac_marks = {}
            kvext_marks = {}
            proj_marks = {-1: 0}

            def emit_qk(oc, wslot, wmark_key, pg01_free, hcs=(0, NHC)):
                @block.tensor
                def _(t, oc=oc, pg01_free=pg01_free, hcs=hcs):
                    if oc == 0 and hcs[0] == 0:
                        t.wait_ge(*w_marks[wmark_key])
                    if hcs[0] == 0:
                        t.wait_ge(act.h, pg01_free)
                    for hc in range(hcs[0], hcs[1]):
                        if oc == 0:
                            t.wait_ge(dve.h, xn_marks[1][hc])
                        for half in range(2):
                            for c in range(2):
                                i = t.matmul(
                                    PG[half][:, c * 512:(c + 1) * 512],
                                    wslot[:, hc * 1024 + oc * 128:
                                          hc * 1024 + (oc + 1) * 128],
                                    xn[hc][:, half * 1024 + c * 512:
                                           half * 1024 + (c + 1) * 512],
                                    start=(hc == 0), stop=(hc == NHC - 1))
                    pe.inc(i)
                if hcs[1] == NHC:
                    qk_marks[oc] = pe.n

            emit_qk(0, w2s, "wk", v_acts[-1])
            prev_tail = v_acts[-1]
            for oc in range(8):
                abase = act.n
                evac_mark = abase + 4
                rows_mark = abase + 8
                kp_mark = [abase + 9, abase + 12]
                kvx_mark = [abase + 11, abase + 14]

                @block.tensor
                def _(t, ea=evac_mark, kv_free=prev_tail):
                    t.wait_ge(act.h, ea)
                    if kv_free:
                        t.wait_ge(act.h, kv_free)
                    for half in range(2):
                        pgn = PG[2] if half == 0 else PG[3]
                        for c in range(2):
                            i = t.matmul(pgn[0:2, c * 512:(c + 1) * 512],
                                         bdiag[:],
                                         sq_sb[:, half * 1024 + c * 512:
                                               half * 1024 + (c + 1) * 512],
                                         start=True, stop=True)
                    pe.inc(i)
                nrm_pe = pe.n

                if oc + 1 < 8:
                    emit_qk(oc + 1, w2s, "wk", evac_mark, hcs=(0, 4))
                elif astage > 2:
                    @block.gpsimd
                    def _(g, wf=qk_marks[7]):
                        g.wait_ge(pe.h, wf)
                        w_marks["wo"] = (w2m.h, w2m.inc(
                            g.dma_start(w_dst(w2s), w_src(d_wo)), 16))

                proj_pe = [0, 0]
                kv_pe = [0, 0]
                for hh in range(2):
                    @block.tensor
                    def _(t, hh=hh, na=rows_mark,
                          pea=(kvx_mark[0] if hh == 1 else 0)):
                        t.wait_ge(act.h, na)
                        if pea:
                            t.wait_ge(act.h, pea)
                        for tt in range(16):
                            v_ = PG[2][:, tt * 64:(tt + 1) * 64]
                            t.matmul(v_,
                                     kq_sb[hh * 64:(hh + 1) * 64,
                                           tt * 128:(tt + 1) * 128],
                                     rf_s[hh * 64:(hh + 1) * 64, :],
                                     start=True, stop=False)
                            i = t.matmul(v_,
                                         nr_bf[0:2, tt * 128:(tt + 1) * 128],
                                         negsel[0:2, hh * 64:(hh + 1) * 64],
                                         start=False, stop=True)
                        pe.inc(i)
                    proj_pe[hh] = pe.n
                    if hh == 1:
                        proj_marks[oc] = pe.n

                    @block.tensor
                    def _(t, oc=oc, hh=hh, ka=kp_mark[hh]):
                        t.wait_ge(act.h, ka)
                        h_ = 2 * oc + hh
                        base = hh * 512
                        for tt in range(16):
                            t.matmul(PG[3][0:64, base:base + 64],
                                     kp_sb[:, tt * 64:(tt + 1) * 64],
                                     vtile(tt)[:, h_ * 64:h_ * 64 + 64],
                                     start=(tt == 0), stop=(tt == 15))
                            i = t.matmul(PG[2][0:64, 960 + hh:960 + hh + 1],
                                         kp_sb[:, tt * 64:(tt + 1) * 64],
                                         ones_col[:],
                                         start=(tt == 0), stop=(tt == 15))
                        pe.inc(i)
                    kv_pe[hh] = pe.n
                    if hh == 0 and oc + 1 < 8:
                        emit_qk(oc + 1, w2s, "wk", 0, hcs=(4, NHC))

                @block.scalar
                def _(s, oc=oc, qp=qk_marks[oc], pp=proj_marks[oc - 1],
                      nrm=nrm_pe, ppe=tuple(proj_pe), kpe=tuple(kv_pe)):
                    s.wait_ge(pe.h, max(qp, pp))
                    act.inc(s.copy(kq_sb[:, 0:1024], PG[0][:]))
                    act.inc(s.copy(kq_sb[:, 1024:2048], PG[1][:]))
                    act.inc(s.activation(sq_sb[:, 0:1024], PG[0][:], AF.Square))
                    act.inc(s.activation(sq_sb[:, 1024:2048], PG[1][:],
                                         AF.Square))
                    s.wait_ge(pe.h, nrm)
                    for half in range(2):
                        pgn = PG[2] if half == 0 else PG[3]
                        act.inc(s.activation(pair_f, pgn[0:2, :], AF.Ln))
                        s.wait_ge(act.h, act.n)
                        act.inc(s.activation(
                            nr_bf[0:2, half * 1024:(half + 1) * 1024],
                            pair_f, AF.Exp, scale=0.5))
                        if half == 0:
                            s.wait_ge(act.h, act.n)
                    for hh in range(2):
                        s.wait_ge(pe.h, ppe[hh])
                        act.inc(s.activation(kp_sb, PG[2][:], AF.Exp))
                        s.wait_ge(pe.h, kpe[hh])
                        base = hh * 512
                        act.inc(s.copy(kvext[hh * 64:(hh + 1) * 64,
                                             oc * 65:oc * 65 + 64],
                                       PG[3][0:64, base:base + 64]))
                        act.inc(s.copy(kvext[hh * 64:(hh + 1) * 64,
                                             oc * 65 + 64:oc * 65 + 65],
                                       PG[2][0:64, 960 + hh:960 + hh + 1]))
                assert act.n == abase + 14
                prev_tail = kvx_mark[1]
                kvext_marks[oc] = kvx_mark[1]
            kvext_marks[7] = prev_tail

            if dbg:
                @block.sync
                def _(sy, ka=kvext_marks[7]):
                    sy.wait_ge(act.h, ka)
                    st.inc(sy.dma_start(d_dbg_kv, kvext[:]), 16)
            if astage <= 2:
                return [(dve.h, xn_all)] * NHC

            out_free = proj_marks[7]
            ar_prev = kvext_marks[7]
            sp_prev = 0
            qk_marks = {}
            emit_qk(0, w1s, "wq", evac_mark)
            for oc in range(8):
                abase = act.n
                evac_mark = abase + 4
                rows_mark = abase + 8
                eq_mark = abase + 10
                ar_mark = [abase + 14, abase + 18]

                @block.tensor
                def _(t, ea=evac_mark, pg23_free=ar_prev):
                    t.wait_ge(act.h, ea)
                    t.wait_ge(act.h, pg23_free)
                    for half in range(2):
                        pgn = PG[2] if half == 0 else PG[3]
                        for c in range(2):
                            i = t.matmul(pgn[0:2, c * 512:(c + 1) * 512],
                                         bdiag[:],
                                         sq_sb[:, half * 1024 + c * 512:
                                               half * 1024 + (c + 1) * 512],
                                         start=True, stop=True)
                    pe.inc(i)
                nrm_pe = pe.n

                if oc + 1 < 8:
                    emit_qk(oc + 1, w1s, "wq", evac_mark, hcs=(0, 4))

                @block.tensor
                def _(t, na=rows_mark):
                    t.wait_ge(act.h, na)
                    for hh in range(2):
                        for half in range(2):
                            pgn = PG[2] if half == 0 else PG[3]
                            for c in range(2):
                                v_ = pgn[hh * 64:(hh + 1) * 64,
                                         c * 512:(c + 1) * 512]
                                t.matmul(v_, rf_s[hh * 64:(hh + 1) * 64, :],
                                         kq_sb[hh * 64:(hh + 1) * 64,
                                               half * 1024 + c * 512:
                                               half * 1024 + (c + 1) * 512],
                                         start=True, stop=False)
                                i = t.matmul(
                                    v_, negsel[0:2, hh * 64:(hh + 1) * 64],
                                    nr_bf[0:2, half * 1024 + c * 512:
                                          half * 1024 + (c + 1) * 512],
                                    start=False, stop=True)
                    pe.inc(i)
                proj_pe = pe.n

                out_pe = [0, 0]
                for hh in range(2):
                    @block.tensor
                    def _(t, oc=oc, hh=hh,
                          ea=(eq_mark if hh == 0 else ar_mark[0])):
                        t.wait_ge(act.h, ea)
                        for half in range(2):
                            pgo = PG[2] if half == 0 else PG[3]
                            for c in range(2):
                                i = t.matmul(
                                    pgo[0:65, c * 512:(c + 1) * 512],
                                    kvext[hh * 64:(hh + 1) * 64,
                                          oc * 65:(oc + 1) * 65],
                                    eq_sb[hh * 64:(hh + 1) * 64,
                                          half * 1024 + c * 512:
                                          half * 1024 + (c + 1) * 512],
                                    start=True, stop=True)
                        pe.inc(i)
                    out_pe[hh] = pe.n
                    if hh == 0 and oc + 1 < 8:
                        emit_qk(oc + 1, w1s, "wq", 0, hcs=(4, NHC))

                @block.scalar
                def _(s, oc=oc, qp=qk_marks[oc], pp=out_free, nrm=nrm_pe,
                      prj=proj_pe, ope=tuple(out_pe), spm=sp_prev, spb=sp.n):
                    s.wait_ge(pe.h, max(qp, pp))
                    act.inc(s.copy(kq_sb[:, 0:1024], PG[0][:]))
                    act.inc(s.copy(kq_sb[:, 1024:2048], PG[1][:]))
                    act.inc(s.activation(sq_sb[:, 0:1024], PG[0][:], AF.Square))
                    act.inc(s.activation(sq_sb[:, 1024:2048], PG[1][:],
                                         AF.Square))
                    s.wait_ge(pe.h, nrm)
                    for half in range(2):
                        pgn = PG[2] if half == 0 else PG[3]
                        act.inc(s.activation(pair_f, pgn[0:2, :], AF.Ln))
                        s.wait_ge(act.h, act.n)
                        act.inc(s.activation(
                            nr_bf[0:2, half * 1024:(half + 1) * 1024],
                            pair_f, AF.Exp, scale=0.5))
                        if half == 0:
                            s.wait_ge(act.h, act.n)
                    s.wait_ge(pe.h, prj)
                    act.inc(s.activation(eq_sb[:, 0:1024], PG[2][:], AF.Exp))
                    act.inc(s.activation(eq_sb[:, 1024:2048], PG[3][:],
                                         AF.Exp))
                    for hh in range(2):
                        s.wait_ge(pe.h, ope[hh])
                        s.wait_ge(sp.h, spm if hh == 0 else spb + 32)
                        for half in range(2):
                            pgo = PG[2] if half == 0 else PG[3]
                            act.inc(s.copy(
                                araw(oc)[hh * 64:(hh + 1) * 64,
                                         half * 1024:(half + 1) * 1024],
                                pgo[0:64, :]))
                            act.inc(s.copy(tmp_r[half][0:1, :],
                                           pgo[64:65, :]))
                assert act.n == abase + 18
                out_free = out_pe[1]
                ar_prev = ar_mark[1]

                for hh in range(2):
                    @block.sync
                    def _(sy, oc=oc, hh=hh, ea=ar_mark[hh]):
                        sy.wait_ge(act.h, ea)
                        row = 2 * oc + hh
                        sp.inc(sy.dma_start(
                            nrm16[row:row + 1, 0:1024], tmp_r[0][:]), 16)
                        sp.inc(sy.dma_start(
                            nrm16[row:row + 1, 1024:2048], tmp_r[1][:]), 16)
                sp_prev = sp.n
            prev_q_evac = ar_prev
            q_evac = {7: act.n}

            if dbg:
                @block.sync
                def _(sy, qa=prev_q_evac, spn=sp.n):
                    sy.wait_ge(act.h, qa)
                    sy.wait_ge(sp.h, spn)
                    st.inc(sy.dma_start(d_dbg_ar, vat[:]), 16)
                    st.inc(sy.dma_start(d_dbg_nr, nrm16[:]), 16)
            if astage <= 3:
                return [(dve.h, xn_all)] * NHC

            @block.vector
            def _(v, spn=sp.n):
                v.wait_ge(sp.h, spn)
                vstep(v, v.tensor_scalar_add(nrm16[:], nrm16[:], 1e-6))
                with nc.allow_low_precision(
                        reason="1/(nrm+1e-6): fp32 DVE internals, bf16 store "
                               "at matmul-operand precision"):
                    dve.inc(v.reciprocal(nrm16[:], nrm16[:]))
            al_dve = dve.n

            sc_dves = []
            for oc in range(8):
                pg_sc = (PG[2], PG[3]) if oc % 2 == 0 else (PG[0], PG[1])

                @block.tensor
                def _(t, oc=oc, ad=al_dve, pqe=prev_q_evac, pg_sc=pg_sc,
                      qe7=q_evac[7]):
                    if oc == 0:
                        t.wait_ge(dve.h, ad)
                        t.wait_ge(act.h, pqe)
                    if oc == 1:
                        t.wait_ge(act.h, qe7)
                    if oc >= 2:
                        t.wait_ge(dve.h, sc_dves[oc - 2])
                    for half in range(2):
                        pgn = pg_sc[half]
                        for c in range(2):
                            i = t.matmul(pgn[:, c * 512:(c + 1) * 512],
                                         hsel[:, oc * 128:(oc + 1) * 128],
                                         nrm16[:, half * 1024 + c * 512:
                                               half * 1024 + (c + 1) * 512],
                                         start=True, stop=True)
                    pe.inc(i)
                hs_pe = pe.n

                @block.vector
                def _(v, oc=oc, hp=hs_pe, pg_sc=pg_sc):
                    v.wait_ge(pe.h, hp)
                    for half in range(2):
                        pgn = pg_sc[half]
                        i = v.tensor_mul(
                            araw(oc)[:, half * 1024:(half + 1) * 1024],
                            araw(oc)[:, half * 1024:(half + 1) * 1024],
                            pgn[:])
                    dve.inc(i)
                sc_dves.append(dve.n)

            wo_marks = []
            for oco in range(8):
                pg_pair = (PG[0], PG[1]) if oco % 2 == 0 else (PG[2], PG[3])

                @block.tensor
                def _(t, oco=oco, pg_pair=pg_pair):
                    if oco == 0:
                        t.wait_ge(*w_marks["wo"])
                        t.wait_ge(dve.h, sc_dves[-1])
                    if oco >= 2:
                        t.wait_ge(dve.h, wo_marks[oco - 2])
                    for hc in range(NHC):
                        for half in range(2):
                            pgn = pg_pair[half]
                            for c in range(2):
                                i = t.matmul(
                                    pgn[:, c * 512:(c + 1) * 512],
                                    w2s[:, hc * 1024 + oco * 128:
                                        hc * 1024 + (oco + 1) * 128],
                                    araw(hc)[:, half * 1024 + c * 512:
                                             half * 1024 + (c + 1) * 512],
                                    start=(hc == 0), stop=(hc == NHC - 1))
                    pe.inc(i)
                wo_pe = pe.n

                @block.vector
                def _(v, oco=oco, wp=wo_pe, pg_pair=pg_pair):
                    v.wait_ge(pe.h, wp)
                    for half in range(2):
                        pgn = pg_pair[half]
                        i = v.scalar_tensor_tensor(
                            x[oco][:, half * 1024:(half + 1) * 1024],
                            pgn[:], boc[:, oco:oco + 1],
                            x[oco][:, half * 1024:(half + 1) * 1024],
                            ALU.add, ALU.add)
                    dve.inc(i)
                wo_marks.append(dve.n)
            return [(dve.h, wo_marks[hc]) for hc in range(NHC)]

        def emit_ffn_sparse(x_ready, w_marks, rearm):
            xn_marks = emit_ln(x_ready)
            g = emit_gate(xn_marks)
            oh_act, oh_dve = g["oh_act"], g["oh_dve"]

            @block.tensor
            def _(t, oa=oh_act, od=oh_dve):
                t.wait_ge(act.h, oa)
                t.wait_ge(dve.h, od)
                for tc in range(16):
                    reg = PG[tc // 8][0:8, (tc % 8) * 128:(tc % 8) * 128 + 128]
                    for kc in range(tc + 1):
                        i = t.matmul(reg, ohb[:, kc * 8:kc * 8 + 8],
                                     ltst[:] if kc == tc else ones2d[:],
                                     start=(kc == 0), stop=(kc == tc))
                    pe.inc(i)
            p_pe = pe.n

            @block.scalar
            def _(s, pp=p_pe):
                s.wait_ge(pe.h, pp)
                act.inc(s.copy(p_sb[:, 0:1024], PG[0][0:8, :]))
                act.inc(s.copy(p_sb[:, 1024:2048], PG[1][0:8, :]))
            p_act = act.n

            @block.tensor
            def _(t, pa=p_act):
                t.wait_ge(act.h, pa)
                for tt in range(16):
                    i = t.transpose(PG[3][:, 128 + tt * 8:128 + tt * 8 + 8],
                                    p_sb[:, tt * 128:(tt + 1) * 128],
                                    ident[0:8, 0:8])
                    pe.inc(i)
            ptr_pe = pe.n

            slot_marks = []

            @block.vector
            def _(v, pp=ptr_pe):
                v.wait_ge(pe.h, pp)
                for tt in range(16):
                    vstep(v, v.tensor_mul(ptmp[:],
                                          PG[3][:, 128 + tt * 8:128 + tt * 8 + 8],
                                          ohf[:, tt * 8:tt * 8 + 8]))
                    vstep(v, v.reduce_sum(rank_c[:], ptmp[:],
                                          axis=mybir.AxisListType.X))
                    vstep(v, v.scalar_tensor_tensor(
                        slot_f[:], idxs[:, tt:tt + 1], 512.0, rank_c[:],
                        ALU.mult, ALU.add))
                    i = v.tensor_copy(slotu[:, tt:tt + 1], slot_f[:])
                    if tt < 15:
                        vstep(v, i)
                    slot_marks.append(dve.n + (1 if tt == 15 else 0))
                dve.inc(i)
            slot_dve = dve.n

            @block.sync
            def _(sy):
                for hc in range(NHC):
                    sy.wait_ge(dve.h, xn_marks[1][hc])
                    xd.inc(sy.dma_start(d_xnd[hc * 128:(hc + 1) * 128, :],
                                        xn[hc][:]), 16)
            xd_full = xd.n

            sct_done = []
            for tt in range(16):
                par = tt % 2

                @block.sync
                def _(sy, tt=tt, par=par, xdf=xd_full):
                    if tt == 0:
                        sy.wait_ge(xd.h, xdf)
                    if tt >= 2:
                        sy.wait_ge(sct[par].h, sct_done[tt - 2])
                    xtl[par].inc(sy.dma_start(
                        xnt_b[par][:],
                        d_xnd[:, tt * 128:(tt + 1) * 128], transpose=True), 16)

                @block.gpsimd
                def _(g_, tt=tt, par=par, sd=slot_marks[tt], xt=xtl[par].n):
                    if tt == 0:
                        g_.wait_ge(zs.h, zs_full)
                    g_.wait_ge(dve.h, sd)
                    g_.wait_ge(xtl[par].h, xt)
                    sct[par].inc(g_.indirect_dma_start(
                        d_xs, IndirectOffsetOnAxis(slotu[:, tt:tt + 1], 0),
                        xnt_b[par][:], None), 16)
                sct_done.append(sct[par].n)
            sct_full = (sct[0].n, sct[1].n)

            gelu_prev = [0, 0]
            yd_prev = (0, 0)
            xg_marks = []
            for e in range(E):
                @block.sync
                def _(sy, e=e, sf=sct_full):
                    if e == 0:
                        sy.wait_ge(sct[0].h, sf[0])
                        sy.wait_ge(sct[1].h, sf[1])
                    if e >= 2:
                        sy.wait_ge(pe.h, xg_marks[e - 2][-1])
                    for hc in range(NHC):
                        xgl[e % 2].inc(sy.dma_start(
                            xgt(e % 2, hc),
                            d_xs[e * 512:(e + 1) * 512,
                                 hc * 128:(hc + 1) * 128], transpose=True), 16)
                xg_full = xgl[e % 2].n

                abase = act.n
                g_mark = lambda ic: abase + ic + 1
                mm1_marks = []
                for ic in range(8):
                    @block.tensor
                    def _(t, e=e, ic=ic, xf=xg_full,
                          pga=gelu_prev[ic % 2], pa=p_act):
                        if ic == 0:
                            t.wait_ge(*w_marks[("w1", e)])
                            t.wait_ge(xgl[e % 2].h, xf)
                        if ic >= 2:
                            t.wait_ge(act.h, g_mark(ic - 2))
                        else:
                            t.wait_ge(act.h, max(pga, pa))
                        for hc in range(NHC):
                            i = t.matmul(
                                PG[ic % 2][:, 0:512],
                                w1s[:, hc * 1024 + ic * 128:
                                    hc * 1024 + (ic + 1) * 128],
                                xgt(e % 2, hc),
                                start=(hc == 0), stop=(hc == NHC - 1))
                        pe.inc(i)
                    mm1_marks.append(pe.n)
                xg_marks.append(tuple(mm1_marks))

                mm2_marks = []
                for sc in range(4):
                    pg2 = PG[2 + sc % 2]

                    @block.tensor
                    def _(t, e=e, sc=sc, pg2=pg2, sd=slot_dve):
                        t.wait_ge(*w_marks[("w2", e)]) if sc == 0 else None
                        t.wait_ge(act.h, abase + 8)
                        if sc >= 2:
                            t.wait_ge(act.h, abase + 8 + (sc - 2) + 1)
                        elif e > 0:
                            t.wait_ge(act.h, abase - 12 + 8 + (sc + 2) + 1)
                        else:
                            t.wait_ge(dve.h, sd)
                        for c in range(2):
                            t.matmul(pg2[:, c * 512:(c + 1) * 512],
                                     esel[:, e * 128:(e + 1) * 128],
                                     b2bf[:, c * 512:(c + 1) * 512],
                                     start=True, stop=False)
                        for ic in range(8):
                            for c in range(2):
                                i = t.matmul(
                                    pg2[:, c * 512:(c + 1) * 512],
                                    h1s(ic)[:, sc * 128:(sc + 1) * 128],
                                    w2s[:, ic * 1024 + c * 512:
                                        ic * 1024 + (c + 1) * 512],
                                    start=False, stop=(ic == 7))
                            pass
                        pe.inc(i)
                    mm2_marks.append(pe.n)

                yd_base = (ysd[0].n, ysd[1].n)

                @block.scalar
                def _(s, e=e, m1=tuple(mm1_marks), m2=tuple(mm2_marks),
                      ydp=yd_prev, ydb=yd_base):
                    for ic in range(8):
                        s.wait_ge(pe.h, m1[ic])
                        act.inc(s.activation(
                            h1s(ic), PG[ic % 2][:, 0:512], AF.Gelu,
                            bias=b1c[:, e * 8 + ic:e * 8 + ic + 1]))
                    for sc in range(4):
                        s.wait_ge(pe.h, m2[sc])
                        if sc < 2:
                            s.wait_ge(ysd[sc % 2].h, ydp[sc % 2])
                        else:
                            s.wait_ge(ysd[sc % 2].h, ydb[sc % 2] + 16)
                        act.inc(s.copy(yt_b[sc % 2][:], PG[2 + sc % 2][:]))
                assert act.n == abase + 12
                gelu_prev = [abase + 7, abase + 8]

                @block.sync
                def _(sy, e=e, ab=abase):
                    for sc in range(4):
                        sy.wait_ge(act.h, ab + 8 + sc + 1)
                        ysd[sc % 2].inc(sy.dma_start(
                            d_ys[e * 512 + sc * 128:e * 512 + (sc + 1) * 128, :],
                            yt_b[sc % 2][:]), 16)
                yd_prev = (ysd[0].n, ysd[1].n)

                if e + 1 < E:
                    @block.gpsimd
                    def _(g_, e=e, wf=mm1_marks[-1], wf2=pe.n):
                        g_.wait_ge(pe.h, wf)
                        w_marks[("w1", e + 1)] = (w1m.h, w1m.inc(
                            g_.dma_start(w_dst(w1s), w_src(d_w1[e + 1])), 16))
                        g_.wait_ge(pe.h, wf2)
                        w_marks[("w2", e + 1)] = (w2m.h, w2m.inc(
                            g_.dma_start(w_dst(w2s), w_src(d_w2[e + 1])), 16))
            ys_full = (ysd[0].n, ysd[1].n)

            cast_acts = []
            tr_pes = []
            acc_dves = []
            for tt in range(16):
                par = tt % 2

                @block.gpsimd
                def _(g_, tt=tt, par=par, yf=ys_full):
                    if tt == 0:
                        g_.wait_ge(ysd[0].h, yf[0])
                        g_.wait_ge(ysd[1].h, yf[1])
                    if tt >= 2:
                        g_.wait_ge(act.h, cast_acts[tt - 2])
                    ygt[par].inc(g_.indirect_dma_start(
                        xnt_b[par][:], None,
                        d_ys, IndirectOffsetOnAxis(slotu[:, tt:tt + 1], 0)), 16)

                @block.scalar
                def _(s, tt=tt, par=par, yg=ygt[par].n, pp=(tr_pes[tt - 2] if tt >= 2 else 0)):
                    s.wait_ge(ygt[par].h, yg)
                    if pp:
                        s.wait_ge(pe.h, pp)
                    act.inc(s.copy(ytf_b[par][:], xnt_b[par][:]))
                cast_acts.append(act.n)

                @block.tensor
                def _(t, tt=tt, par=par, ca=cast_acts[-1],
                      pd=(acc_dves[tt - 2] if tt >= 2 else 0)):
                    t.wait_ge(act.h, ca)
                    if pd:
                        t.wait_ge(dve.h, pd)
                    for hc in range(NHC):
                        i = t.transpose(PG[par][:, hc * 128:(hc + 1) * 128],
                                        ytf_b[par][:, hc * 128:(hc + 1) * 128],
                                        ident[:])
                        pe.inc(i)
                tr_pes.append(pe.n)

                @block.vector
                def _(v, tt=tt, par=par, tp=tr_pes[-1]):
                    v.wait_ge(pe.h, tp)
                    for hc in range(NHC):
                        dve.inc(v.scalar_tensor_tensor(
                            x[hc][:, tt * 128:(tt + 1) * 128],
                            PG[par][:, hc * 128:(hc + 1) * 128], 0.5,
                            x[hc][:, tt * 128:(tt + 1) * 128],
                            ALU.mult, ALU.add))
                acc_dves.append(dve.n)

            if rearm is not None:
                d1, d2 = rearm

                @block.gpsimd
                def _(g_, wf=pe.n):
                    g_.wait_ge(pe.h, wf)
                    w_marks[("w1", 0)] = (w1m.h, w1m.inc(
                        g_.dma_start(w_dst(w1s), w_src(d1)), 16))
                    w_marks[("w2", 0)] = (w2m.h, w2m.inc(
                        g_.dma_start(w_dst(w2s), w_src(d2)), 16))
            return [(dve.h, acc_dves[-1])] * NHC

        xl_base = xl.n

        @block.sync
        def _(sy):
            for hc in range(NHC):
                xl.inc(sy.dma_start(x[hc][:],
                                    d_x[0, hc * 128:(hc + 1) * 128, :]), 16)
            sy.wait_ge(im.h, init_im)
            for r in range(48):
                for c in range(2):
                    zs.inc(sy.dma_start(
                        d_xs[r * 128:(r + 1) * 128, c * 512:(c + 1) * 512],
                        zt[:]), 16)

        @block.scalar
        def _(s):
            s.wait_ge(im.h, init_im)
            for r in range(48):
                for c in range(2):
                    zs.inc(s.dma_start(
                        d_ys[r * 128:(r + 1) * 128, c * 512:(c + 1) * 512],
                        zt[:]), 16)
        zs_full = zs.n

        w_marks = {}
        first_kind = sub[0] if sub else "f"

        @block.gpsimd
        def _(g):
            if first_kind == "f":
                w_marks[("w1", 0)] = (w1m.h, w1m.inc(
                    g.dma_start(w_dst(w1s), w_src(d_w1[0])), 16))
                w_marks[("w2", 0)] = (w2m.h, w2m.inc(
                    g.dma_start(w_dst(w2s), w_src(d_w2[0])), 16))
            else:
                w_marks["wv"] = (w1m.h, w1m.inc(
                    g.dma_start(w_dst(w1s), w_src(d_wv)), 16))
                w_marks["wk"] = (w2m.h, w2m.inc(
                    g.dma_start(w_dst(w2s), w_src(d_wk)), 16))

        x_ready = [(xl.h, xl.n)] * NHC
        for s_i in range(n_seq):
            for sub_i, kind in enumerate(sub):
                last = (s_i, sub_i) == (n_seq - 1, len(sub) - 1)
                nxt_kind = None
                if not last:
                    nxt_kind = sub[sub_i + 1] if sub_i + 1 < len(sub) else sub[0]
                if kind == "f":
                    rearm = None
                    if nxt_kind == "a":
                        rearm = (d_wv, d_wk)
                    elif nxt_kind == "f":
                        rearm = (d_w1[0], d_w2[0])
                    if _CFG.get("sparse", True):
                        done = emit_ffn_sparse(x_ready, w_marks, rearm=rearm)
                    else:
                        done = emit_ffn(x_ready, w_marks, rearm=rearm)
                    if rearm is not None and nxt_kind == "a":
                        w_marks["wv"] = w_marks[("w1", 0)]
                        w_marks["wk"] = w_marks[("w2", 0)]
                    x_ready = done
                elif kind == "a":
                    done = emit_attn(x_ready, w_marks)
                    x_ready = done
                    if nxt_kind == "f":
                        @block.gpsimd
                        def _(g, wf=pe.n):
                            g.wait_ge(pe.h, wf)
                            w_marks[("w1", 0)] = (w1m.h, w1m.inc(
                                g.dma_start(w_dst(w1s), w_src(d_w1[0])), 16))
                            w_marks[("w2", 0)] = (w2m.h, w2m.inc(
                                g.dma_start(w_dst(w2s), w_src(d_w2[0])), 16))
                    elif nxt_kind == "a":
                        @block.gpsimd
                        def _(g, wf=pe.n):
                            g.wait_ge(pe.h, wf)
                            w_marks["wv"] = (w1m.h, w1m.inc(
                                g.dma_start(w_dst(w1s), w_src(d_wv)), 16))
                            w_marks["wk"] = (w2m.h, w2m.inc(
                                g.dma_start(w_dst(w2s), w_src(d_wk)), 16))

            xl_base2 = xl.n

            @block.sync
            def _(sy, s_i=s_i, xr=x_ready):
                for hc in range(NHC):
                    sem, val = xr[hc]
                    sy.wait_ge(sem, val)
                    st.inc(sy.dma_start(d_y[s_i, hc * 128:(hc + 1) * 128, :],
                                        x[hc][:]), 16)
                if s_i + 1 < n_seq:
                    sy.wait_ge(st.h, st.n)
                    for hc in range(NHC):
                        xl.inc(sy.dma_start(
                            x[hc][:],
                            d_x[s_i + 1, hc * 128:(hc + 1) * 128, :]), 16)
            if s_i + 1 < n_seq:
                x_ready = [(xl.h, xl.n)] * NHC

        @block.sync
        def _(sy):
            sy.wait_ge(st.h, st.n)

    return nc


def kernel(L0_attn_x, L0_attn_w_qkv, L0_attn_w_out, L0_attn_b_out,
           L0_attn_random_features, L0_ffn_Wg, L0_ffn_bg, L0_ffn_W1,
           L0_ffn_b1, L0_ffn_W2, L0_ffn_b2):
    n_seq = _CFG["n_seq"]
    B = L0_attn_x.shape[0]
    n_cores = B // n_seq
    xt = np.ascontiguousarray(np.transpose(np.asarray(L0_attn_x), (0, 2, 1)))

    wqkv = np.asarray(L0_attn_w_qkv)
    wqT = np.ascontiguousarray(wqkv[0:1024].T)
    wkT = np.ascontiguousarray(wqkv[1024:2048].T)
    wvT = np.ascontiguousarray(wqkv[2048:3072].T)
    woT = np.ascontiguousarray(np.asarray(L0_attn_w_out).T)
    rf = np.asarray(L0_attn_random_features)
    rfdup = np.ascontiguousarray(np.tile(rf.T, (2, 1)))
    b1 = np.asarray(L0_ffn_b1)
    b1c = np.ascontiguousarray(
        b1.reshape(E, 8, 128).transpose(2, 0, 1).reshape(128, 64))
    wgc = np.ascontiguousarray(
        np.asarray(L0_ffn_Wg).reshape(8, 128, 8).transpose(1, 0, 2).reshape(128, 64))
    bgc = np.ascontiguousarray(np.asarray(L0_ffn_bg).reshape(8, 1))
    boc = np.ascontiguousarray(np.asarray(L0_attn_b_out).reshape(8, 128).T)
    ident = np.eye(128, dtype=np.float32)
    io8 = np.tile(np.arange(8, dtype=np.float32), (128, 1))
    negsel = np.zeros((2, 128), np.float32)
    negsel[0, 0:64] = -1.0
    negsel[1, 64:128] = -1.0
    hselm = np.zeros((16, H), np.float32)
    for hh in range(16):
        hselm[hh, hh * 64:(hh + 1) * 64] = 1.0
    eselm = np.zeros((8, H), np.float32)
    for e in range(8):
        eselm[e, e * 128:(e + 1) * 128] = 1.0

    shared = {
        "W1": np.ascontiguousarray(np.asarray(L0_ffn_W1), dtype=np.float32),
        "W2": np.ascontiguousarray(np.asarray(L0_ffn_W2), dtype=np.float32),
        "b1c": b1c.astype(np.float32),
        "b2": np.ascontiguousarray(np.asarray(L0_ffn_b2), dtype=np.float32),
        "wgc": wgc.astype(np.float32), "bgc": bgc.astype(np.float32),
        "wqT": wqT.astype(np.float32), "wkT": wkT.astype(np.float32),
        "wvT": wvT.astype(np.float32), "woT": woT.astype(np.float32),
        "rfdup": rfdup.astype(np.float32), "boc": boc.astype(np.float32),
        "ident": ident, "iota8": io8, "negsel": negsel,
        "hsel": hselm, "esel": eselm,
        "ltstrict": np.triu(np.ones((128, 128), np.float32), k=1),
        "ones2d": np.ones((128, 128), np.float32),
    }
    nc = build_program(n_seq, _CFG["sub"])
    in_maps = []
    for c in range(n_cores):
        m = dict(shared)
        m["x_t"] = np.ascontiguousarray(xt[c * n_seq:(c + 1) * n_seq])
        in_maps.append(m)
    res = run_bass_kernel_spmd(nc, in_maps, core_ids=list(range(n_cores)))
    global _LAST
    _LAST = res.results
    outs = [r["y_t"] for r in res.results]
    y = np.concatenate(outs, axis=0)
    return np.ascontiguousarray(np.transpose(y, (0, 2, 1)))


# version 08
# speedup vs baseline: 1.5749x; 1.0056x over previous
import contextlib
import numpy as np
import concourse.bass as bass
import concourse.mybir as mybir
from concourse.bass_utils import run_bass_kernel_spmd
from concourse.bass_core_rust import IndirectOffsetOnAxis

F32 = mybir.dt.float32
BF16 = mybir.dt.bfloat16
U32 = mybir.dt.uint32
AF = mybir.ActivationFunctionType
ALU = mybir.AluOpType

H = 1024
T = 2048
E = 8
HD = 64
NHC = H // 128

_CFG = {"n_seq": 2, "sub": ("f", "a", "f")}
_LAST = None


class Sem:
    def __init__(self, es, nc, name):
        self.h = es.enter_context(nc.semaphore(name))
        self.n = 0

    def inc(self, instr, k=1):
        instr.then_inc(self.h, k)
        self.n += k
        return self.n


def build_program(n_seq, sub):
    nc = bass.Bass("TRN2", target_bir_lowering=False)

    d_x = nc.dram_tensor("x_t", [n_seq, H, T], F32, kind="ExternalInput").ap()
    d_y = nc.dram_tensor("y_t", [n_seq, H, T], F32, kind="ExternalOutput").ap()
    d_w1 = nc.dram_tensor("W1", [E, H, H], F32, kind="ExternalInput").ap()
    d_w2 = nc.dram_tensor("W2", [E, H, H], F32, kind="ExternalInput").ap()
    d_b1c = nc.dram_tensor("b1c", [128, 64], F32, kind="ExternalInput").ap()
    d_b2 = nc.dram_tensor("b2", [E, H], F32, kind="ExternalInput").ap()
    d_wgc = nc.dram_tensor("wgc", [128, 64], F32, kind="ExternalInput").ap()
    d_bgc = nc.dram_tensor("bgc", [8, 1], F32, kind="ExternalInput").ap()
    d_wq = nc.dram_tensor("wqT", [H, H], F32, kind="ExternalInput").ap()
    d_wk = nc.dram_tensor("wkT", [H, H], F32, kind="ExternalInput").ap()
    d_wv = nc.dram_tensor("wvT", [H, H], F32, kind="ExternalInput").ap()
    d_wo = nc.dram_tensor("woT", [H, H], F32, kind="ExternalInput").ap()
    d_rf = nc.dram_tensor("rfdup", [128, HD], F32, kind="ExternalInput").ap()
    d_boc = nc.dram_tensor("boc", [128, 8], F32, kind="ExternalInput").ap()
    d_ident = nc.dram_tensor("ident", [128, 128], F32, kind="ExternalInput").ap()
    d_iota8 = nc.dram_tensor("iota8", [128, 8], F32, kind="ExternalInput").ap()
    d_negsel = nc.dram_tensor("negsel", [2, 128], F32, kind="ExternalInput").ap()
    d_hsel = nc.dram_tensor("hsel", [16, H], F32, kind="ExternalInput").ap()
    d_esel = nc.dram_tensor("esel", [8, H], F32, kind="ExternalInput").ap()
    d_lt = nc.dram_tensor("ltstrict", [128, 128], F32, kind="ExternalInput").ap()
    d_on = nc.dram_tensor("ones2d", [128, 128], F32, kind="ExternalInput").ap()
    d_xs = nc.dram_tensor("xs_scr", [6144, H], BF16, kind="Internal").ap()
    d_ys = nc.dram_tensor("ys_scr", [6144, H], BF16, kind="Internal").ap()
    d_xnd = nc.dram_tensor("xnd_scr", [H, T], BF16, kind="Internal").ap()

    dbg = _CFG.get("dbg", False)
    if dbg:
        d_dbg_xn = nc.dram_tensor("dbg_xn", [H, T], BF16, kind="ExternalOutput").ap()
        d_dbg_lg = nc.dram_tensor("dbg_lg", [8, T], F32, kind="ExternalOutput").ap()
        d_dbg_oh = nc.dram_tensor("dbg_oh", [8, T], BF16, kind="ExternalOutput").ap()
        d_dbg_h1 = nc.dram_tensor("dbg_h1", [128, 8192], BF16, kind="ExternalOutput").ap()
        d_dbg_mk = nc.dram_tensor("dbg_mk", [128, T], BF16, kind="ExternalOutput").ap()
        d_dbg_xe = nc.dram_tensor("dbg_xe", [E, H, T], F32, kind="ExternalOutput").ap()
        d_dbg_v = nc.dram_tensor("dbg_v", [128, 16384], BF16, kind="ExternalOutput").ap()
        d_dbg_kv = nc.dram_tensor("dbg_kv", [128, 520], BF16, kind="ExternalOutput").ap()
        d_dbg_ar = nc.dram_tensor("dbg_ar", [128, 16384], BF16, kind="ExternalOutput").ap()
        d_dbg_nr = nc.dram_tensor("dbg_nr", [16, T], BF16, kind="ExternalOutput").ap()
    es = contextlib.ExitStack()
    with es:
        sbt = lambda name, shape, dt: es.enter_context(nc.sbuf_tensor(name, shape, dt))
        pst = lambda name, shape: es.enter_context(nc.psum_tensor(name, shape, F32))

        x = [sbt(f"x{i}", [128, T], F32) for i in range(NHC)]
        xn = [sbt(f"xn{i}", [128, T], BF16) for i in range(NHC)]
        w1s = sbt("w1s", [128, 8192], BF16)
        w2s = sbt("w2s", [128, 8192], BF16)
        h1m = sbt("h1m", [128, 8192], BF16)
        vat = sbt("vat", [128, 16384], BF16)
        ar32 = sbt("ar32", [128, 3072], F32)
        ident = sbt("sb_ident", [128, 128], F32)
        iota8 = sbt("sb_iota8", [128, 8], F32)
        negsel = sbt("sb_negsel", [2, 128], BF16)
        hsel = sbt("sb_hsel", [16, H], BF16)
        esel = sbt("sb_esel", [8, H], BF16)
        rf_f = sbt("rf_f", [128, HD], F32)
        rf_s = sbt("rf_s", [128, HD], BF16)
        b1c = sbt("sb_b1c", [128, 64], F32)
        b2bf = sbt("b2bf", [8, H], BF16)
        wg = sbt("wg", [128, 64], BF16)
        bgc = sbt("sb_bgc", [8, 1], F32)
        boc = sbt("sb_boc", [128, 8], F32)
        ones_col = sbt("ones_col", [128, 1], BF16)
        ones_bc = sbt("ones_bc", [1, 128], BF16)
        bdiag = sbt("bdiag", [128, 2], BF16)
        eps_col = sbt("eps_col", [1, 1], F32)
        kvext = sbt("kvext", [128, 520], BF16)
        mx8 = sbt("mx8", [128, 8], F32)
        idxu = sbt("idxu", [128, 8], U32)
        idxf = sbt("idxf", [128, 1], F32)
        ohf = sbt("ohf", [128, 128], F32)
        ohstage = sbt("ohstage", [128, 8], F32)
        ltst = sbt("sb_ltst", [128, 128], BF16)
        ones2d = sbt("sb_ones2d", [128, 128], BF16)
        idxs = sbt("idxs", [128, 16], F32)
        slotu = sbt("slotu", [128, 16], U32)
        ohb = sbt("ohb", [128, 128], BF16)
        rank_c = sbt("rank_c", [128, 1], F32)
        slot_f = sbt("slot_f", [128, 1], F32)
        ptmp = sbt("ptmp", [128, 8], F32)
        zt = sbt("zt", [128, 512], BF16)
        nrm16 = sbt("nrm16", [16, T], BF16)
        tmp_r = [sbt("tmp_r0", [1, 1024], BF16), sbt("tmp_r1", [1, 1024], BF16)]

        h1t = lambda ic: h1m[:, ic * 1024:(ic + 1) * 1024]
        xb_sl = [h1m[:, 0:2048], h1m[:, 2048:4096]]
        xsq_sl = [h1m[:, 4096:6144], h1m[:, 6144:8192]]
        kq_sb = h1m[:, 0:2048]
        sq_sb = h1m[:, 2048:4096]
        kp_sb = h1m[:, 4096:5120]
        nr_bf = h1m[0:2, 5120:7168]
        eq_sb = h1m[:, 2048:4096]
        mask_sl = [vat[:, 0:2048], vat[:, 2048:4096]]
        onehotT = vat[0:8, 4096:6144]
        m_bf = vat[0:1, 6144:7168]
        rstd_bf = vat[0:1, 7168:8192]
        vtile = lambda tt: vat[:, tt * 1024:(tt + 1) * 1024]
        araw = lambda oc: vat[:, oc * 2048:(oc + 1) * 2048]
        h1s = lambda ic: h1m[:, ic * 512:(ic + 1) * 512]
        xnt_b = [h1m[:, 4096 + 1024 * i:4096 + 1024 * (i + 1)] for i in range(2)]
        yt_b = [h1m[:, 6144 + 1024 * i:6144 + 1024 * (i + 1)] for i in range(2)]
        xgt = lambda sl, hc: vat[:, sl * 4096 + hc * 512:sl * 4096 + (hc + 1) * 512]
        p_sb = ar32[0:8, 0:2048]
        ytf_b = [ar32[:, 0:1024], ar32[:, 1024:2048]]
        t1_sl = ar32[:, 0:1024]
        m_row = ar32[0:1, 1024:2048]
        pair_f = ar32[0:2, 1024:2048]
        msq_row = ar32[0:1, 2048:3072]
        var_row = ar32[0:1, 0:1024]
        rstd_f = ar32[0:1, 2048:3072]
        lg_sb = ar32[0:8, 0:2048]

        PG = [pst(f"PG{i}", [128, 1024]) for i in range(4)]

        dl = Sem(es, nc, "dl")
        w1m = Sem(es, nc, "w1m")
        w2m = Sem(es, nc, "w2m")
        im = Sem(es, nc, "im")
        xl = Sem(es, nc, "xl")
        pe = Sem(es, nc, "pe")
        act = Sem(es, nc, "act")
        dve = Sem(es, nc, "dve")
        sp = Sem(es, nc, "sp")
        st = Sem(es, nc, "st")
        xd = Sem(es, nc, "xd")
        xtl = [Sem(es, nc, "xtl0"), Sem(es, nc, "xtl1")]
        sct = [Sem(es, nc, "sct0"), Sem(es, nc, "sct1")]
        xgl = [Sem(es, nc, "xgl0"), Sem(es, nc, "xgl1")]
        ysd = [Sem(es, nc, "ysd0"), Sem(es, nc, "ysd1")]
        ygt = [Sem(es, nc, "ygt0"), Sem(es, nc, "ygt1")]
        zs = Sem(es, nc, "zs")

        block = es.enter_context(nc.Block())

        @block.gpsimd
        def _(g):
            for dst, src in [(ident[:], d_ident), (iota8[:], d_iota8),
                             (negsel[:], d_negsel), (hsel[:], d_hsel),
                             (esel[:], d_esel), (rf_f[:], d_rf),
                             (ltst[:], d_lt), (ones2d[:], d_on),
                             (b1c[:], d_b1c), (b2bf[:], d_b2),
                             (wg[:], d_wgc), (bgc[:], d_bgc), (boc[:], d_boc)]:
                dl.inc(g.dma_start(dst, src), 16)
            im.inc(g.memset(ones_col[:], 1.0))
            im.inc(g.memset(ones_bc[:], 1.0))
            im.inc(g.memset(bdiag[:], 0.0))
            g.wait_ge(im.h, im.n)
            im.inc(g.memset(bdiag[0:64, 0:1], 1.0))
            g.wait_ge(im.h, im.n)
            im.inc(g.memset(bdiag[64:128, 1:2], 1.0))
            im.inc(g.memset(eps_col[:], 1e-5))
            im.inc(g.memset(zt[:], 0.0))
        init_dl = dl.n
        init_im = im.n

        @block.scalar
        def _(s):
            s.wait_ge(dl.h, init_dl)
            s.wait_ge(im.h, init_im)
            act.inc(s.activation(rf_s[:], rf_f[:], AF.Copy, scale=0.125))

        def vstep(v, instr):
            dve.inc(instr)
            v.wait_ge(dve.h, dve.n)

        def w_src(d_w):
            return d_w.rearrange("(c p) i -> p c i", p=128)

        def w_dst(slot):
            return slot[:].rearrange("p (c i) -> p c i", c=8)

        def emit_ln(x_ready):
            pe_base = pe.n
            pe_mark = lambda hc: pe_base + hc + 1
            act_base = act.n
            dve_base = dve.n

            @block.scalar
            def _(s):
                for hc in range(NHC):
                    sem, val = x_ready[hc]
                    s.wait_ge(sem, val)
                    if hc == 0:
                        s.wait_ge(x_ready[NHC - 1][0], x_ready[NHC - 1][1])
                    if hc >= 2:
                        s.wait_ge(pe.h, pe_mark(hc - 2))
                    act.inc(s.activation(xb_sl[hc % 2], x[hc][:], AF.Copy))
            assert act.n == act_base + NHC

            @block.vector
            def _(v):
                for hc in range(NHC):
                    sem, val = x_ready[hc]
                    v.wait_ge(sem, val)
                    if hc == 0:
                        v.wait_ge(x_ready[NHC - 1][0], x_ready[NHC - 1][1])
                    if hc >= 2:
                        v.wait_ge(pe.h, pe_mark(hc - 2))
                    dve.inc(v.tensor_mul(xsq_sl[hc % 2], x[hc][:], x[hc][:]))
            assert dve.n == dve_base + NHC

            @block.tensor
            def _(t):
                t.wait_ge(im.h, init_im)
                for hc in range(NHC):
                    t.wait_ge(act.h, act_base + hc + 1)
                    t.wait_ge(dve.h, dve_base + hc + 1)
                    first = hc == 0
                    last = hc == NHC - 1
                    sl = hc % 2
                    for half in range(2):
                        pgs, pgq = (PG[0], PG[1]) if half == 0 else (PG[2], PG[3])
                        for c in range(2):
                            co = half * 1024 + c * 512
                            t.matmul(pgs[0:1, c * 512:(c + 1) * 512],
                                     ones_col[:], xb_sl[sl][:, co:co + 512],
                                     start=first, stop=last)
                            i2 = t.matmul(pgq[0:1, c * 512:(c + 1) * 512],
                                          ones_col[:], xsq_sl[sl][:, co:co + 512],
                                          start=first, stop=last)
                    pe.inc(i2)
            sums_pe = pe.n
            assert sums_pe == pe_base + NHC

            xn_marks = [[0] * NHC, [0] * NHC]
            for half in range(2):
                pgs, pgq = (PG[0], PG[1]) if half == 0 else (PG[2], PG[3])

                @block.vector
                def _(v, pgs=pgs, pgq=pgq):
                    v.wait_ge(pe.h, sums_pe)
                    vstep(v, v.tensor_scalar_mul(m_row, pgs[0:1, :], 1.0 / H))
                    vstep(v, v.tensor_scalar_mul(msq_row, pgq[0:1, :], 1.0 / H))
                    vstep(v, v.tensor_mul(var_row, m_row, m_row))
                    dve.inc(v.tensor_sub(var_row, msq_row, var_row))
                v_rows = dve.n

                @block.scalar
                def _(s):
                    s.wait_ge(dve.h, v_rows)
                    act.inc(s.activation(rstd_f, var_row, AF.Ln,
                                         bias=eps_col[0:1, :]))
                    s.wait_ge(act.h, act.n)
                    act.inc(s.activation(rstd_f, rstd_f, AF.Exp, scale=-0.5))
                    s.wait_ge(act.h, act.n)
                    s.activation(m_bf[0:1, 0:1024], m_row, AF.Copy)
                    act.inc(s.activation(rstd_bf[0:1, 0:1024], rstd_f, AF.Copy))
                s_rows = act.n

                @block.tensor
                def _(t, pgs=pgs, pgq=pgq):
                    t.wait_ge(act.h, s_rows)
                    for c in range(2):
                        t.matmul(pgs[:, c * 512:(c + 1) * 512], ones_bc[:],
                                 m_bf[0:1, c * 512:(c + 1) * 512],
                                 start=True, stop=True)
                        i = t.matmul(pgq[:, c * 512:(c + 1) * 512], ones_bc[:],
                                     rstd_bf[0:1, c * 512:(c + 1) * 512],
                                     start=True, stop=True)
                    pe.inc(i)
                bc_pe = pe.n

                @block.vector
                def _(v, half=half, pgs=pgs, pgq=pgq):
                    v.wait_ge(pe.h, bc_pe)
                    co = half * 1024
                    for hc in range(NHC):
                        vstep(v, v.scalar_tensor_tensor(
                            t1_sl, x[hc][:, co:co + 1024], 1.0, pgs[:],
                            ALU.mult, ALU.subtract))
                        i = v.tensor_mul(xn[hc][:, co:co + 1024], t1_sl, pgq[:])
                        if hc < NHC - 1:
                            vstep(v, i)
                        else:
                            dve.inc(i)
                        xn_marks[half][hc] = dve.n
            return xn_marks

        def emit_gate(xn_marks):
            @block.tensor
            def _(t):
                t.wait_ge(dl.h, init_dl)
                t.wait_ge(im.h, init_im)
                for half in range(2):
                    pg = PG[half]
                    for hc in range(NHC):
                        t.wait_ge(dve.h, xn_marks[half][hc])
                        for c in range(2):
                            co = half * 1024 + c * 512
                            i = t.matmul(pg[0:8, c * 512:(c + 1) * 512],
                                         wg[:, hc * 8:hc * 8 + 8],
                                         xn[hc][:, co:co + 512],
                                         start=(hc == 0), stop=(hc == NHC - 1))
                pe.inc(i)
            g_pe = pe.n

            @block.scalar
            def _(s):
                s.wait_ge(pe.h, g_pe)
                for half in range(2):
                    i = s.activation(lg_sb[:, half * 1024:(half + 1) * 1024],
                                     PG[half][0:8, :], AF.Identity, bias=bgc[:])
                act.inc(i)
            lg_act = act.n

            @block.tensor
            def _(t):
                t.wait_ge(act.h, lg_act)
                for tt in range(16):
                    i = t.transpose(PG[2][:, tt * 8:tt * 8 + 8],
                                    lg_sb[:, tt * 128:(tt + 1) * 128],
                                    ident[0:8, 0:8])
                pe.inc(i)
            tr_pe = pe.n

            @block.vector
            def _(v):
                v.wait_ge(pe.h, tr_pe)
                for tt in range(16):
                    vstep(v, v.tensor_copy(ohstage[:], PG[2][:, tt * 8:tt * 8 + 8]))
                    vstep(v, v.max(mx8[:], ohstage[:]))
                    vstep(v, v.max_index(idxu[:], mx8[:], ohstage[:]))
                    vstep(v, v.tensor_copy(idxf[:], idxu[:, 0:1]))
                    vstep(v, v.tensor_copy(idxs[:, tt:tt + 1], idxf[:]))
                    vstep(v, v.tensor_scalar(ohf[:, tt * 8:tt * 8 + 8], iota8[:],
                                             idxf[:], None, ALU.is_equal))
                    i = v.tensor_copy(ohb[:, tt * 8:tt * 8 + 8],
                                      ohf[:, tt * 8:tt * 8 + 8])
                    if tt < 15:
                        vstep(v, i)
                dve.inc(i)
            oh_dve = dve.n

            pe_base = pe.n
            act_base = act.n

            @block.tensor
            def _(t):
                t.wait_ge(dve.h, oh_dve)
                for tt in range(16):
                    if tt >= 2:
                        t.wait_ge(act.h, act_base + tt - 1)
                    i = t.transpose(PG[3][0:8, (tt % 2) * 512:(tt % 2) * 512 + 128],
                                    ohf[:, tt * 8:tt * 8 + 8], ident[:])
                    pe.inc(i)

            @block.scalar
            def _(s):
                for tt in range(16):
                    s.wait_ge(pe.h, pe_base + tt + 1)
                    i = s.copy(onehotT[:, tt * 128:(tt + 1) * 128],
                               PG[3][0:8, (tt % 2) * 512:(tt % 2) * 512 + 128])
                    act.inc(i)
            return {"oh_act": act.n, "oh_dve": oh_dve, "lg_act": lg_act}

        def emit_mask(e_idx, slot, oh_act, war_act):
            @block.tensor
            def _(t):
                t.wait_ge(act.h, max(oh_act, war_act))
                for half in range(2):
                    for c in range(2):
                        co = half * 1024 + c * 512
                        i = t.matmul(PG[half][:, c * 512:(c + 1) * 512],
                                     esel[:, e_idx * 128:(e_idx + 1) * 128],
                                     onehotT[:, co:co + 512],
                                     start=True, stop=True)
                pe.inc(i)
            mpe = pe.n

            @block.scalar
            def _(s):
                s.wait_ge(pe.h, mpe)
                for half in range(2):
                    i = s.copy(mask_sl[slot][:, half * 1024:(half + 1) * 1024],
                               PG[half][:])
                act.inc(i)
            return act.n

        def emit_ffn(x_ready, w_marks, rearm):
            xn_marks = emit_ln(x_ready)
            oh_act = emit_gate(xn_marks)["oh_act"]
            mask_acts = {0: emit_mask(0, 0, oh_act, war_act=oh_act)}
            last_ps1_act = mask_acts[0]
            prev_gelu = [0, 0]
            prev_acc = [0, 0]
            for e in range(E):
                abase = act.n
                g_mark = lambda tb, ic: abase + tb * 8 + ic + 1
                has_mask = e + 1 < E
                dbase = dve.n
                mk_mark = lambda tb, ic: dbase + tb * 16 + ic + 1
                ac_mark = lambda tb, hc: dbase + tb * 16 + 8 + hc + 1

                mm1_marks = [[0] * 8, [0] * 8]
                mm2_marks = [[0] * 8, [0] * 8]
                mask_mpe = 0
                for tb in range(2):
                    for ic in range(8):
                        pg = PG[ic % 2]

                        @block.tensor
                        def _(t, e=e, tb=tb, ic=ic, pg=pg,
                              lpa=last_ps1_act, pga=prev_gelu[ic % 2]):
                            if tb == 0 and ic == 0:
                                t.wait_ge(*w_marks[("w1", e)])
                            if tb == 0 and ic < 2:
                                t.wait_ge(act.h, max(lpa, pga))
                            else:
                                t.wait_ge(act.h, g_mark(tb, ic - 2) if ic >= 2
                                          else g_mark(tb - 1, ic + 6))
                            for hc in range(NHC):
                                if e == 0:
                                    t.wait_ge(dve.h, xn_marks[tb][hc])
                                for c in range(2):
                                    i = t.matmul(
                                        pg[:, c * 512:(c + 1) * 512],
                                        w1s[:, hc * 1024 + ic * 128:
                                            hc * 1024 + (ic + 1) * 128],
                                        xn[hc][:, tb * 1024 + c * 512:
                                               tb * 1024 + (c + 1) * 512],
                                        start=(hc == 0), stop=(hc == NHC - 1))
                            pe.inc(i)
                        mm1_marks[tb][ic] = pe.n
                    if tb == 1:
                        w1_free_pe = pe.n

                    if tb == 1 and has_mask:
                        @block.tensor
                        def _(t, e=e, oa=oh_act):
                            t.wait_ge(act.h, max(oa, g_mark(1, 7)))
                            for half in range(2):
                                for c in range(2):
                                    co = half * 1024 + c * 512
                                    i = t.matmul(
                                        PG[half][:, c * 512:(c + 1) * 512],
                                        esel[:, (e + 1) * 128:(e + 2) * 128],
                                        onehotT[:, co:co + 512],
                                        start=True, stop=True)
                            pe.inc(i)
                        mask_mpe = pe.n

                    for hc in range(NHC):
                        pg2 = PG[2 + hc % 2]

                        @block.tensor
                        def _(t, e=e, tb=tb, hc=hc, pg2=pg2,
                              pac=prev_acc[hc % 2]):
                            if tb == 0 and hc == 0:
                                t.wait_ge(*w_marks[("w2", e)])
                            t.wait_ge(dve.h, mk_mark(tb, 7))
                            if hc >= 2:
                                t.wait_ge(dve.h, ac_mark(tb, hc - 2))
                            elif tb == 1:
                                t.wait_ge(dve.h, ac_mark(0, hc + 6))
                            else:
                                t.wait_ge(dve.h, pac)
                            first = True
                            if e == 0:
                                for c in range(2):
                                    t.matmul(pg2[:, c * 512:(c + 1) * 512],
                                             b2bf[:, hc * 128:(hc + 1) * 128],
                                             onehotT[:, tb * 1024 + c * 512:
                                                     tb * 1024 + (c + 1) * 512],
                                             start=True, stop=False)
                                first = False
                            for ic in range(8):
                                for c in range(2):
                                    i = t.matmul(
                                        pg2[:, c * 512:(c + 1) * 512],
                                        w2s[:, ic * 1024 + hc * 128:
                                            ic * 1024 + (hc + 1) * 128],
                                        h1t(ic)[:, c * 512:(c + 1) * 512],
                                        start=(first and ic == 0),
                                        stop=(ic == 7))
                            pe.inc(i)
                        mm2_marks[tb][hc] = pe.n

                @block.scalar
                def _(s, e=e, m1=mm1_marks, mpe=mask_mpe, hm=has_mask):
                    for tb in range(2):
                        for ic in range(8):
                            s.wait_ge(pe.h, m1[tb][ic])
                            act.inc(s.activation(
                                h1t(ic), PG[ic % 2][:], AF.Gelu,
                                bias=b1c[:, e * 8 + ic:e * 8 + ic + 1]))
                    if hm:
                        s.wait_ge(pe.h, mpe)
                        for half in range(2):
                            act.inc(s.copy(
                                mask_sl[(e + 1) % 2][:, half * 1024:
                                                     (half + 1) * 1024],
                                PG[half][:]))
                assert act.n == abase + 16 + (2 if has_mask else 0)
                if has_mask:
                    mask_acts[e + 1] = act.n
                    last_ps1_act = act.n
                prev_gelu = [g_mark(1, 6), g_mark(1, 7)]

                @block.vector
                def _(v, e=e, ma=mask_acts[e], m2=mm2_marks):
                    for tb in range(2):
                        if tb == 0:
                            v.wait_ge(act.h, ma)
                        for ic in range(8):
                            v.wait_ge(act.h, g_mark(tb, ic))
                            dve.inc(v.tensor_mul(
                                h1t(ic), h1t(ic),
                                mask_sl[e % 2][:, tb * 1024:(tb + 1) * 1024]))
                        for hc in range(NHC):
                            v.wait_ge(pe.h, m2[tb][hc])
                            dve.inc(v.scalar_tensor_tensor(
                                x[hc][:, tb * 1024:(tb + 1) * 1024],
                                PG[2 + hc % 2][:], 0.5,
                                x[hc][:, tb * 1024:(tb + 1) * 1024],
                                ALU.mult, ALU.add))
                assert dve.n == dbase + 32
                prev_acc = [ac_mark(1, 6), ac_mark(1, 7)]
                if e == E - 1:
                    final_acc = [ac_mark(1, hc) for hc in range(NHC)]

                if e + 1 < E:
                    @block.gpsimd
                    def _(g, e=e, wf=w1_free_pe, wf2=pe.n):
                        g.wait_ge(pe.h, wf)
                        w_marks[("w1", e + 1)] = (w1m.h, w1m.inc(
                            g.dma_start(w_dst(w1s), w_src(d_w1[e + 1])), 16))
                        g.wait_ge(pe.h, wf2)
                        w_marks[("w2", e + 1)] = (w2m.h, w2m.inc(
                            g.dma_start(w_dst(w2s), w_src(d_w2[e + 1])), 16))
            if rearm is not None:
                d1, d2 = rearm

                @block.gpsimd
                def _(g, wf=pe.n):
                    g.wait_ge(pe.h, wf)
                    w_marks[("w1", 0)] = (w1m.h, w1m.inc(
                        g.dma_start(w_dst(w1s), w_src(d1)), 16))
                    w_marks[("w2", 0)] = (w2m.h, w2m.inc(
                        g.dma_start(w_dst(w2s), w_src(d2)), 16))
            return [(dve.h, m) for m in final_acc]

        def emit_attn(x_ready, w_marks):
            astage = _CFG.get("astage", 4)
            xn_marks = emit_ln(x_ready)
            xn_all = xn_marks[1][NHC - 1]

            v_acts = []
            for tt in range(16):
                pg = PG[tt % 2]

                @block.tensor
                def _(t, tt=tt, pg=pg):
                    if tt == 0:
                        t.wait_ge(*w_marks["wv"])
                    if tt >= 2:
                        t.wait_ge(act.h, v_acts[tt - 2])
                    half = tt // 8
                    for hc in range(NHC):
                        if tt % 8 == 0:
                            t.wait_ge(dve.h, xn_marks[half][hc])
                        for c in range(2):
                            i = t.matmul(
                                pg[:, c * 512:(c + 1) * 512],
                                xn[hc][:, tt * 128:(tt + 1) * 128],
                                w1s[:, hc * 1024 + c * 512:
                                    hc * 1024 + (c + 1) * 512],
                                start=(hc == 0), stop=(hc == NHC - 1))
                    pe.inc(i)
                v_pe = pe.n

                @block.scalar
                def _(s, tt=tt, pg=pg, vp=v_pe):
                    s.wait_ge(pe.h, vp)
                    act.inc(s.copy(vtile(tt), pg[:]))
                v_acts.append(act.n)
            v_done_pe = pe.n
            if dbg:
                @block.sync
                def _(sy, va=v_acts[-1]):
                    sy.wait_ge(act.h, va)
                    st.inc(sy.dma_start(d_dbg_v, vat[:]), 16)
            if astage <= 1:
                return [(dve.h, xn_all)] * NHC

            @block.gpsimd
            def _(g, wf=v_done_pe):
                g.wait_ge(pe.h, wf)
                w_marks["wq"] = (w1m.h, w1m.inc(
                    g.dma_start(w_dst(w1s), w_src(d_wq)), 16))

            qk_marks = {}
            evac_marks = {}
            kvext_marks = {}
            proj_marks = {-1: 0}

            def emit_qk(oc, wslot, wmark_key, pg01_free):
                @block.tensor
                def _(t, oc=oc, pg01_free=pg01_free):
                    if oc == 0:
                        t.wait_ge(*w_marks[wmark_key])
                    t.wait_ge(act.h, pg01_free)
                    for hc in range(NHC):
                        if oc == 0:
                            t.wait_ge(dve.h, xn_marks[1][hc])
                        for half in range(2):
                            for c in range(2):
                                i = t.matmul(
                                    PG[half][:, c * 512:(c + 1) * 512],
                                    wslot[:, hc * 1024 + oc * 128:
                                          hc * 1024 + (oc + 1) * 128],
                                    xn[hc][:, half * 1024 + c * 512:
                                           half * 1024 + (c + 1) * 512],
                                    start=(hc == 0), stop=(hc == NHC - 1))
                    pe.inc(i)
                qk_marks[oc] = pe.n

            emit_qk(0, w2s, "wk", v_acts[-1])
            prev_tail = v_acts[-1]
            for oc in range(8):
                abase = act.n
                evac_mark = abase + 4
                rows_mark = abase + 8
                kp_mark = [abase + 9, abase + 12]
                kvx_mark = [abase + 11, abase + 14]

                @block.tensor
                def _(t, ea=evac_mark, kv_free=prev_tail):
                    t.wait_ge(act.h, ea)
                    if kv_free:
                        t.wait_ge(act.h, kv_free)
                    for half in range(2):
                        pgn = PG[2] if half == 0 else PG[3]
                        for c in range(2):
                            i = t.matmul(pgn[0:2, c * 512:(c + 1) * 512],
                                         bdiag[:],
                                         sq_sb[:, half * 1024 + c * 512:
                                               half * 1024 + (c + 1) * 512],
                                         start=True, stop=True)
                    pe.inc(i)
                nrm_pe = pe.n

                if oc + 1 < 8:
                    emit_qk(oc + 1, w2s, "wk", evac_mark)
                elif astage > 2:
                    @block.gpsimd
                    def _(g, wf=qk_marks[7]):
                        g.wait_ge(pe.h, wf)
                        w_marks["wo"] = (w2m.h, w2m.inc(
                            g.dma_start(w_dst(w2s), w_src(d_wo)), 16))

                proj_pe = [0, 0]
                kv_pe = [0, 0]
                for hh in range(2):
                    @block.tensor
                    def _(t, hh=hh, na=rows_mark,
                          pea=(kvx_mark[0] if hh == 1 else 0)):
                        t.wait_ge(act.h, na)
                        if pea:
                            t.wait_ge(act.h, pea)
                        for tt in range(16):
                            v_ = PG[2][:, tt * 64:(tt + 1) * 64]
                            t.matmul(v_,
                                     kq_sb[hh * 64:(hh + 1) * 64,
                                           tt * 128:(tt + 1) * 128],
                                     rf_s[hh * 64:(hh + 1) * 64, :],
                                     start=True, stop=False)
                            i = t.matmul(v_,
                                         nr_bf[0:2, tt * 128:(tt + 1) * 128],
                                         negsel[0:2, hh * 64:(hh + 1) * 64],
                                         start=False, stop=True)
                        pe.inc(i)
                    proj_pe[hh] = pe.n
                    if hh == 1:
                        proj_marks[oc] = pe.n

                    @block.tensor
                    def _(t, oc=oc, hh=hh, ka=kp_mark[hh]):
                        t.wait_ge(act.h, ka)
                        h_ = 2 * oc + hh
                        base = hh * 512
                        for tt in range(16):
                            t.matmul(PG[3][0:64, base:base + 64],
                                     kp_sb[:, tt * 64:(tt + 1) * 64],
                                     vtile(tt)[:, h_ * 64:h_ * 64 + 64],
                                     start=(tt == 0), stop=(tt == 15))
                            i = t.matmul(PG[2][0:64, 960 + hh:960 + hh + 1],
                                         kp_sb[:, tt * 64:(tt + 1) * 64],
                                         ones_col[:],
                                         start=(tt == 0), stop=(tt == 15))
                        pe.inc(i)
                    kv_pe[hh] = pe.n

                @block.scalar
                def _(s, oc=oc, qp=qk_marks[oc], pp=proj_marks[oc - 1],
                      nrm=nrm_pe, ppe=tuple(proj_pe), kpe=tuple(kv_pe)):
                    s.wait_ge(pe.h, max(qp, pp))
                    act.inc(s.copy(kq_sb[:, 0:1024], PG[0][:]))
                    act.inc(s.copy(kq_sb[:, 1024:2048], PG[1][:]))
                    act.inc(s.activation(sq_sb[:, 0:1024], PG[0][:], AF.Square))
                    act.inc(s.activation(sq_sb[:, 1024:2048], PG[1][:],
                                         AF.Square))
                    s.wait_ge(pe.h, nrm)
                    for half in range(2):
                        pgn = PG[2] if half == 0 else PG[3]
                        act.inc(s.activation(pair_f, pgn[0:2, :], AF.Ln))
                        s.wait_ge(act.h, act.n)
                        act.inc(s.activation(
                            nr_bf[0:2, half * 1024:(half + 1) * 1024],
                            pair_f, AF.Exp, scale=0.5))
                        if half == 0:
                            s.wait_ge(act.h, act.n)
                    for hh in range(2):
                        s.wait_ge(pe.h, ppe[hh])
                        act.inc(s.activation(kp_sb, PG[2][:], AF.Exp))
                        s.wait_ge(pe.h, kpe[hh])
                        base = hh * 512
                        act.inc(s.copy(kvext[hh * 64:(hh + 1) * 64,
                                             oc * 65:oc * 65 + 64],
                                       PG[3][0:64, base:base + 64]))
                        act.inc(s.copy(kvext[hh * 64:(hh + 1) * 64,
                                             oc * 65 + 64:oc * 65 + 65],
                                       PG[2][0:64, 960 + hh:960 + hh + 1]))
                assert act.n == abase + 14
                prev_tail = kvx_mark[1]
                kvext_marks[oc] = kvx_mark[1]
            kvext_marks[7] = prev_tail

            if dbg:
                @block.sync
                def _(sy, ka=kvext_marks[7]):
                    sy.wait_ge(act.h, ka)
                    st.inc(sy.dma_start(d_dbg_kv, kvext[:]), 16)
            if astage <= 2:
                return [(dve.h, xn_all)] * NHC

            out_free = proj_marks[7]
            ar_prev = kvext_marks[7]
            sp_prev = 0
            qk_marks = {}
            emit_qk(0, w1s, "wq", evac_mark)
            for oc in range(8):
                abase = act.n
                evac_mark = abase + 4
                rows_mark = abase + 8
                eq_mark = abase + 10
                ar_mark = [abase + 14, abase + 18]

                @block.tensor
                def _(t, ea=evac_mark, pg23_free=ar_prev):
                    t.wait_ge(act.h, ea)
                    t.wait_ge(act.h, pg23_free)
                    for half in range(2):
                        pgn = PG[2] if half == 0 else PG[3]
                        for c in range(2):
                            i = t.matmul(pgn[0:2, c * 512:(c + 1) * 512],
                                         bdiag[:],
                                         sq_sb[:, half * 1024 + c * 512:
                                               half * 1024 + (c + 1) * 512],
                                         start=True, stop=True)
                    pe.inc(i)
                nrm_pe = pe.n

                if oc + 1 < 8:
                    emit_qk(oc + 1, w1s, "wq", evac_mark)

                @block.tensor
                def _(t, na=rows_mark):
                    t.wait_ge(act.h, na)
                    for hh in range(2):
                        for half in range(2):
                            pgn = PG[2] if half == 0 else PG[3]
                            for c in range(2):
                                v_ = pgn[hh * 64:(hh + 1) * 64,
                                         c * 512:(c + 1) * 512]
                                t.matmul(v_, rf_s[hh * 64:(hh + 1) * 64, :],
                                         kq_sb[hh * 64:(hh + 1) * 64,
                                               half * 1024 + c * 512:
                                               half * 1024 + (c + 1) * 512],
                                         start=True, stop=False)
                                i = t.matmul(
                                    v_, negsel[0:2, hh * 64:(hh + 1) * 64],
                                    nr_bf[0:2, half * 1024 + c * 512:
                                          half * 1024 + (c + 1) * 512],
                                    start=False, stop=True)
                    pe.inc(i)
                proj_pe = pe.n

                out_pe = [0, 0]
                for hh in range(2):
                    @block.tensor
                    def _(t, oc=oc, hh=hh,
                          ea=(eq_mark if hh == 0 else ar_mark[0])):
                        t.wait_ge(act.h, ea)
                        for half in range(2):
                            pgo = PG[2] if half == 0 else PG[3]
                            for c in range(2):
                                i = t.matmul(
                                    pgo[0:65, c * 512:(c + 1) * 512],
                                    kvext[hh * 64:(hh + 1) * 64,
                                          oc * 65:(oc + 1) * 65],
                                    eq_sb[hh * 64:(hh + 1) * 64,
                                          half * 1024 + c * 512:
                                          half * 1024 + (c + 1) * 512],
                                    start=True, stop=True)
                        pe.inc(i)
                    out_pe[hh] = pe.n

                @block.scalar
                def _(s, oc=oc, qp=qk_marks[oc], pp=out_free, nrm=nrm_pe,
                      prj=proj_pe, ope=tuple(out_pe), spm=sp_prev, spb=sp.n):
                    s.wait_ge(pe.h, max(qp, pp))
                    act.inc(s.copy(kq_sb[:, 0:1024], PG[0][:]))
                    act.inc(s.copy(kq_sb[:, 1024:2048], PG[1][:]))
                    act.inc(s.activation(sq_sb[:, 0:1024], PG[0][:], AF.Square))
                    act.inc(s.activation(sq_sb[:, 1024:2048], PG[1][:],
                                         AF.Square))
                    s.wait_ge(pe.h, nrm)
                    for half in range(2):
                        pgn = PG[2] if half == 0 else PG[3]
                        act.inc(s.activation(pair_f, pgn[0:2, :], AF.Ln))
                        s.wait_ge(act.h, act.n)
                        act.inc(s.activation(
                            nr_bf[0:2, half * 1024:(half + 1) * 1024],
                            pair_f, AF.Exp, scale=0.5))
                        if half == 0:
                            s.wait_ge(act.h, act.n)
                    s.wait_ge(pe.h, prj)
                    act.inc(s.activation(eq_sb[:, 0:1024], PG[2][:], AF.Exp))
                    act.inc(s.activation(eq_sb[:, 1024:2048], PG[3][:],
                                         AF.Exp))
                    for hh in range(2):
                        s.wait_ge(pe.h, ope[hh])
                        s.wait_ge(sp.h, spm if hh == 0 else spb + 32)
                        for half in range(2):
                            pgo = PG[2] if half == 0 else PG[3]
                            act.inc(s.copy(
                                araw(oc)[hh * 64:(hh + 1) * 64,
                                         half * 1024:(half + 1) * 1024],
                                pgo[0:64, :]))
                            act.inc(s.copy(tmp_r[half][0:1, :],
                                           pgo[64:65, :]))
                assert act.n == abase + 18
                out_free = out_pe[1]
                ar_prev = ar_mark[1]

                for hh in range(2):
                    @block.sync
                    def _(sy, oc=oc, hh=hh, ea=ar_mark[hh]):
                        sy.wait_ge(act.h, ea)
                        row = 2 * oc + hh
                        sp.inc(sy.dma_start(
                            nrm16[row:row + 1, 0:1024], tmp_r[0][:]), 16)
                        sp.inc(sy.dma_start(
                            nrm16[row:row + 1, 1024:2048], tmp_r[1][:]), 16)
                sp_prev = sp.n
            prev_q_evac = ar_prev
            q_evac = {7: act.n}

            if dbg:
                @block.sync
                def _(sy, qa=prev_q_evac, spn=sp.n):
                    sy.wait_ge(act.h, qa)
                    sy.wait_ge(sp.h, spn)
                    st.inc(sy.dma_start(d_dbg_ar, vat[:]), 16)
                    st.inc(sy.dma_start(d_dbg_nr, nrm16[:]), 16)
            if astage <= 3:
                return [(dve.h, xn_all)] * NHC

            @block.vector
            def _(v, spn=sp.n):
                v.wait_ge(sp.h, spn)
                vstep(v, v.tensor_scalar_add(nrm16[:], nrm16[:], 1e-6))
                with nc.allow_low_precision(
                        reason="1/(nrm+1e-6): fp32 DVE internals, bf16 store "
                               "at matmul-operand precision"):
                    dve.inc(v.reciprocal(nrm16[:], nrm16[:]))
            al_dve = dve.n

            sc_dves = []
            for oc in range(8):
                pg_sc = (PG[2], PG[3]) if oc % 2 == 0 else (PG[0], PG[1])

                @block.tensor
                def _(t, oc=oc, ad=al_dve, pqe=prev_q_evac, pg_sc=pg_sc,
                      qe7=q_evac[7]):
                    if oc == 0:
                        t.wait_ge(dve.h, ad)
                        t.wait_ge(act.h, pqe)
                    if oc == 1:
                        t.wait_ge(act.h, qe7)
                    if oc >= 2:
                        t.wait_ge(dve.h, sc_dves[oc - 2])
                    for half in range(2):
                        pgn = pg_sc[half]
                        for c in range(2):
                            i = t.matmul(pgn[:, c * 512:(c + 1) * 512],
                                         hsel[:, oc * 128:(oc + 1) * 128],
                                         nrm16[:, half * 1024 + c * 512:
                                               half * 1024 + (c + 1) * 512],
                                         start=True, stop=True)
                    pe.inc(i)
                hs_pe = pe.n

                @block.vector
                def _(v, oc=oc, hp=hs_pe, pg_sc=pg_sc):
                    v.wait_ge(pe.h, hp)
                    for half in range(2):
                        pgn = pg_sc[half]
                        i = v.tensor_mul(
                            araw(oc)[:, half * 1024:(half + 1) * 1024],
                            araw(oc)[:, half * 1024:(half + 1) * 1024],
                            pgn[:])
                    dve.inc(i)
                sc_dves.append(dve.n)

            wo_marks = []
            for oco in range(8):
                pg_pair = (PG[0], PG[1]) if oco % 2 == 0 else (PG[2], PG[3])

                @block.tensor
                def _(t, oco=oco, pg_pair=pg_pair):
                    if oco == 0:
                        t.wait_ge(*w_marks["wo"])
                        t.wait_ge(dve.h, sc_dves[-1])
                    if oco >= 2:
                        t.wait_ge(dve.h, wo_marks[oco - 2])
                    for hc in range(NHC):
                        for half in range(2):
                            pgn = pg_pair[half]
                            for c in range(2):
                                i = t.matmul(
                                    pgn[:, c * 512:(c + 1) * 512],
                                    w2s[:, hc * 1024 + oco * 128:
                                        hc * 1024 + (oco + 1) * 128],
                                    araw(hc)[:, half * 1024 + c * 512:
                                             half * 1024 + (c + 1) * 512],
                                    start=(hc == 0), stop=(hc == NHC - 1))
                    pe.inc(i)
                wo_pe = pe.n

                @block.vector
                def _(v, oco=oco, wp=wo_pe, pg_pair=pg_pair):
                    v.wait_ge(pe.h, wp)
                    for half in range(2):
                        pgn = pg_pair[half]
                        i = v.scalar_tensor_tensor(
                            x[oco][:, half * 1024:(half + 1) * 1024],
                            pgn[:], boc[:, oco:oco + 1],
                            x[oco][:, half * 1024:(half + 1) * 1024],
                            ALU.add, ALU.add)
                    dve.inc(i)
                wo_marks.append(dve.n)
            return [(dve.h, wo_marks[hc]) for hc in range(NHC)]

        def emit_ffn_sparse(x_ready, w_marks, rearm):
            xn_marks = emit_ln(x_ready)
            g = emit_gate(xn_marks)
            oh_act, oh_dve = g["oh_act"], g["oh_dve"]

            @block.tensor
            def _(t, oa=oh_act, od=oh_dve):
                t.wait_ge(act.h, oa)
                t.wait_ge(dve.h, od)
                for tc in range(16):
                    reg = PG[tc // 8][0:8, (tc % 8) * 128:(tc % 8) * 128 + 128]
                    for kc in range(tc + 1):
                        i = t.matmul(reg, ohb[:, kc * 8:kc * 8 + 8],
                                     ltst[:] if kc == tc else ones2d[:],
                                     start=(kc == 0), stop=(kc == tc))
                    pe.inc(i)
            p_pe = pe.n

            @block.scalar
            def _(s, pp=p_pe):
                s.wait_ge(pe.h, pp)
                act.inc(s.copy(p_sb[:, 0:1024], PG[0][0:8, :]))
                act.inc(s.copy(p_sb[:, 1024:2048], PG[1][0:8, :]))
            p_act = act.n

            @block.tensor
            def _(t, pa=p_act):
                t.wait_ge(act.h, pa)
                for tt in range(16):
                    i = t.transpose(PG[3][:, 128 + tt * 8:128 + tt * 8 + 8],
                                    p_sb[:, tt * 128:(tt + 1) * 128],
                                    ident[0:8, 0:8])
                    pe.inc(i)
            ptr_pe = pe.n

            slot_marks = []

            @block.vector
            def _(v, pp=ptr_pe):
                v.wait_ge(pe.h, pp)
                for tt in range(16):
                    vstep(v, v.tensor_mul(ptmp[:],
                                          PG[3][:, 128 + tt * 8:128 + tt * 8 + 8],
                                          ohf[:, tt * 8:tt * 8 + 8]))
                    vstep(v, v.reduce_sum(rank_c[:], ptmp[:],
                                          axis=mybir.AxisListType.X))
                    vstep(v, v.scalar_tensor_tensor(
                        slot_f[:], idxs[:, tt:tt + 1], 512.0, rank_c[:],
                        ALU.mult, ALU.add))
                    i = v.tensor_copy(slotu[:, tt:tt + 1], slot_f[:])
                    if tt < 15:
                        vstep(v, i)
                    slot_marks.append(dve.n + (1 if tt == 15 else 0))
                dve.inc(i)
            slot_dve = dve.n

            @block.sync
            def _(sy):
                for hc in range(NHC):
                    sy.wait_ge(dve.h, xn_marks[1][hc])
                    xd.inc(sy.dma_start(d_xnd[hc * 128:(hc + 1) * 128, :],
                                        xn[hc][:]), 16)
            xd_full = xd.n

            sct_done = []
            for tt in range(16):
                par = tt % 2

                @block.sync
                def _(sy, tt=tt, par=par, xdf=xd_full):
                    if tt == 0:
                        sy.wait_ge(xd.h, xdf)
                    if tt >= 2:
                        sy.wait_ge(sct[par].h, sct_done[tt - 2])
                    xtl[par].inc(sy.dma_start(
                        xnt_b[par][:],
                        d_xnd[:, tt * 128:(tt + 1) * 128], transpose=True), 16)

                @block.gpsimd
                def _(g_, tt=tt, par=par, sd=slot_marks[tt], xt=xtl[par].n):
                    if tt == 0:
                        g_.wait_ge(zs.h, zs_full)
                    g_.wait_ge(dve.h, sd)
                    g_.wait_ge(xtl[par].h, xt)
                    sct[par].inc(g_.indirect_dma_start(
                        d_xs, IndirectOffsetOnAxis(slotu[:, tt:tt + 1], 0),
                        xnt_b[par][:], None), 16)
                sct_done.append(sct[par].n)
            sct_full = (sct[0].n, sct[1].n)

            gelu_prev = [0, 0]
            yd_prev = (0, 0)
            xg_marks = []
            for e in range(E):
                @block.sync
                def _(sy, e=e, sf=sct_full):
                    if e == 0:
                        sy.wait_ge(sct[0].h, sf[0])
                        sy.wait_ge(sct[1].h, sf[1])
                    if e >= 2:
                        sy.wait_ge(pe.h, xg_marks[e - 2][-1])
                    for hc in range(NHC):
                        xgl[e % 2].inc(sy.dma_start(
                            xgt(e % 2, hc),
                            d_xs[e * 512:(e + 1) * 512,
                                 hc * 128:(hc + 1) * 128], transpose=True), 16)
                xg_full = xgl[e % 2].n

                abase = act.n
                g_mark = lambda ic: abase + ic + 1
                mm1_marks = []
                for ic in range(8):
                    @block.tensor
                    def _(t, e=e, ic=ic, xf=xg_full,
                          pga=gelu_prev[ic % 2], pa=p_act):
                        if ic == 0:
                            t.wait_ge(*w_marks[("w1", e)])
                            t.wait_ge(xgl[e % 2].h, xf)
                        if ic >= 2:
                            t.wait_ge(act.h, g_mark(ic - 2))
                        else:
                            t.wait_ge(act.h, max(pga, pa))
                        for hc in range(NHC):
                            i = t.matmul(
                                PG[ic % 2][:, 0:512],
                                w1s[:, hc * 1024 + ic * 128:
                                    hc * 1024 + (ic + 1) * 128],
                                xgt(e % 2, hc),
                                start=(hc == 0), stop=(hc == NHC - 1))
                        pe.inc(i)
                    mm1_marks.append(pe.n)
                xg_marks.append(tuple(mm1_marks))

                mm2_marks = []
                for sc in range(4):
                    pg2 = PG[2 + sc % 2]

                    @block.tensor
                    def _(t, e=e, sc=sc, pg2=pg2, sd=slot_dve):
                        t.wait_ge(*w_marks[("w2", e)]) if sc == 0 else None
                        t.wait_ge(act.h, abase + 8)
                        if sc >= 2:
                            t.wait_ge(act.h, abase + 8 + (sc - 2) + 1)
                        elif e > 0:
                            t.wait_ge(act.h, abase - 12 + 8 + (sc + 2) + 1)
                        else:
                            t.wait_ge(dve.h, sd)
                        for c in range(2):
                            t.matmul(pg2[:, c * 512:(c + 1) * 512],
                                     esel[:, e * 128:(e + 1) * 128],
                                     b2bf[:, c * 512:(c + 1) * 512],
                                     start=True, stop=False)
                        for ic in range(8):
                            for c in range(2):
                                i = t.matmul(
                                    pg2[:, c * 512:(c + 1) * 512],
                                    h1s(ic)[:, sc * 128:(sc + 1) * 128],
                                    w2s[:, ic * 1024 + c * 512:
                                        ic * 1024 + (c + 1) * 512],
                                    start=False, stop=(ic == 7))
                            pass
                        pe.inc(i)
                    mm2_marks.append(pe.n)

                yd_base = (ysd[0].n, ysd[1].n)

                @block.scalar
                def _(s, e=e, m1=tuple(mm1_marks), m2=tuple(mm2_marks),
                      ydp=yd_prev, ydb=yd_base):
                    for ic in range(8):
                        s.wait_ge(pe.h, m1[ic])
                        act.inc(s.activation(
                            h1s(ic), PG[ic % 2][:, 0:512], AF.Gelu,
                            bias=b1c[:, e * 8 + ic:e * 8 + ic + 1]))
                    for sc in range(4):
                        s.wait_ge(pe.h, m2[sc])
                        if sc < 2:
                            s.wait_ge(ysd[sc % 2].h, ydp[sc % 2])
                        else:
                            s.wait_ge(ysd[sc % 2].h, ydb[sc % 2] + 16)
                        act.inc(s.copy(yt_b[sc % 2][:], PG[2 + sc % 2][:]))
                assert act.n == abase + 12
                gelu_prev = [abase + 7, abase + 8]

                if e + 1 < E:
                    @block.gpsimd
                    def _(g_, e=e, wf=mm1_marks[-1], wf2=pe.n):
                        g_.wait_ge(pe.h, wf)
                        w_marks[("w1", e + 1)] = (w1m.h, w1m.inc(
                            g_.dma_start(w_dst(w1s), w_src(d_w1[e + 1])), 16))
                        g_.wait_ge(pe.h, wf2)
                        w_marks[("w2", e + 1)] = (w2m.h, w2m.inc(
                            g_.dma_start(w_dst(w2s), w_src(d_w2[e + 1])), 16))
                @block.gpsimd
                def _(g2, e=e, ab=abase):
                    for sc in range(4):
                        g2.wait_ge(act.h, ab + 8 + sc + 1)
                        ysd[sc % 2].inc(g2.dma_start(
                            d_ys[e * 512 + sc * 128:e * 512 + (sc + 1) * 128, :],
                            yt_b[sc % 2][:]), 16)
                yd_prev = (ysd[0].n, ysd[1].n)

            ys_full = (ysd[0].n, ysd[1].n)

            cast_acts = []
            tr_pes = []
            acc_dves = []
            for tt in range(16):
                par = tt % 2

                @block.gpsimd
                def _(g_, tt=tt, par=par, yf=ys_full):
                    if tt == 0:
                        g_.wait_ge(ysd[0].h, yf[0])
                        g_.wait_ge(ysd[1].h, yf[1])
                    if tt >= 2:
                        g_.wait_ge(act.h, cast_acts[tt - 2])
                    ygt[par].inc(g_.indirect_dma_start(
                        xnt_b[par][:], None,
                        d_ys, IndirectOffsetOnAxis(slotu[:, tt:tt + 1], 0)), 16)

                @block.scalar
                def _(s, tt=tt, par=par, yg=ygt[par].n, pp=(tr_pes[tt - 2] if tt >= 2 else 0)):
                    s.wait_ge(ygt[par].h, yg)
                    if pp:
                        s.wait_ge(pe.h, pp)
                    act.inc(s.copy(ytf_b[par][:], xnt_b[par][:]))
                cast_acts.append(act.n)

                @block.tensor
                def _(t, tt=tt, par=par, ca=cast_acts[-1],
                      pd=(acc_dves[tt - 2] if tt >= 2 else 0)):
                    t.wait_ge(act.h, ca)
                    if pd:
                        t.wait_ge(dve.h, pd)
                    for hc in range(NHC):
                        i = t.transpose(PG[par][:, hc * 128:(hc + 1) * 128],
                                        ytf_b[par][:, hc * 128:(hc + 1) * 128],
                                        ident[:])
                        pe.inc(i)
                tr_pes.append(pe.n)

                @block.vector
                def _(v, tt=tt, par=par, tp=tr_pes[-1]):
                    v.wait_ge(pe.h, tp)
                    for hc in range(NHC):
                        dve.inc(v.scalar_tensor_tensor(
                            x[hc][:, tt * 128:(tt + 1) * 128],
                            PG[par][:, hc * 128:(hc + 1) * 128], 0.5,
                            x[hc][:, tt * 128:(tt + 1) * 128],
                            ALU.mult, ALU.add))
                acc_dves.append(dve.n)

            if rearm is not None:
                d1, d2 = rearm

                @block.gpsimd
                def _(g_, wf=pe.n):
                    g_.wait_ge(pe.h, wf)
                    w_marks[("w1", 0)] = (w1m.h, w1m.inc(
                        g_.dma_start(w_dst(w1s), w_src(d1)), 16))
                    w_marks[("w2", 0)] = (w2m.h, w2m.inc(
                        g_.dma_start(w_dst(w2s), w_src(d2)), 16))
            return [(dve.h, acc_dves[-1])] * NHC

        xl_base = xl.n

        @block.sync
        def _(sy):
            for hc in range(0, NHC, 2):
                xl.inc(sy.dma_start(x[hc][:],
                                    d_x[0, hc * 128:(hc + 1) * 128, :]), 16)
            sy.wait_ge(im.h, init_im)
            for r in range(48):
                for c in range(2):
                    zs.inc(sy.dma_start(
                        d_xs[r * 128:(r + 1) * 128, c * 512:(c + 1) * 512],
                        zt[:]), 16)

        @block.scalar
        def _(s2):
            for hc in range(1, NHC, 2):
                xl.inc(s2.dma_start(x[hc][:],
                                    d_x[0, hc * 128:(hc + 1) * 128, :]), 16)

        @block.scalar
        def _(s):
            s.wait_ge(im.h, init_im)
            for r in range(48):
                for c in range(2):
                    zs.inc(s.dma_start(
                        d_ys[r * 128:(r + 1) * 128, c * 512:(c + 1) * 512],
                        zt[:]), 16)
        zs_full = zs.n

        w_marks = {}
        first_kind = sub[0] if sub else "f"

        @block.gpsimd
        def _(g):
            if first_kind == "f":
                w_marks[("w1", 0)] = (w1m.h, w1m.inc(
                    g.dma_start(w_dst(w1s), w_src(d_w1[0])), 16))
                w_marks[("w2", 0)] = (w2m.h, w2m.inc(
                    g.dma_start(w_dst(w2s), w_src(d_w2[0])), 16))
            else:
                w_marks["wv"] = (w1m.h, w1m.inc(
                    g.dma_start(w_dst(w1s), w_src(d_wv)), 16))
                w_marks["wk"] = (w2m.h, w2m.inc(
                    g.dma_start(w_dst(w2s), w_src(d_wk)), 16))

        x_ready = [(xl.h, xl.n)] * NHC
        for s_i in range(n_seq):
            for sub_i, kind in enumerate(sub):
                last = (s_i, sub_i) == (n_seq - 1, len(sub) - 1)
                nxt_kind = None
                if not last:
                    nxt_kind = sub[sub_i + 1] if sub_i + 1 < len(sub) else sub[0]
                if kind == "f":
                    rearm = None
                    if nxt_kind == "a":
                        rearm = (d_wv, d_wk)
                    elif nxt_kind == "f":
                        rearm = (d_w1[0], d_w2[0])
                    if _CFG.get("sparse", True):
                        done = emit_ffn_sparse(x_ready, w_marks, rearm=rearm)
                    else:
                        done = emit_ffn(x_ready, w_marks, rearm=rearm)
                    if rearm is not None and nxt_kind == "a":
                        w_marks["wv"] = w_marks[("w1", 0)]
                        w_marks["wk"] = w_marks[("w2", 0)]
                    x_ready = done
                elif kind == "a":
                    done = emit_attn(x_ready, w_marks)
                    x_ready = done
                    if nxt_kind == "f":
                        @block.gpsimd
                        def _(g, wf=pe.n):
                            g.wait_ge(pe.h, wf)
                            w_marks[("w1", 0)] = (w1m.h, w1m.inc(
                                g.dma_start(w_dst(w1s), w_src(d_w1[0])), 16))
                            w_marks[("w2", 0)] = (w2m.h, w2m.inc(
                                g.dma_start(w_dst(w2s), w_src(d_w2[0])), 16))
                    elif nxt_kind == "a":
                        @block.gpsimd
                        def _(g, wf=pe.n):
                            g.wait_ge(pe.h, wf)
                            w_marks["wv"] = (w1m.h, w1m.inc(
                                g.dma_start(w_dst(w1s), w_src(d_wv)), 16))
                            w_marks["wk"] = (w2m.h, w2m.inc(
                                g.dma_start(w_dst(w2s), w_src(d_wk)), 16))

            xl_base2 = xl.n

            @block.scalar
            def _(s2, s_i=s_i, xr=x_ready):
                for hc in range(1, NHC, 2):
                    sem, val = xr[hc]
                    s2.wait_ge(sem, val)
                    st.inc(s2.dma_start(d_y[s_i, hc * 128:(hc + 1) * 128, :],
                                        x[hc][:]), 16)

            @block.sync
            def _(sy, s_i=s_i, xr=x_ready):
                for hc in range(0, NHC, 2):
                    sem, val = xr[hc]
                    sy.wait_ge(sem, val)
                    st.inc(sy.dma_start(d_y[s_i, hc * 128:(hc + 1) * 128, :],
                                        x[hc][:]), 16)
                if s_i + 1 < n_seq:
                    sy.wait_ge(st.h, st.n)
                    for hc in range(NHC):
                        xl.inc(sy.dma_start(
                            x[hc][:],
                            d_x[s_i + 1, hc * 128:(hc + 1) * 128, :]), 16)
            if s_i + 1 < n_seq:
                x_ready = [(xl.h, xl.n)] * NHC

        @block.sync
        def _(sy):
            sy.wait_ge(st.h, st.n)

    return nc


def kernel(L0_attn_x, L0_attn_w_qkv, L0_attn_w_out, L0_attn_b_out,
           L0_attn_random_features, L0_ffn_Wg, L0_ffn_bg, L0_ffn_W1,
           L0_ffn_b1, L0_ffn_W2, L0_ffn_b2):
    n_seq = _CFG["n_seq"]
    B = L0_attn_x.shape[0]
    n_cores = B // n_seq
    xt = np.ascontiguousarray(np.transpose(np.asarray(L0_attn_x), (0, 2, 1)))

    wqkv = np.asarray(L0_attn_w_qkv)
    wqT = np.ascontiguousarray(wqkv[0:1024].T)
    wkT = np.ascontiguousarray(wqkv[1024:2048].T)
    wvT = np.ascontiguousarray(wqkv[2048:3072].T)
    woT = np.ascontiguousarray(np.asarray(L0_attn_w_out).T)
    rf = np.asarray(L0_attn_random_features)
    rfdup = np.ascontiguousarray(np.tile(rf.T, (2, 1)))
    b1 = np.asarray(L0_ffn_b1)
    b1c = np.ascontiguousarray(
        b1.reshape(E, 8, 128).transpose(2, 0, 1).reshape(128, 64))
    wgc = np.ascontiguousarray(
        np.asarray(L0_ffn_Wg).reshape(8, 128, 8).transpose(1, 0, 2).reshape(128, 64))
    bgc = np.ascontiguousarray(np.asarray(L0_ffn_bg).reshape(8, 1))
    boc = np.ascontiguousarray(np.asarray(L0_attn_b_out).reshape(8, 128).T)
    ident = np.eye(128, dtype=np.float32)
    io8 = np.tile(np.arange(8, dtype=np.float32), (128, 1))
    negsel = np.zeros((2, 128), np.float32)
    negsel[0, 0:64] = -1.0
    negsel[1, 64:128] = -1.0
    hselm = np.zeros((16, H), np.float32)
    for hh in range(16):
        hselm[hh, hh * 64:(hh + 1) * 64] = 1.0
    eselm = np.zeros((8, H), np.float32)
    for e in range(8):
        eselm[e, e * 128:(e + 1) * 128] = 1.0

    shared = {
        "W1": np.ascontiguousarray(np.asarray(L0_ffn_W1), dtype=np.float32),
        "W2": np.ascontiguousarray(np.asarray(L0_ffn_W2), dtype=np.float32),
        "b1c": b1c.astype(np.float32),
        "b2": np.ascontiguousarray(np.asarray(L0_ffn_b2), dtype=np.float32),
        "wgc": wgc.astype(np.float32), "bgc": bgc.astype(np.float32),
        "wqT": wqT.astype(np.float32), "wkT": wkT.astype(np.float32),
        "wvT": wvT.astype(np.float32), "woT": woT.astype(np.float32),
        "rfdup": rfdup.astype(np.float32), "boc": boc.astype(np.float32),
        "ident": ident, "iota8": io8, "negsel": negsel,
        "hsel": hselm, "esel": eselm,
        "ltstrict": np.triu(np.ones((128, 128), np.float32), k=1),
        "ones2d": np.ones((128, 128), np.float32),
    }
    nc = build_program(n_seq, _CFG["sub"])
    in_maps = []
    for c in range(n_cores):
        m = dict(shared)
        m["x_t"] = np.ascontiguousarray(xt[c * n_seq:(c + 1) * n_seq])
        in_maps.append(m)
    res = run_bass_kernel_spmd(nc, in_maps, core_ids=list(range(n_cores)))
    global _LAST
    _LAST = res.results
    outs = [r["y_t"] for r in res.results]
    y = np.concatenate(outs, axis=0)
    return np.ascontiguousarray(np.transpose(y, (0, 2, 1)))
